# version 01
# baseline (speedup 1.0000x reference)
import numpy as np
from contextlib import ExitStack

import concourse.bass as bass
import concourse.mybir as mybir
from concourse.bass_utils import run_bass_kernel_spmd

F32 = mybir.dt.float32
BF16 = mybir.dt.bfloat16
I32 = mybir.dt.int32
AF = mybir.ActivationFunctionType
ALU = mybir.AluOpType
AX = mybir.AxisListType
ds = bass.ds

DM = 2048
DFF = 8192
NT = DM // 128
FT = DFF // 128
ROWS = 2048
NB = 4
BLKR = 512
SEQ = 4096
NG = 8
H = 8
KVH = 2
HD = 128
EPS = 1e-5

N_SUBLAYERS = 3


def build_nc():
    nc = bass.Bass("TRN2", target_bir_lowering=False, detect_race_conditions=False)

    xT_in = nc.dram_tensor("xT_in", [DM, ROWS], F32, kind="ExternalInput").ap()
    w1s_in = nc.dram_tensor("w1s", [8, NT, 128, 128], F32, kind="ExternalInput").ap()
    w2s_in = nc.dram_tensor("w2s", [2, FT, 128, 128], F32, kind="ExternalInput").ap()
    wqkv_in = nc.dram_tensor("wqkvN", [6, NT, 128, 512], F32, kind="ExternalInput").ap()
    wo_in = nc.dram_tensor("woT", [16, 16, 128, 128], F32, kind="ExternalInput").ap()
    constf_in = nc.dram_tensor("constf", [128, 96], F32, kind="ExternalInput").ap()
    constb_in = nc.dram_tensor("constb", [128, 5120], BF16, kind="ExternalInput").ap()
    trig_in = nc.dram_tensor("trigN", [128, 4096], BF16, kind="ExternalInput").ap()
    offs_in = nc.dram_tensor("offs", [1, 16], I32, kind="ExternalInput").ap()

    out_T = nc.dram_tensor("out_T", [DM, ROWS], F32, kind="ExternalOutput").ap()

    x1T = nc.dram_tensor("x1T", [DM, ROWS], F32, kind="Internal").ap()
    x2T = nc.dram_tensor("x2T", [DM, ROWS], F32, kind="Internal").ap()
    sc_loc = nc.dram_tensor("sc_loc", [128, 128], F32, kind="Internal").ap()
    sc_all = nc.dram_tensor("sc_all", [1024, 128], F32, kind="Internal").ap()
    w1q_loc = nc.dram_tensor("w1q_loc", [8 * NT * 128, 128], BF16, kind="Internal").ap()
    w1q_all = nc.dram_tensor("w1q_all", [FT * NT * 128, 128], BF16, kind="Internal").ap()
    w2q_loc = nc.dram_tensor("w2q_loc", [2 * FT * 128, 128], BF16, kind="Internal").ap()
    w2q_all = nc.dram_tensor("w2q_all", [16 * FT * 128, 128], BF16, kind="Internal").ap()
    qkv_in = nc.dram_tensor("qkv_in", [4, 6656, 512], BF16, kind="Internal").ap()
    qkv_out = nc.dram_tensor("qkv_out", [4, 13312, 512], BF16, kind="Internal").ap()
    o_in = nc.dram_tensor("o_in", [NG, 1024, 512], BF16, kind="Internal").ap()
    o_out = nc.dram_tensor("o_out", [NG, 2048, 512], BF16, kind="Internal").ap()

    es = ExitStack()

    def SBp(name, shape, dt):
        return es.enter_context(nc.sbuf_tensor(name, shape, dt))

    constf = SBp("constf_s", [128, 96], F32)
    constb = SBp("constb_s", [128, 5120], BF16)
    trig = SBp("trig_s", [128, 4096], BF16)
    offs_s = SBp("offs_s", [1, 16], I32)
    onesrow_f = SBp("onesrow_f", [1, 128], F32)
    thr_s = SBp("thr_s", [1, 4], F32)
    thrv = SBp("thrv", [128, 2], F32)
    nthrv = SBp("nthrv", [128, 2], F32)
    stat = SBp("stat", [1, 2048], F32)
    statb = SBp("statb", [1, 1024], BF16)

    pb = [es.enter_context(nc.psum_tensor(f"pb{i}", [128, 512], F32)) for i in range(8)]

    sems = {}
    cnt = {}
    for s in ["dW", "dX", "dG", "dS", "CC", "PE", "AC", "VE", "OUT"]:
        sems[s] = es.enter_context(nc.semaphore(s))
        cnt[s] = 0

    def bump(s, by=1):
        cnt[s] += by
        return cnt[s]

    blk = nc.Block()
    blk.__enter__()

    W = {}

    ONES_F = constf[:, 0:1]
    B2COL = lambda dmt: constf[:, 1 + dmt: 2 + dmt]
    B1COL = lambda ft: constf[:, 17 + ft: 18 + ft]
    ONES_B = constb[:, 0:1]
    IDENT = constb[:, 2:130]
    MASKGLA = constb[:, 130:258]
    MASKDIAG = lambda j: constb[:, 258 + j * 512: 258 + (j + 1) * 512]
    ONESROW_B = constb[0:1, 2306:2434]
    ONESROW512_B = constb[0:1, 2434:2946]
    B2ROW = lambda dmt: constb[0:1, 2946 + dmt * 128: 2946 + (dmt + 1) * 128]

    def phase_prologue(pes):
        wslab = pes.enter_context(nc.sbuf_tensor("wslab", [128, 16384], F32))
        wqslab = pes.enter_context(nc.sbuf_tensor("wqslab", [128, 16384], BF16))
        tmpq = pes.enter_context(nc.sbuf_tensor("tmpq", [128, 8192], BF16))
        sc_s = pes.enter_context(nc.sbuf_tensor("sc_s", [128, 128], F32))
        sc8 = pes.enter_context(nc.sbuf_tensor("sc8", [128, 16], F32))
        scacc = pes.enter_context(nc.sbuf_tensor("scacc", [128, 2], F32))

        def _(sync):
            sync.dma_start(out=constf[:], in_=constf_in).then_inc(sems["dX"], 16)
            sync.dma_start(out=constb[:], in_=constb_in).then_inc(sems["dX"], 16)
            sync.dma_start(out=trig[:], in_=trig_in).then_inc(sems["dX"], 16)
            sync.dma_start(out=offs_s[:], in_=offs_in).then_inc(sems["dX"], 16)
            sync.dma_start(out=wslab[:].rearrange("p (f d c) -> p f d c", f=8, d=NT),
                           in_=w1s_in.rearrange("f d p c -> p f d c")).then_inc(sems["dX"], 16)
        blk.sync(_)
        bump("dX", 16 * 5)
        W["p_loads"] = cnt["dX"]

        def _(v):
            v.wait_ge(sems["dX"], W["p_loads"])
            v.memset(onesrow_f[:], 1.0)
            v.memset(sc_s[:], 0.0)
            v.reduce_sum(sc_s[:, 0:1], wslab[:], AX.X, apply_absolute_value=True).then_inc(sems["VE"], 1)
        blk.vector(_)
        W["p_w1red"] = bump("VE")

        def _(sync):
            sync.wait_ge(sems["VE"], W["p_w1red"])
            sync.dma_start(out=wslab[:].rearrange("p (f d c) -> p f d c", f=2, d=FT),
                           in_=w2s_in.rearrange("f d p c -> p f d c")).then_inc(sems["dX"], 16)
        blk.sync(_)
        W["p_w2load"] = bump("dX", 16)

        def _(v):
            v.wait_ge(sems["dX"], W["p_w2load"])
            v.reduce_sum(sc_s[:, 1:2], wslab[:], AX.X, apply_absolute_value=True).then_inc(sems["VE"], 1)
        blk.vector(_)
        W["p_w2red"] = bump("VE")

        def _(sync):
            sync.wait_ge(sems["VE"], W["p_w2red"])
            sync.dma_start(out=sc_loc, in_=sc_s[:]).then_inc(sems["dX"], 16)
        blk.sync(_)
        W["p_scst"] = bump("dX", 16)

        def _(g):
            g.wait_ge(sems["dX"], W["p_scst"])
            g.collective_compute("AllGather", ALU.bypass, replica_groups=[list(range(8))],
                                 ins=[sc_loc], outs=[sc_all]).then_inc(sems["CC"], 1)
        blk.gpsimd(_)
        W["p_scAG"] = bump("CC")

        def _(sync):
            sync.wait_ge(sems["CC"], W["p_scAG"])
            sync.dma_start(out=sc8[:].rearrange("p (k c) -> p k c", k=8), in_=sc_all[:, 0:2].rearrange("(k p) c -> p k c", p=128)).then_inc(sems["dX"], 16)
            sync.dma_start(out=wslab[:].rearrange("p (f d c) -> p f d c", f=8, d=NT),
                           in_=w1s_in.rearrange("f d p c -> p f d c")).then_inc(sems["dX"], 16)
        blk.sync(_)
        W["p_sc8"] = bump("dX", 16)
        W["p_w1reload"] = bump("dX", 16)

        def _(v):
            v.wait_ge(sems["dX"], W["p_sc8"])
            v.reduce_sum(scacc[:], sc8[:].rearrange("p (k c) -> p c k", k=8), AX.X).then_inc(sems["VE"], 1)
        blk.vector(_)
        W["p_scsum"] = bump("VE")

        def _(t):
            t.wait_ge(sems["VE"], W["p_scsum"])
            t.matmul(pb[6][0:1, 0:2], ONES_F, scacc[:], start=True, stop=True).then_inc(sems["PE"], 1)
        blk.tensor(_)
        W["p_tot"] = bump("PE")

        def _(sc):
            sc.wait_ge(sems["PE"], W["p_tot"])
            sc.mul(thr_s[0:1, 0:2], pb[6][0:1, 0:2], 0.7 / (DFF * DM)).then_inc(sems["AC"], 1)
        blk.scalar(_)
        W["p_thr"] = bump("AC")

        def _(t):
            t.wait_ge(sems["AC"], W["p_thr"])
            t.matmul(pb[6][:, 0:1], onesrow_f[:], thr_s[0:1, 0:1], start=True, stop=True)
            t.matmul(pb[6][:, 1:2], onesrow_f[:], thr_s[0:1, 1:2], start=True, stop=True).then_inc(sems["PE"], 1)
        blk.tensor(_)
        W["p_thrb"] = bump("PE")

        def _(sc):
            sc.wait_ge(sems["PE"], W["p_thrb"])
            sc.copy(thrv[:], pb[6][:, 0:2])
            sc.mul(nthrv[:], pb[6][:, 0:2], -1.0).then_inc(sems["AC"], 1)
        blk.scalar(_)
        W["p_thrv"] = bump("AC")

        for wi in range(2):
            wld = W["p_w1reload"] if wi == 0 else W["p_w2reload"]
            wq_loc_t = w1q_loc if wi == 0 else w2q_loc
            wq_all_t = w1q_all if wi == 0 else w2q_all
            dpar = NT if wi == 0 else FT

            def _(v, wld=wld, wi=wi):
                v.wait_ge(sems["dX"], wld)
                v.wait_ge(sems["AC"], W["p_thrv"])
                for hh in range(2):
                    sl = slice(hh * 8192, (hh + 1) * 8192)
                    v.tensor_scalar(tmpq[:], wslab[:, sl], nthrv[:, wi:wi + 1], None, ALU.is_le)
                    ins = v.scalar_tensor_tensor(wqslab[:, sl], wslab[:, sl], thrv[:, wi:wi + 1],
                                                 tmpq[:], ALU.is_ge, ALU.subtract)
                ins.then_inc(sems["VE"], 1)
            blk.vector(_)
            W[f"p_q{wi}"] = bump("VE")

            def _(sync, wi=wi, wq_loc_t=wq_loc_t, dpar=dpar):
                sync.wait_ge(sems["VE"], W[f"p_q{wi}"])
                sync.dma_start(out=wq_loc_t.rearrange("(f d p) c -> p f d c", p=128, d=dpar),
                               in_=wqslab[:].rearrange("p (f d c) -> p f d c", d=dpar, c=128)).then_inc(sems["dX"], 16)
                if wi == 0:
                    sync.dma_start(out=wslab[:].rearrange("p (f d c) -> p f d c", f=2, d=FT),
                           in_=w2s_in.rearrange("f d p c -> p f d c")).then_inc(sems["dX"], 16)
            blk.sync(_)
            W[f"p_qst{wi}"] = bump("dX", 16)
            if wi == 0:
                W["p_w2reload"] = bump("dX", 16)

            def _(g, wi=wi, wq_loc_t=wq_loc_t, wq_all_t=wq_all_t):
                g.wait_ge(sems["dX"], W[f"p_qst{wi}"])
                g.wait_ge(sems["CC"], cnt["CC"])
                g.collective_compute("AllGather", ALU.bypass, replica_groups=[list(range(8))],
                                     ins=[wq_loc_t], outs=[wq_all_t]).then_inc(sems["CC"], 1)
            blk.gpsimd(_)
            W[f"p_wAG{wi}"] = bump("CC")

    def emit_ln(tag, F, free_after, pre_bf16=False, xc_sem="dX"):
        xc, xcb, xsq, xn, scr = F["xc"], F["xcb"], F["xsq"], F["xn"], F["scr"]

        def _(v):
            v.wait_ge(sems[xc_sem], W[tag + "_xc"])
            v.wait_ge(sems["VE"], free_after)
            if pre_bf16:
                v.engine_nop().then_inc(sems["VE"], 1)
            else:
                v.tensor_copy(xcb[:, 0:NT * BLKR], xc[:]).then_inc(sems["VE"], 1)
        blk.vector(_)
        ve_cast = bump("VE")

        def _(sc):
            sc.wait_ge(sems["VE"], ve_cast)
            sc.activation(xsq[:], xc[:], AF.Square).then_inc(sems["AC"], 1)
        blk.scalar(_)
        ac_sq = bump("AC")

        def _(t):
            t.wait_ge(sems["VE"], ve_cast)
            t.wait_ge(sems["AC"], ac_sq)
            for dt in range(NT):
                t.matmul(pb[4][0:1, :], ONES_B, xcb[:, dt * 512:(dt + 1) * 512],
                         start=(dt == 0), stop=(dt == NT - 1))
            for dt in range(NT):
                ins = t.matmul(pb[5][0:1, :], ONES_B, xsq[:, dt * 512:(dt + 1) * 512],
                               start=(dt == 0), stop=(dt == NT - 1))
            ins.then_inc(sems["PE"], 1)
        blk.tensor(_)
        pe_sums = bump("PE")

        def _(sc):
            sc.wait_ge(sems["PE"], pe_sums)
            sc.mul(stat[0:1, 0:512], pb[4][0:1, :], -1.0 / DM)
            sc.mul(stat[0:1, 512:1024], pb[5][0:1, :], 1.0 / DM).then_inc(sems["AC"], 1)
        blk.scalar(_)
        ac_stat = bump("AC")

        def _(v):
            v.wait_ge(sems["AC"], ac_stat)
            v.tensor_mul(stat[0:1, 1024:1536], stat[0:1, 0:512], stat[0:1, 0:512])
            v.tensor_sub(stat[0:1, 1536:2048], stat[0:1, 512:1024], stat[0:1, 1024:1536]).then_inc(sems["VE"], 1)
        blk.vector(_)
        ve_var = bump("VE")

        def _(sc):
            sc.wait_ge(sems["VE"], ve_var)
            sc.activation(stat[0:1, 1024:1536], stat[0:1, 1536:2048], AF.Ln, bias=constf[0:1, 82:83])
            sc.activation(stat[0:1, 1536:2048], stat[0:1, 1024:1536], AF.Exp, scale=-0.5)
            sc.copy(statb[0:1, 0:512], stat[0:1, 0:512])
            sc.copy(statb[0:1, 512:1024], stat[0:1, 1536:2048]).then_inc(sems["AC"], 1)
        blk.scalar(_)
        ac_rstd = bump("AC")

        def _(t):
            t.wait_ge(sems["AC"], ac_rstd)
            t.matmul(pb[4][:], ONESROW_B, statb[0:1, 0:512], start=True, stop=True)
            t.matmul(pb[5][:], ONESROW_B, statb[0:1, 512:1024], start=True, stop=True).then_inc(sems["PE"], 1)
        blk.tensor(_)
        pe_bc = bump("PE")

        def _(v):
            v.wait_ge(sems["PE"], pe_bc)
            for dt in range(NT):
                v.tensor_add(scr[:], xc[:, dt * 512:(dt + 1) * 512], pb[4][:])
                ins = v.tensor_mul(xn[:, dt * 512:(dt + 1) * 512], scr[:], pb[5][:])
            ins.then_inc(sems["VE"], 1)
        blk.vector(_)
        W[tag + "_apply"] = bump("VE")

    def phase_ffn(name, F, xin, xout, out_sem, free0, dx_gate=0):
        for b in range(NB):
            tag = f"{name}_b{b}"
            prev_free = W.get(f"{name}_b{b-1}_evacall", free0)

            def _(sync, b=b, prev_free=prev_free, dx_gate=dx_gate if b == 0 else 0, xin=xin, F=F):
                sync.wait_ge(sems["VE"], prev_free)
                if dx_gate:
                    sync.wait_ge(sems["dX"], dx_gate)
                sync.dma_start(
                    out=F["xc"][:].rearrange("p (dt r) -> p dt r", dt=NT),
                    in_=xin.rearrange("(dt p) r -> p dt r", p=128)[:, :, ds(b * BLKR, BLKR)],
                ).then_inc(sems["dX"], 16)
            blk.sync(_)
            W[tag + "_xc"] = bump("dX", 16)

            emit_ln(tag, F, prev_free)

            gelu_base = cnt["AC"]
            dw_w1_base = cnt["dW"]

            def _(sync, gelu_base=gelu_base, F=F):
                for ft in range(FT):
                    if ft >= 2:
                        sync.wait_ge(sems["AC"], gelu_base + ft - 1)
                    sync.dma_start(
                        out=F["w1b"][:, (ft % 2) * DM:(ft % 2) * DM + DM].rearrange("p (d c) -> p d c", c=128),
                        in_=w1q_all[ft * DM:(ft + 1) * DM, :].rearrange("(d p) c -> p d c", p=128),
                    ).then_inc(sems["dW"], 16)
            blk.sync(_)
            bump("dW", 16 * FT)

            pe_h_base = cnt["PE"]

            def _(t, tag=tag, gelu_base=gelu_base, dw_w1_base=dw_w1_base, F=F):
                t.wait_ge(sems["VE"], W[tag + "_apply"])
                for ft in range(FT):
                    t.wait_ge(sems["dW"], dw_w1_base + (ft + 1) * 16)
                    if ft >= 2:
                        t.wait_ge(sems["AC"], gelu_base + ft - 1)
                    for dt in range(NT):
                        ins = t.matmul(pb[ft % 2][:],
                                       F["w1b"][:, (ft % 2) * DM + dt * 128:(ft % 2) * DM + (dt + 1) * 128],
                                       F["xn"][:, dt * 512:(dt + 1) * 512],
                                       start=(dt == 0), stop=(dt == NT - 1))
                    ins.then_inc(sems["PE"], 1)
            blk.tensor(_)
            bump("PE", FT)

            def _(sc, pe_h_base=pe_h_base, F=F):
                for ft in range(FT):
                    sc.wait_ge(sems["PE"], pe_h_base + ft + 1)
                    sc.activation(F["hT"][:, ft * 512:(ft + 1) * 512], pb[ft % 2][:],
                                  AF.Gelu, bias=B1COL(ft)).then_inc(sems["AC"], 1)
            blk.scalar(_)
            bump("AC", FT)
            ac_gelu_all = cnt["AC"]
            assert ac_gelu_all == gelu_base + FT

            evac_base = cnt["VE"]
            dw_w2_base = cnt["dW"]
            prev_evacall = W.get(f"{name}_b{b-1}_evacall", 0)

            def _(sync, evac_base=evac_base, F=F):
                for dmt in range(16):
                    if dmt >= 2:
                        sync.wait_ge(sems["VE"], evac_base + dmt - 1)
                    sync.dma_start(
                        out=F["w2b"][:, (dmt % 2) * DFF:(dmt % 2) * DFF + DFF].rearrange("p (f c) -> p f c", c=128),
                        in_=w2q_all[dmt * DFF:(dmt + 1) * DFF, :].rearrange("(f p) c -> p f c", p=128),
                    ).then_inc(sems["dW"], 16)
            blk.sync(_)
            bump("dW", 16 * 16)

            pe_o_base = cnt["PE"]

            def _(t, ac_gelu_all=ac_gelu_all, evac_base=evac_base,
                  dw_w2_base=dw_w2_base, prev_evacall=prev_evacall, F=F):
                t.wait_ge(sems["AC"], ac_gelu_all)
                if prev_evacall:
                    t.wait_ge(sems["VE"], prev_evacall)
                for dmt in range(16):
                    t.wait_ge(sems["dW"], dw_w2_base + (dmt + 1) * 16)
                    if dmt >= 2:
                        t.wait_ge(sems["VE"], evac_base + dmt - 1)
                    for ft in range(FT):
                        t.matmul(pb[2 + dmt % 2][:],
                                 F["w2b"][:, (dmt % 2) * DFF + ft * 128:(dmt % 2) * DFF + (ft + 1) * 128],
                                 F["hT"][:, ft * 512:(ft + 1) * 512],
                                 start=(ft == 0), stop=False)
                    ins = t.matmul(pb[2 + dmt % 2][:], B2ROW(dmt), ONESROW512_B,
                                   start=False, stop=True)
                    ins.then_inc(sems["PE"], 1)
            blk.tensor(_)
            bump("PE", 16)

            xst_base = cnt[out_sem]

            def _(v, pe_o_base=pe_o_base, xst_base=xst_base, out_sem=out_sem, F=F):
                for dmt in range(16):
                    v.wait_ge(sems["PE"], pe_o_base + dmt + 1)
                    if dmt >= 2:
                        v.wait_ge(sems[out_sem], xst_base + (dmt - 1) * 16)
                    v.scalar_tensor_tensor(F["xo"][:, (dmt % 2) * 512:(dmt % 2) * 512 + 512],
                                           pb[2 + dmt % 2][:], 0.5,
                                           F["xc"][:, dmt * 512:(dmt + 1) * 512],
                                           ALU.mult, ALU.add).then_inc(sems["VE"], 1)
            blk.vector(_)
            bump("VE", 16)
            W[tag + "_evacall"] = cnt["VE"]

            def _(g, b=b, evac_base=evac_base, xout=xout, out_sem=out_sem, F=F):
                for dmt in range(16):
                    g.wait_ge(sems["VE"], evac_base + dmt + 1)
                    g.dma_start(
                        out=xout.rearrange("(dt p) r -> p dt r", p=128)[:, dmt:dmt + 1, ds(b * BLKR, BLKR)],
                        in_=F["xo"][:, (dmt % 2) * 512:(dmt % 2) * 512 + 512].rearrange("p (o r) -> p o r", o=1),
                    ).then_inc(sems[out_sem], 16)
            blk.gpsimd(_)
            bump(out_sem, 16 * 16)
            W[tag + "_stored"] = cnt[out_sem]
        return W[f"{name}_b{NB-1}_evacall"]

    def alloc_ffn_bufs(stack, pfx):
        F = {}
        F["xc"] = stack.enter_context(nc.sbuf_tensor(pfx + "xc", [128, NT * BLKR], F32))
        F["xn"] = stack.enter_context(nc.sbuf_tensor(pfx + "xn", [128, NT * BLKR], BF16))
        F["hT"] = stack.enter_context(nc.sbuf_tensor(pfx + "hT", [128, FT * BLKR], BF16))
        F["w1b"] = stack.enter_context(nc.sbuf_tensor(pfx + "w1b", [128, 2 * DM], BF16))
        F["w2b"] = stack.enter_context(nc.sbuf_tensor(pfx + "w2b", [128, 2 * DFF], BF16))
        F["scr"] = stack.enter_context(nc.sbuf_tensor(pfx + "scr", [128, 512], F32))
        F["scr2"] = stack.enter_context(nc.sbuf_tensor(pfx + "scr2", [128, 2 * 512], F32))
        F["xo"] = stack.enter_context(nc.sbuf_tensor(pfx + "xo", [128, 2 * 512], F32))
        F["xsq"] = F["xn"]
        F["xcb"] = F["hT"]
        return F



    BLK_SHIP = 3328
    BSQR, BQS, BSKR, BKS, BGL, BV = 0, 1024, 2048, 2304, 2560, 2816
    CSQR, CQS, CSKR, CKS, CGL, CV = 0, 4096, 8192, 9216, 10240, 11264
    PoolE = mybir.EngineType.Pool

    def phase_attn(free0_ve, pe_free, dx_x1_stored):
        cc_qkv_base = cnt["CC"]

        snd = ExitStack()
        SA = lambda n, s, d: snd.enter_context(nc.sbuf_tensor(n, s, d))
        axc = SA("axc", [128, NT * BLKR], BF16)
        axn = SA("axn", [128, NT * BLKR], BF16)
        ascr = SA("ascr", [128, 512], F32)
        wnb = SA("wnb", [128, 2 * NT * 512], BF16)
        qnat = SA("qnat", [128, 4 * 2048], BF16)
        rnat = SA("rnat", [128, 4 * 2048], BF16)
        qsn = SA("qsn", [128, 4 * 2048], BF16)
        eqb = SA("eqb", [128, 4 * 2048], BF16)
        rsc = SA("rsc", [128, 4096], BF16)
        srec = SA("srec", [128, 128], F32)
        knat = SA("knat", [128, 4 * 512], BF16)
        krn = SA("krn", [128, 4 * 512], BF16)
        ksn = SA("ksn", [128, 4 * 512], BF16)
        ekb = SA("ekb", [128, 4 * 512], BF16)
        ktrb = SA("ktrb", [128, 4 * 512], BF16)
        glb = SA("glb", [128, 4 * 512], BF16)
        dsc = SA("dsc", [128, 2048], F32)
        shs = SA("shs", [128, 4 * 512], BF16)
        AF_ = {"xc": axc, "xcb": axc, "xsq": axn, "xn": axn, "scr": ascr}
        qrecv = SA("qrecv", [128, 80], F32)

        for c in range(NB):
            tag = f"at_c{c}"
            prev_apply = W.get(f"at_c{c-1}_apply", free0_ve)
            prev_done = W.get(f"at_c{c-1}_shipdone", 0)

            def _(g, c=c, prev_apply=prev_apply, prev_done=prev_done):
                g.wait_ge(sems["dX"], dx_x1_stored[c])
                g.wait_ge(sems["VE"], prev_apply)
                if c == 0:
                    g.wait_ge(sems["PE"], pe_free)
                if prev_done:
                    g.wait_ge(sems["dX"], prev_done)
                ins = g.dma_start(
                    out=axc[:].rearrange("p (dt r) -> p dt r", dt=NT),
                    in_=x1T.rearrange("(dt p) r -> p dt r", p=128)[:, :, ds(c * BLKR, BLKR)],
                )
                ins.then_inc(sems["dG"], 16)
                W[f"at_c{c}_xc"] = bump("dG", 16)
            blk.gpsimd(_)

            emit_ln(tag, AF_, prev_apply, pre_bf16=True, xc_sem="dG")

            pe_proj_base = cnt["PE"]
            ac_ev_base = cnt["AC"]
            dg_wn = []

            def _(g, pe_proj_base=pe_proj_base):
                for og in range(6):
                    if og >= 2:
                        g.wait_ge(sems["PE"], pe_proj_base + (og - 1) * 4)
                    ins = g.dma_start(
                        out=wnb[:, (og % 2) * 8192:(og % 2) * 8192 + 8192].rearrange("p (dt c) -> p dt c", dt=NT),
                        in_=wqkv_in[og].rearrange("dt p c -> p dt c"),
                    )
                    ins.then_inc(sems["dG"], 16)
                    dg_wn.append(bump("dG", 16))
            blk.gpsimd(_)

            def _(t, tag=tag, ac_ev_base=ac_ev_base):
                t.wait_ge(sems["VE"], W[tag + "_apply"])
                for og in range(6):
                    t.wait_ge(sems["dG"], dg_wn[og])
                    for rs in range(4):
                        i = og * 4 + rs
                        if i >= 2:
                            t.wait_ge(sems["AC"], ac_ev_base + i - 1)
                        for dt in range(NT):
                            ins = t.matmul(pb[i % 2][:],
                                           axn[:, dt * 512 + rs * 128: dt * 512 + (rs + 1) * 128],
                                           wnb[:, (og % 2) * 8192 + dt * 512:(og % 2) * 8192 + (dt + 1) * 512],
                                           start=(dt == 0), stop=(dt == NT - 1))
                        ins.then_inc(sems["PE"], 1)
                        bump("PE")
            blk.tensor(_)

            dx_vship_base = cnt["dX"]
            ac_evs = []

            def _(sc, pe_proj_base=pe_proj_base, dx_vship_base=dx_vship_base):
                for og in range(6):
                    for rs in range(4):
                        i = og * 4 + rs
                        sc.wait_ge(sems["PE"], pe_proj_base + i + 1)
                        if og < 4:
                            ins = sc.copy(qnat[:, rs * 2048 + og * 512: rs * 2048 + (og + 1) * 512], pb[i % 2][:])
                        elif og == 4:
                            ins = sc.copy(knat[:, rs * 512:(rs + 1) * 512], pb[i % 2][:])
                        else:
                            if rs >= 2:
                                sc.wait_ge(sems["dX"], dx_vship_base + (rs - 1) * 16)
                            ins = sc.copy(shs[:, (rs % 2) * 512:(rs % 2) * 512 + 512], pb[i % 2][:])
                        ins.then_inc(sems["AC"], 1)
                        ac_evs.append(bump("AC"))
            blk.scalar(_)

            def _(sync, c=c):
                for rs in range(4):
                    sync.wait_ge(sems["AC"], ac_evs[20 + rs])
                    for b_ in range(2):
                        row = b_ * BLK_SHIP + BV + rs * 128
                        ins = sync.dma_start(
                            out=qkv_in[c][row: row + 128, 0:256],
                            in_=shs[:, (rs % 2) * 512 + b_ * 256:(rs % 2) * 512 + b_ * 256 + 256])
                        ins.then_inc(sems["dX"], 16)
                        bump("dX", 16)
                        ins = sync.dma_start(
                            out=qkv_in[c][row: row + 128, 256:512],
                            in_=shs[:, (rs % 2) * 512 + (1 - b_) * 256:(rs % 2) * 512 + (1 - b_) * 256 + 256])
                        ins.then_inc(sems["dX"], 16)
                        bump("dX", 16)
            blk.sync(_)

            def _(sc):
                sc.wait_ge(sems["AC"], ac_evs[15])
                ins = sc.activation(eqb[:], qnat[:], AF.Exp)
                ins.then_inc(sems["AC"], 1)
                W[tag + "_eq"] = bump("AC")
                sc.wait_ge(sems["AC"], ac_evs[19])
                ins = sc.activation(ekb[:], knat[:], AF.Exp)
                ins.then_inc(sems["AC"], 1)
                W[tag + "_ek"] = bump("AC")
            blk.scalar(_)

            def _(v, c=c, tag=tag):
                v.wait_ge(sems["AC"], ac_evs[19])
                for rs in range(4):
                    tile = c * 4 + rs
                    cosNq = trig[:, tile * 64: tile * 64 + 64].rearrange("p (o d) -> p o d", o=1).broadcast_to([128, 16, 64])
                    sinNq = trig[:, 2048 + tile * 64: 2048 + tile * 64 + 64].rearrange("p (o d) -> p o d", o=1).broadcast_to([128, 16, 64])
                    qv = qnat[:, rs * 2048:(rs + 1) * 2048].rearrange("p (h d) -> p h d", d=128)
                    rv = rnat[:, rs * 2048:(rs + 1) * 2048].rearrange("p (h d) -> p h d", d=128)
                    s0 = rsc[:, 0:1024].rearrange("p (h d) -> p h d", d=64)
                    s1 = rsc[:, 1024:2048].rearrange("p (h d) -> p h d", d=64)
                    s2 = rsc[:, 2048:3072].rearrange("p (h d) -> p h d", d=64)
                    s3 = rsc[:, 3072:4096].rearrange("p (h d) -> p h d", d=64)
                    v.tensor_mul(s0, qv[:, :, 0:64], cosNq)
                    v.tensor_mul(s1, qv[:, :, 64:128], sinNq)
                    v.tensor_mul(s2, qv[:, :, 64:128], cosNq)
                    ins = v.tensor_mul(s3, qv[:, :, 0:64], sinNq)
                    ins.then_inc(sems["VE"], 1)
                    v.wait_ge(sems["VE"], bump("VE"))
                    v.tensor_sub(rv[:, :, 0:64], s0, s1)
                    v.tensor_add(rv[:, :, 64:128], s2, s3)
                    cosNk = trig[:, tile * 64: tile * 64 + 64].rearrange("p (o d) -> p o d", o=1).broadcast_to([128, 4, 64])
                    sinNk = trig[:, 2048 + tile * 64: 2048 + tile * 64 + 64].rearrange("p (o d) -> p o d", o=1).broadcast_to([128, 4, 64])
                    kv_ = knat[:, rs * 512:(rs + 1) * 512].rearrange("p (h d) -> p h d", d=128)
                    kr = krn[:, rs * 512:(rs + 1) * 512].rearrange("p (h d) -> p h d", d=128)
                    k0 = rsc[:, 0:256].rearrange("p (h d) -> p h d", d=64)
                    k1 = rsc[:, 256:512].rearrange("p (h d) -> p h d", d=64)
                    k2 = rsc[:, 512:768].rearrange("p (h d) -> p h d", d=64)
                    k3 = rsc[:, 768:1024].rearrange("p (h d) -> p h d", d=64)
                    v.tensor_mul(k0, kv_[:, :, 0:64], cosNk)
                    v.tensor_mul(k1, kv_[:, :, 64:128], sinNk)
                    v.tensor_mul(k2, kv_[:, :, 64:128], cosNk)
                    ins = v.tensor_mul(k3, kv_[:, :, 0:64], sinNk)
                    ins.then_inc(sems["VE"], 1)
                    v.wait_ge(sems["VE"], bump("VE"))
                    v.tensor_sub(kr[:, :, 0:64], k0, k1)
                    ins = v.tensor_add(kr[:, :, 64:128], k2, k3)
                    ins.then_inc(sems["VE"], 1)
                    bump("VE")
                v.wait_ge(sems["AC"], W[tag + "_ek"])
                for rs in range(4):
                    v.reduce_sum(srec[:, rs * 16:(rs + 1) * 16],
                                 eqb[:, rs * 2048:(rs + 1) * 2048].rearrange("p (h d) -> p h d", d=128), AX.X)
                for rs in range(4):
                    ins = v.reduce_sum(srec[:, 64 + rs * 4:64 + (rs + 1) * 4],
                                       ekb[:, rs * 512:(rs + 1) * 512].rearrange("p (h d) -> p h d", d=128), AX.X)
                ins.then_inc(sems["VE"], 1)
                v.wait_ge(sems["VE"], bump("VE"))
                ins = v.reciprocal(qrecv[:], srec[:, 0:80])
                ins.then_inc(sems["VE"], 1)
                v.wait_ge(sems["VE"], bump("VE"))
                for rs in range(4):
                    v.tensor_mul(qsn[:, rs * 2048:(rs + 1) * 2048].rearrange("p (h d) -> p h d", d=128),
                                 eqb[:, rs * 2048:(rs + 1) * 2048].rearrange("p (h d) -> p h d", d=128),
                                 qrecv[:, rs * 16:(rs + 1) * 16].rearrange("p (h o) -> p h o", o=1).broadcast_to([128, 16, 128]))
                for rs in range(4):
                    ins = v.tensor_mul(ksn[:, rs * 512:(rs + 1) * 512].rearrange("p (h d) -> p h d", d=128),
                                       ekb[:, rs * 512:(rs + 1) * 512].rearrange("p (h d) -> p h d", d=128),
                                       qrecv[:, 64 + rs * 4:64 + (rs + 1) * 4].rearrange("p (h o) -> p h o", o=1).broadcast_to([128, 4, 128]))
                ins.then_inc(sems["VE"], 1)
                W[tag + "_derive"] = bump("VE")
            blk.vector(_)

            tlist = [(rnat, BSQR, 16, 2048), (qsn, BQS, 16, 2048),
                     (krn, BSKR, 4, 512), (ksn, BKS, 4, 512), (knat, -1, 4, 512)]
            pe_tr_base = cnt["PE"]
            ac_tr_base = cnt["AC"]
            dx_sh_base = cnt["dX"]

            def _(t, tag=tag, ac_tr_base=ac_tr_base):
                t.wait_ge(sems["VE"], W[tag + "_derive"])
                gi = 0
                for (buf, shiprow, nodt, w) in tlist:
                    for odt in range(nodt):
                        if gi >= 2:
                            t.wait_ge(sems["AC"], ac_tr_base + gi - 1)
                        for rs in range(4):
                            ins = t.matmul(pb[2 + gi % 2][:, rs * 128:(rs + 1) * 128],
                                           buf[:, rs * w + odt * 128: rs * w + (odt + 1) * 128],
                                           IDENT, start=(rs == 0), stop=(rs == 3))
                        ins.then_inc(sems["PE"], 1)
                        bump("PE")
                        gi += 1
            blk.tensor(_)

            def _(sc, tag=tag, pe_tr_base=pe_tr_base, dx_sh_base=dx_sh_base):
                gi = 0
                si = 0
                for (buf, shiprow, nodt, w) in tlist:
                    for odt in range(nodt):
                        sc.wait_ge(sems["PE"], pe_tr_base + gi + 1)
                        if shiprow >= 0:
                            if si < 4:
                                sc.wait_ge(sems["dX"], dx_sh_base)
                            else:
                                sc.wait_ge(sems["dX"], dx_sh_base + (si - 3) * 16)
                            ins = sc.copy(shs[:, (si % 4) * 512:(si % 4) * 512 + 512], pb[2 + gi % 2][:])
                            si += 1
                        else:
                            ins = sc.copy(ktrb[:, odt * 512:(odt + 1) * 512], pb[2 + gi % 2][:])
                        ins.then_inc(sems["AC"], 1)
                        bump("AC")
                        gi += 1
            blk.scalar(_)

            def _(sync, c=c, ac_tr_base=ac_tr_base):
                gi = 0
                si = 0
                for (buf, shiprow, nodt, w) in tlist:
                    for odt in range(nodt):
                        if shiprow >= 0:
                            sync.wait_ge(sems["AC"], ac_tr_base + gi + 1)
                            b_ = odt // (nodt // 2)
                            lo = odt % (nodt // 2)
                            row = b_ * BLK_SHIP + shiprow + lo * 128
                            ins = sync.dma_start(
                                out=qkv_in[c][row: row + 128, :],
                                in_=shs[:, (si % 4) * 512:(si % 4) * 512 + 512])
                            ins.then_inc(sems["dX"], 16)
                            bump("dX", 16)
                            si += 1
                        gi += 1
            blk.sync(_)
            ac_ktr_done = cnt["AC"]

            ve_gl0 = cnt["VE"] + 1

            def _(sc, ac_ktr_done=ac_ktr_done, ve_gl0=ve_gl0):
                sc.wait_ge(sems["AC"], ac_ktr_done)
                for odt in range(4):
                    if odt >= 1:
                        sc.wait_ge(sems["VE"], ve_gl0 + odt - 1)
                    sc.activation(dsc[:, 0:512], ktrb[:, odt * 512:(odt + 1) * 512], AF.Abs)
                    sc.activation(dsc[:, 512:1024], dsc[:, 0:512], AF.Exp, scale=-1.0)
                    ins = sc.activation(dsc[:, 0:512], dsc[:, 512:1024], AF.Ln, bias=1.0)
                    ins.then_inc(sems["AC"], 1)
                    bump("AC")
            blk.scalar(_)
            ac_gl_base = ac_ktr_done

            def _(v, tag=tag, ac_gl_base=ac_gl_base):
                for odt in range(4):
                    v.wait_ge(sems["AC"], ac_gl_base + odt + 1)
                    v.tensor_scalar_min(dsc[:, 1024:1536], ktrb[:, odt * 512:(odt + 1) * 512], 0.0)
                    ins = v.tensor_sub(glb[:, odt * 512:(odt + 1) * 512], dsc[:, 1024:1536], dsc[:, 0:512])
                    ins.then_inc(sems["VE"], 1)
                    bump("VE")
            blk.vector(_)
            ve_gl_base = cnt["VE"] - 4

            def _(sync, c=c, ve_gl_base=ve_gl_base):
                for odt in range(4):
                    sync.wait_ge(sems["VE"], ve_gl_base + odt + 1)
                    b_ = odt // 2
                    row = b_ * BLK_SHIP + BGL + (odt % 2) * 128
                    ins = sync.dma_start(out=qkv_in[c][row: row + 128, :],
                                         in_=glb[:, odt * 512:(odt + 1) * 512])
                    ins.then_inc(sems["dX"], 16)
                    bump("dX", 16)
            blk.sync(_)
            W[tag + "_shipdone"] = cnt["dX"]

            def _(g, c=c, tag=tag):
                g.wait_ge(sems["dX"], W[tag + "_shipdone"])
                g.wait_ge(sems["CC"], cnt["CC"])
                ins = g.collective_compute("AllGather", ALU.bypass,
                                           replica_groups=[[0, 1], [2, 3], [4, 5], [6, 7]],
                                           ins=[qkv_in[c]], outs=[qkv_out[c]])
                ins.then_inc(sems["CC"], 1)
                bump("CC")
            blk.gpsimd(_)

        snd_pe_end = cnt["PE"]
        snd_ve_end = cnt["VE"]
        snd_ac_end = cnt["AC"]
        snd.close()

        rcv = ExitStack()
        RA = lambda n, s, d: rcv.enter_context(nc.sbuf_tensor(n, s, d))
        stg = RA("stg", [128, 26 * 512], BF16)
        skrT = RA("skrT", [128, KVH * SEQ], BF16)
        vful = RA("vful", [128, KVH * 32 * 128], BF16)
        Acum = RA("Acum", [128, KVH * 512], F32)
        eAb = RA("eAb", [128, KVH * 512], F32)
        eAn = RA("eAn", [128, KVH * 512], F32)
        qt_ = RA("qt_", [128, H * 512], BF16)
        kt_ = RA("kt_", [128, KVH * 512], BF16)
        ktn = RA("ktn", [128, KVH * 128], BF16)
        zf = RA("zf", [128, 128], BF16)
        Sst = RA("Sst", [128, KVH * 128], F32)
        Ssc = RA("Ssc", [128, KVH * 128], F32)
        Sb = RA("Sb", [128, KVH * 128], BF16)
        pgb = RA("pgb", [128, 128], BF16)
        o05 = RA("o05", [128, H * 512], BF16)
        oTb = RA("oTb", [128, H * 512], BF16)
        Pring = RA("Pring", [128, 3 * 512], BF16)
        hdr = RA("hdr", [1, 512], F32)
        hdrb = RA("hdrb", [1, 512], BF16)
        hden = RA("hden", [128, 512], F32)
        yscr = RA("yscr", [128, 512], F32)
        ost = RA("ost", [128, 16 * 512], BF16)
        wob = RA("wob", [128, 2 * 2048], BF16)
        x1r = RA("x1r", [128, 2 * 512], F32)
        x2o = RA("x2o", [128, 2 * 512], F32)

        def _(v):
            v.wait_ge(sems["VE"], snd_ve_end)
            v.wait_ge(sems["PE"], snd_pe_end)
            v.wait_ge(sems["AC"], snd_ac_end)
            v.memset(zf[:], 0.0)
            ins = v.memset(Sst[:], 0.0)
            ins.then_inc(sems["VE"], 1)
            W["rcv_init"] = bump("VE")
        blk.vector(_)

        dg0 = cnt["dG"]
        for g_ in range(NG):
            W[f"g{g_}_staged"] = dg0 + (g_ + 1) * 16
        dg_p3 = dg0 + NG * 16
        for c_ in range(NB):
            W[f"wo_c{c_}_ost"] = dg_p3 + (c_ * 17 + 1) * 16
            for dmt_ in range(16):
                W[f"wo_c{c_}_wo{dmt_}"] = dg_p3 + (c_ * 17 + 1 + dmt_ + 1) * 16
        out20 = cnt["OUT"]
        for c_ in range(NB):
            for dmt_ in range(16):
                W[f"wo_c{c_}_st{dmt_}"] = out20 + (c_ * 16 + dmt_ + 1) * 16

        g_pe_end = {-1: snd_pe_end}
        g_ve_end = {-1: W["rcv_init"]}
        g_ac_end = {-1: snd_ac_end}
        g_ship = {-1: 0}

        for g in range(NG):
            slot, step = g // 4, g % 4
            src2d = qkv_out[step]


            def _(v, g=g):
                v.wait_ge(sems["dG"], W[f"g{g}_staged"])
                v.tensor_copy(skrT[:].rearrange("p (kv r) -> p kv r", kv=KVH)[:, :, g * 512:(g + 1) * 512],
                              stg[:, CSKR: CSKR + 1024].rearrange("p (kv r) -> p kv r", kv=KVH))
                for kv in range(KVH):
                    v.tensor_copy(
                        vful[:, kv * 4096 + g * 512: kv * 4096 + (g + 1) * 512].rearrange("p (rs d) -> p rs d", d=128),
                        stg[:, CV:].rearrange("p (rs c) -> p rs c", c=512)[:, :, kv * 128:(kv + 1) * 128])
                for kv in range(KVH):
                    for j in range(4):
                        ins = v.tensor_tensor_scan(Acum[:, kv * 512 + j * 128: kv * 512 + (j + 1) * 128],
                                                   stg[:, CGL + kv * 512 + j * 128: CGL + kv * 512 + (j + 1) * 128],
                                                   zf[:], 0.0, ALU.add, ALU.add)
                ins.then_inc(sems["VE"], 1)
                W[f"g{g}_scan"] = bump("VE")
            blk.vector(_)

            def _(sc, g=g):
                sc.wait_ge(sems["VE"], W[f"g{g}_scan"])
                sc.activation(eAb[:], Acum[:], AF.Exp, scale=1.0 / 16.0)
                ins = sc.activation(eAn[:], Acum[:], AF.Exp, scale=-1.0 / 16.0)
                ins.then_inc(sems["AC"], 1)
                W[f"g{g}_eA"] = bump("AC")
            blk.scalar(_)

            def _(v, g=g):
                v.wait_ge(sems["AC"], W[f"g{g}_eA"])
                for h in range(H):
                    v.tensor_mul(qt_[:, h * 512:(h + 1) * 512], stg[:, CQS + h * 512: CQS + (h + 1) * 512],
                                 eAb[:, (h // 4) * 512:(h // 4 + 1) * 512])
                for kv in range(KVH):
                    ins = v.tensor_mul(kt_[:, kv * 512:(kv + 1) * 512], stg[:, CKS + kv * 512: CKS + (kv + 1) * 512],
                                       eAn[:, kv * 512:(kv + 1) * 512])
                ins.then_inc(sems["VE"], 1)
                W[f"g{g}_qtkt"] = bump("VE")
            blk.vector(_)

            for j in range(4):
                jt = f"g{g}j{j}"
                def _(sc, g=g, j=j, jt=jt):
                    if j > 0:
                        sc.wait_ge(sems["VE"], W[f"g{g}j{j-1}_supd"])
                    elif g > 0:
                        sc.wait_ge(sems["VE"], W[f"g{g-1}j3_supd"])
                    else:
                        sc.wait_ge(sems["VE"], W["rcv_init"])
                    if g == 0 and j == 0:
                        pass
                    sc.wait_ge(sems["VE"], W[f"g{g}_qtkt"])
                    if j == 0 and g > 0:
                        sc.wait_ge(sems["VE"], W[f"g{g-1}_oT"])
                    ins = sc.copy(Sb[:], Sst[:])
                    ins.then_inc(sems["AC"], 1)
                    W[jt + "_sb"] = bump("AC")
                blk.scalar(_)

                ac_ktn0 = cnt["AC"] + 1

                def _(t, g=g, j=j, jt=jt, ac_ktn0=ac_ktn0):
                    t.wait_ge(sems["VE"], W[f"g{g}_qtkt"])
                    if j > 0:
                        t.wait_ge(sems["VE"], W[f"g{g}j{j-1}_supd"])
                    elif g > 0:
                        t.wait_ge(sems["VE"], W[f"g{g-1}j3_supd"])
                    for kv in range(KVH):
                        if kv == 1:
                            t.wait_ge(sems["AC"], ac_ktn0)
                        ins = t.matmul(pb[6][:, 0:128],
                                       kt_[:, kv * 512 + j * 128: kv * 512 + (j + 1) * 128],
                                       IDENT, start=True, stop=True)
                        ins.then_inc(sems["PE"], 1)
                        W[jt + f"_ktr{kv}"] = bump("PE")
                blk.tensor(_)

                def _(sc, jt=jt):
                    for kv in range(KVH):
                        sc.wait_ge(sems["PE"], W[jt + f"_ktr{kv}"])
                        ins = sc.copy(ktn[:, kv * 128:(kv + 1) * 128], pb[6][:, 0:128])
                        ins.then_inc(sems["AC"], 1)
                        W[jt + f"_ktn{kv}"] = bump("AC")
                blk.scalar(_)

                def _(t, g=g, j=j, jt=jt):
                    t.wait_ge(sems["AC"], W[jt + "_ktn1"])
                    for kv in range(KVH):
                        ins = t.matmul(pb[6][:, 128 + kv * 128: 128 + (kv + 1) * 128],
                                       ktn[:, kv * 128:(kv + 1) * 128],
                                       vful[:, kv * 4096 + (g * 4 + j) * 128: kv * 4096 + (g * 4 + j) * 128 + 128],
                                       start=True, stop=True)
                        ins.then_inc(sems["PE"], 1)
                        W[jt + f"_u0{kv}"] = bump("PE")
                blk.tensor(_)

                for h in range(H):
                    kv = h // 4
                    def _(t, g=g, j=j, h=h, kv=kv, jt=jt):
                        if h >= 2:
                            t.wait_ge(sems["VE"], W[jt + f"_pg{h-2}"])
                        ins = t.matmul(pb[4 + h % 2][:, 0:128],
                                       kt_[:, kv * 512 + j * 128: kv * 512 + (j + 1) * 128],
                                       qt_[:, h * 512 + j * 128: h * 512 + (j + 1) * 128],
                                       start=True, stop=True)
                        ins.then_inc(sems["PE"], 1)
                        W[jt + f"_pgm{h}"] = bump("PE")
                    blk.tensor(_)

                    def _(v, h=h, jt=jt):
                        v.wait_ge(sems["PE"], W[jt + f"_pgm{h}"])
                        if h >= 1:
                            v.wait_ge(sems["PE"], W[jt + f"_om{h-1}"])
                        ins = v.tensor_mul(pgb[:], pb[4 + h % 2][:, 0:128], MASKGLA)
                        ins.then_inc(sems["VE"], 1)
                        W[jt + f"_pg{h}"] = bump("VE")
                    blk.vector(_)

                    def _(t, g=g, j=j, h=h, kv=kv, jt=jt):
                        t.wait_ge(sems["VE"], W[jt + f"_pg{h}"])
                        t.wait_ge(sems["AC"], W[jt + "_sb"])
                        if h >= 2:
                            t.wait_ge(sems["AC"], W[jt + f"_oev{h-2}"])
                        t.matmul(pb[2 + h % 2][:, 0:128],
                                 vful[:, kv * 4096 + (g * 4 + j) * 128: kv * 4096 + (g * 4 + j) * 128 + 128],
                                 pgb[:], start=True, stop=False)
                        ins = t.matmul(pb[2 + h % 2][:, 0:128],
                                       Sb[:, kv * 128:(kv + 1) * 128],
                                       qt_[:, h * 512 + j * 128: h * 512 + (j + 1) * 128],
                                       start=False, stop=True)
                        ins.then_inc(sems["PE"], 1)
                        W[jt + f"_om{h}"] = bump("PE")
                    blk.tensor(_)

                    def _(sc, g=g, j=j, h=h, jt=jt):
                        sc.wait_ge(sems["PE"], W[jt + f"_om{h}"])
                        ins = sc.mul(o05[:, h * 512 + j * 128: h * 512 + (j + 1) * 128],
                                     pb[2 + h % 2][:, 0:128], 0.5)
                        ins.then_inc(sems["AC"], 1)
                        W[jt + f"_oev{h}"] = bump("AC")
                    blk.scalar(_)

                def _(v, g=g, j=j, jt=jt):
                    v.wait_ge(sems["PE"], W[jt + "_u01"])
                    v.wait_ge(sems["AC"], W[jt + "_sb"])
                    for kv in range(KVH):
                        eac = eAb[:, kv * 512 + j * 128 + 127: kv * 512 + j * 128 + 128]
                        v.tensor_add(Ssc[:, kv * 128:(kv + 1) * 128], Sst[:, kv * 128:(kv + 1) * 128],
                                     pb[6][:, 128 + kv * 128: 128 + (kv + 1) * 128])
                        ins = v.tensor_scalar(Sst[:, kv * 128:(kv + 1) * 128], Ssc[:, kv * 128:(kv + 1) * 128],
                                              eac, None, ALU.mult)
                        ins.then_inc(sems["VE"], 1)
                        v.wait_ge(sems["VE"], bump("VE"))
                    W[f"g{g}j{j}_supd"] = cnt["VE"]
                blk.vector(_)

            K = 4 * g + 4
            for h in range(H):
                kv = h // 4
                ht = f"g{g}h{h}"
                pe0 = cnt["PE"]
                ac0 = cnt["AC"]
                ve0 = cnt["VE"]
                def pe_sc_f(kt, pe0=pe0):
                    return pe0 + (1 if kt == 0 else 2 * kt)
                def pe_av_f(kt, K=K, pe0=pe0):
                    return pe0 + (2 * K if kt == K - 1 else 2 * kt + 3)
                def ac_ex_f(kt, ac0=ac0):
                    return ac0 + kt + 1
                def ve_mk_f(kt, g=g, ve0=ve0):
                    return ve0 + (kt - 4 * g) + 1

                def _(t, g=g, h=h, kv=kv, K=K, pe0=pe0, ac0=ac0, ve0=ve0):
                    if h == 0:
                        t.wait_ge(sems["AC"], W[f"g{g}j3_oev{H-1}"])
                        t.wait_ge(sems["VE"], W[f"g{g}j3_pg{H-1}"])
                    else:
                        t.wait_ge(sems["VE"], W[f"g{g}h{h-1}_norm"])

                    def avden(ktp):
                        if ktp >= 4 * g:
                            t.wait_ge(sems["VE"], ve_mk_f(ktp))
                        else:
                            t.wait_ge(sems["AC"], ac_ex_f(ktp))
                        t.matmul(pb[2][:],
                                 vful[:, kv * 4096 + ktp * 128: kv * 4096 + (ktp + 1) * 128],
                                 Pring[:, (ktp % 3) * 512:(ktp % 3) * 512 + 512],
                                 start=(ktp == 0), stop=(ktp == K - 1))
                        ins = t.matmul(pb[3][0:1, :], ONES_B,
                                       Pring[:, (ktp % 3) * 512:(ktp % 3) * 512 + 512],
                                       start=(ktp == 0), stop=(ktp == K - 1))
                        ins.then_inc(sems["PE"], 1)
                        assert bump("PE") == pe_av_f(ktp)

                    for kt in range(K):
                        if kt >= 2:
                            t.wait_ge(sems["AC"], ac_ex_f(kt - 2))
                        ins = t.matmul(pb[kt % 2][:],
                                       skrT[:, kv * SEQ + kt * 128: kv * SEQ + (kt + 1) * 128],
                                       stg[:, CSQR + h * 512: CSQR + (h + 1) * 512], start=True, stop=True)
                        ins.then_inc(sems["PE"], 1)
                        assert bump("PE") == pe_sc_f(kt)
                        if kt >= 1:
                            avden(kt - 1)
                    avden(K - 1)
                blk.tensor(_)

                def _(sc, g=g, K=K):
                    for kt in range(K):
                        sc.wait_ge(sems["PE"], pe_sc_f(kt))
                        if kt >= 3:
                            sc.wait_ge(sems["PE"], pe_av_f(kt - 3))
                        ins = sc.activation(Pring[:, (kt % 3) * 512:(kt % 3) * 512 + 512],
                                            pb[kt % 2][:], AF.Exp, scale=float(HD) ** -0.5)
                        ins.then_inc(sems["AC"], 1)
                        assert bump("AC") == ac_ex_f(kt)
                blk.scalar(_)

                def _(v, g=g, K=K):
                    for kt in range(4 * g, K):
                        v.wait_ge(sems["AC"], ac_ex_f(kt))
                        ins = v.tensor_mul(Pring[:, (kt % 3) * 512:(kt % 3) * 512 + 512],
                                           Pring[:, (kt % 3) * 512:(kt % 3) * 512 + 512],
                                           MASKDIAG(kt - 4 * g))
                        ins.then_inc(sems["VE"], 1)
                        assert bump("VE") == ve_mk_f(kt)
                blk.vector(_)

                def _(v, g=g, h=h, K=K, ht=ht):
                    v.wait_ge(sems["PE"], pe_av_f(K - 1))
                    ins = v.reciprocal(hdr[0:1, :], pb[3][0:1, :])
                    ins.then_inc(sems["VE"], 1)
                    W[ht + "_rec"] = bump("VE")
                blk.vector(_)

                def _(sc, ht=ht):
                    sc.wait_ge(sems["VE"], W[ht + "_rec"])
                    ins = sc.mul(hdrb[0:1, :], hdr[0:1, :], 0.5)
                    ins.then_inc(sems["AC"], 1)
                    W[ht + "_hdb"] = bump("AC")
                blk.scalar(_)

                def _(t, ht=ht):
                    t.wait_ge(sems["AC"], W[ht + "_hdb"])
                    ins = t.matmul(pb[4][:], ONESROW_B, hdrb[0:1, :], start=True, stop=True)
                    ins.then_inc(sems["PE"], 1)
                    W[ht + "_hbc"] = bump("PE")
                blk.tensor(_)

                def _(v, g=g, h=h, ht=ht):
                    v.wait_ge(sems["PE"], W[ht + "_hbc"])
                    if g > 0 and h == 0:
                        v.wait_ge(sems["dX"], g_ship[g - 1])
                    v.tensor_copy(hden[:], pb[4][:])
                    ins = v.tensor_mul(yscr[:], pb[2][:], hden[:])
                    ins.then_inc(sems["VE"], 1)
                    v.wait_ge(sems["VE"], bump("VE"))
                    ins = v.tensor_add(oTb[:, h * 512:(h + 1) * 512], yscr[:],
                                       o05[:, h * 512:(h + 1) * 512])
                    ins.then_inc(sems["VE"], 1)
                    W[ht + "_norm"] = bump("VE")
                blk.vector(_)
            W[f"g{g}_oT"] = cnt["VE"]

            def _(sync, g=g):
                sync.wait_ge(sems["VE"], W[f"g{g}_oT"])
                ins = sync.dma_start(out=o_in[g].rearrange("(h p) r -> p h r", p=128),
                                     in_=oTb[:].rearrange("p (h r) -> p h r", h=H))
                ins.then_inc(sems["dX"], 16)
                g_ship[g] = bump("dX", 16)
            blk.sync(_)


            g_pe_end[g] = cnt["PE"]
            g_ve_end[g] = cnt["VE"]
            g_ac_end[g] = cnt["AC"]

        o_cc_base = cnt["CC"]

        o_out_flat = o_out.rearrange("g row c -> (g row) c")
        ve_p3 = cnt["VE"]
        pe_p3 = cnt["PE"]
        for c_ in range(NB):
            for dmt_ in range(16):
                W[f"wo_c{c_}_ev{dmt_}"] = ve_p3 + c_ * 16 + dmt_ + 1
                W[f"wo_c{c_}_m{dmt_}"] = pe_p3 + c_ * 16 + dmt_ + 1
            W[f"wo_c{c_}_pe_end"] = pe_p3 + (c_ + 1) * 16
        for c in range(NB):
            wt = f"wo_c{c}"

            def _(sync, c=c, wt=wt):
                for dmt in range(16):
                    if dmt >= 2:
                        sync.wait_ge(sems["VE"], W[wt + f"_ev{dmt-2}"])
                    elif c > 0:
                        sync.wait_ge(sems["VE"], W[f"wo_c{c-1}_ev{dmt+14}"])
                    ins = sync.dma_start(
                        out=x1r[:, (dmt % 2) * 512:(dmt % 2) * 512 + 512].rearrange("p (o r) -> p o r", o=1),
                        in_=x1T.rearrange("(dt p) r -> p dt r", p=128)[:, dmt:dmt + 1, c * 512:(c + 1) * 512])
                    ins.then_inc(sems["dX"], 16)
                    W[wt + f"_x1{dmt}"] = bump("dX", 16)
            blk.sync(_)

            def _(t, c=c, wt=wt):
                t.wait_ge(sems["dG"], W[wt + "_ost"])
                if c == 0:
                    t.wait_ge(sems["AC"], g_ac_end[NG - 1])
                    t.wait_ge(sems["VE"], g_ve_end[NG - 1])
                for dmt in range(16):
                    t.wait_ge(sems["dG"], W[wt + f"_wo{dmt}"])
                    if dmt >= 2:
                        t.wait_ge(sems["VE"], W[wt + f"_ev{dmt-2}"])
                    for ot in range(16):
                        ins = t.matmul(pb[dmt % 2][:],
                                       wob[:, (dmt % 2) * 2048 + ot * 128:(dmt % 2) * 2048 + (ot + 1) * 128],
                                       ost[:, ot * 512:(ot + 1) * 512],
                                       start=(ot == 0), stop=(ot == 15))
                    ins.then_inc(sems["PE"], 1)
                    assert bump("PE") == W[wt + f"_m{dmt}"]
                assert cnt["PE"] == W[wt + "_pe_end"]
            blk.tensor(_)

            def _(v, c=c, wt=wt):
                for dmt in range(16):
                    v.wait_ge(sems["PE"], W[wt + f"_m{dmt}"])
                    v.wait_ge(sems["dX"], W[wt + f"_x1{dmt}"])
                    if dmt >= 2:
                        v.wait_ge(sems["OUT"], W[wt + f"_st{dmt-2}"])
                    elif c > 0:
                        v.wait_ge(sems["OUT"], W[f"wo_c{c-1}_st{dmt+14}"])
                    ins = v.tensor_add(x2o[:, (dmt % 2) * 512:(dmt % 2) * 512 + 512],
                                       pb[dmt % 2][:], x1r[:, (dmt % 2) * 512:(dmt % 2) * 512 + 512])
                    ins.then_inc(sems["VE"], 1)
                    assert bump("VE") == W[wt + f"_ev{dmt}"]
            blk.vector(_)


        qkv_flat = qkv_out.rearrange("s r c -> (s r c)")
        o_flat1 = o_out.rearrange("g row c -> (g row c)")

        def _(gp):
            gp.wait_ge(sems["dX"], W["p_loads"])
            REG = [nc.values_load(offs_s[0:1, i:i + 1], engines=[PoolE],
                                  skip_runtime_bounds_check=True) for i in range(12)]
            for g in range(NG):
                slot, step = g // 4, g % 4
                gp.wait_ge(sems["CC"], cc_qkv_base + step + 1)
                gp.wait_ge(sems["PE"], g_pe_end[g - 1])
                gp.wait_ge(sems["VE"], g_ve_end[g - 1])
                gp.wait_ge(sems["AC"], g_ac_end[g - 1])
                gp.dma_start(
                    out=stg[:].rearrange("p (grp r) -> p grp r", r=512),
                    in_=qkv_flat[ds(REG[g], 26 * 128 * 512)]
                    .rearrange("(grp p r) -> p grp r", p=128, r=512)).then_inc(sems["dG"], 16)
                bump("dG", 16)
                assert cnt["dG"] == W[f"g{g}_staged"]
                if g > 0:
                    gp.wait_ge(sems["dX"], g_ship[g - 1])
                    gp.wait_ge(sems["CC"], cnt["CC"])
                    gp.collective_compute("AllGather", ALU.bypass,
                                          replica_groups=[[0, 1], [2, 3], [4, 5], [6, 7]],
                                          ins=[o_in[g - 1]], outs=[o_out[g - 1]]).then_inc(sems["CC"], 1)
                    assert bump("CC") == o_cc_base + g
            gp.wait_ge(sems["dX"], g_ship[NG - 1])
            gp.wait_ge(sems["CC"], cnt["CC"])
            gp.collective_compute("AllGather", ALU.bypass,
                                  replica_groups=[[0, 1], [2, 3], [4, 5], [6, 7]],
                                  ins=[o_in[NG - 1]], outs=[o_out[NG - 1]]).then_inc(sems["CC"], 1)
            assert bump("CC") == o_cc_base + NG

            for c in range(NB):
                wt = f"wo_c{c}"
                gp.wait_ge(sems["CC"], o_cc_base + min(5 + c, NG))
                if c > 0:
                    gp.wait_ge(sems["PE"], W[f"wo_c{c-1}_pe_end"])
                gp.dma_start(out=ost[:].rearrange("p (ot r) -> p ot r", ot=16),
                             in_=o_flat1[ds(REG[8 + c], 2048 * 512)].rearrange("(ot p r) -> p ot r", p=128, r=512)
                             ).then_inc(sems["dG"], 16)
                assert bump("dG", 16) == W[wt + "_ost"]
                for dmt in range(16):
                    if dmt >= 2:
                        gp.wait_ge(sems["PE"], W[wt + f"_m{dmt-2}"])
                        gp.wait_ge(sems["VE"], W[wt + f"_ev{dmt-2}"])
                        gp.dma_start(
                            out=x2T.rearrange("(dt p) r -> p dt r", p=128)[:, dmt - 2:dmt - 1, c * 512:(c + 1) * 512],
                            in_=x2o[:, (dmt % 2) * 512:(dmt % 2) * 512 + 512].rearrange("p (o r) -> p o r", o=1)
                        ).then_inc(sems["OUT"], 16)
                        assert bump("OUT", 16) == W[wt + f"_st{dmt-2}"]
                    gp.dma_start(out=wob[:, (dmt % 2) * 2048:(dmt % 2) * 2048 + 2048].rearrange("p (ot cc) -> p ot cc", ot=16),
                                 in_=wo_in[dmt].rearrange("ot p cc -> p ot cc")).then_inc(sems["dG"], 16)
                    assert bump("dG", 16) == W[wt + f"_wo{dmt}"]
                for dmt in range(14, 16):
                    gp.wait_ge(sems["VE"], W[wt + f"_ev{dmt}"])
                    gp.dma_start(
                        out=x2T.rearrange("(dt p) r -> p dt r", p=128)[:, dmt:dmt + 1, c * 512:(c + 1) * 512],
                        in_=x2o[:, (dmt % 2) * 512:(dmt % 2) * 512 + 512].rearrange("p (o r) -> p o r", o=1)
                    ).then_inc(sems["OUT"], 16)
                    assert bump("OUT", 16) == W[wt + f"_st{dmt}"]
        blk.gpsimd(_)

        rcv.close()
        return cnt["VE"], max(cnt["dX"], 0), cnt["OUT"]

    with ExitStack() as pes:
        phase_prologue(pes)

    def _(t):
        t.wait_ge(sems["CC"], W["p_wAG1"])
    blk.tensor(_)
    def _(sync):
        sync.wait_ge(sems["CC"], W["p_wAG1"])
    blk.sync(_)

    if N_SUBLAYERS == 1:
        with ExitStack() as fes:
            F = alloc_ffn_bufs(fes, "f1")
            phase_ffn("f1", F, xT_in, out_T, "OUT", free0=0)
        final_out = cnt["OUT"]
    else:
        with ExitStack() as fes:
            F = alloc_ffn_bufs(fes, "f1")
            phase_ffn("f1", F, xT_in, x1T, "dS", free0=0)
        f1_pe_end = cnt["PE"]
        f1_stored = [W[f"f1_b{b}_stored"] for b in range(NB)]
        at_ve_end, at_dx_end, at_out_end = phase_attn(W[f"f1_b{NB-1}_evacall"], f1_pe_end, f1_stored)

        def _(sync, at_out_end=at_out_end):
            sync.wait_ge(sems["OUT"], at_out_end)
        blk.sync(_)

        with ExitStack() as fes:
            F3 = alloc_ffn_bufs(fes, "f3")
            phase_ffn("f3", F3, x2T, out_T, "OUT", free0=at_ve_end, dx_gate=at_dx_end)
        final_out = cnt["OUT"]

    def _(sync, final_out=final_out):
        sync.wait_ge(sems["OUT"], final_out)
    blk.sync(_)

    blk.__exit__(None, None, None)
    es.close()
    return nc



def _host_prep(inputs):
    x = np.ascontiguousarray(np.asarray(inputs["L0_attn_hidden_states"], dtype=np.float32))
    Wq = np.asarray(inputs["L0_attn_Wq"], dtype=np.float32)
    Wk = np.asarray(inputs["L0_attn_Wk"], dtype=np.float32)
    Wv = np.asarray(inputs["L0_attn_Wv"], dtype=np.float32)
    Wo = np.asarray(inputs["L0_attn_Wo"], dtype=np.float32)
    w1 = np.asarray(inputs["L0_ffn_w1"], dtype=np.float32)
    b1 = np.asarray(inputs["L0_ffn_b1"], dtype=np.float32)
    w2 = np.asarray(inputs["L0_ffn_w2"], dtype=np.float32)
    b2 = np.asarray(inputs["L0_ffn_b2"], dtype=np.float32)
    import ml_dtypes
    bf = ml_dtypes.bfloat16

    w1t = np.ascontiguousarray(w1.reshape(FT, 128, NT, 128).transpose(0, 2, 3, 1))
    w2t = np.ascontiguousarray(w2.reshape(16, 128, FT, 128).transpose(0, 2, 3, 1))
    wqkv_cat = np.concatenate([Wq, Wk, Wv], axis=1)
    wqkvN = np.ascontiguousarray(wqkv_cat.reshape(NT, 128, 6, 512).transpose(2, 0, 1, 3))
    woT = np.ascontiguousarray(Wo.reshape(16, 128, 16, 128).transpose(2, 0, 1, 3))

    constf = np.zeros((128, 96), np.float32)
    constf[:, 0] = 1.0
    constf[:, 1:17] = b2.reshape(16, 128).T
    constf[:, 17:81] = b1.reshape(FT, 128).T
    constf[0, 82] = EPS

    constb = np.zeros((128, 5120), np.float32)
    constb[:, 0] = 1.0
    constb[:, 2:130] = np.eye(128)
    s_idx = np.arange(128)[:, None]
    q_idx512 = np.arange(512)[None, :]
    t_idx = np.arange(128)[None, :]
    constb[:, 130:258] = (t_idx >= s_idx).astype(np.float32)
    for j in range(4):
        constb[:, 258 + j * 512: 258 + (j + 1) * 512] = (q_idx512 >= (s_idx + 128 * j)).astype(np.float32)
    constb[0, 2306:2434] = 1.0
    constb[0, 2434:2946] = 1.0
    constb[0, 2946:4994] = 0.5 * b2
    constb = constb.astype(bf)

    inv_freq = 1.0 / (10000.0 ** (np.arange(0, HD, 2, dtype=np.float64) / HD))
    pos = np.arange(SEQ, dtype=np.float64)
    fr = pos[:, None] * inv_freq[None, :]
    cosn = np.cos(fr).astype(np.float32).reshape(32, 128, 64).transpose(1, 0, 2).reshape(128, 2048)
    sinn = np.sin(fr).astype(np.float32).reshape(32, 128, 64).transpose(1, 0, 2).reshape(128, 2048)
    trig = np.concatenate([cosn, sinn], axis=1).astype(bf)

    per_core = []
    for c in range(8):
        b_, r = c // 2, c % 2
        xT = np.ascontiguousarray(x[b_, r * ROWS:(r + 1) * ROWS, :].T)
        offs = np.zeros((1, 16), np.int32)
        for g_ in range(8):
            slot_, step_ = g_ // 4, g_ % 4
            offs[0, g_] = (step_ * 13312 + slot_ * 6656 + r * 3328) * 512
        for lc in range(4):
            offs[0, 8 + lc] = ((4 * r + lc) * 2048) * 512
        per_core.append({
            "xT_in": xT,
            "w1s": np.ascontiguousarray(w1t[c * 8:(c + 1) * 8]),
            "w2s": np.ascontiguousarray(w2t[c * 2:(c + 1) * 2]),
            "wqkvN": wqkvN,
            "woT": woT,
            "constf": constf,
            "constb": constb,
            "trigN": trig,
            "offs": offs,
        })
    return per_core


def kernel(**inputs):
    per_core = _host_prep(inputs)
    nc = build_nc()
    res = run_bass_kernel_spmd(nc, per_core, core_ids=list(range(8)))
    out = np.empty((4, SEQ, DM), np.float32)
    for c in range(8):
        b_, r = c // 2, c % 2
        out[b_, r * ROWS:(r + 1) * ROWS, :] = res.results[c]["out_T"].T
    return out


# version 02
# speedup vs baseline: 1.0258x; 1.0258x over previous
import numpy as np
from contextlib import ExitStack

import concourse.bass as bass
import concourse.mybir as mybir
from concourse.bass_utils import run_bass_kernel_spmd

F32 = mybir.dt.float32
BF16 = mybir.dt.bfloat16
I32 = mybir.dt.int32
AF = mybir.ActivationFunctionType
ALU = mybir.AluOpType
AX = mybir.AxisListType
ds = bass.ds

DM = 2048
DFF = 8192
NT = DM // 128
FT = DFF // 128
ROWS = 2048
NB = 4
BLKR = 512
SEQ = 4096
NG = 8
H = 8
KVH = 2
HD = 128
EPS = 1e-5

N_SUBLAYERS = 3


def build_nc():
    nc = bass.Bass("TRN2", target_bir_lowering=False, detect_race_conditions=False)

    xT_in = nc.dram_tensor("xT_in", [DM, ROWS], F32, kind="ExternalInput").ap()
    w1s_in = nc.dram_tensor("w1s", [8, NT, 128, 128], F32, kind="ExternalInput").ap()
    w2s_in = nc.dram_tensor("w2s", [2, FT, 128, 128], F32, kind="ExternalInput").ap()
    wqkv_in = nc.dram_tensor("wqkvN", [6, NT, 128, 512], F32, kind="ExternalInput").ap()
    wo_in = nc.dram_tensor("woT", [16, 16, 128, 128], F32, kind="ExternalInput").ap()
    constf_in = nc.dram_tensor("constf", [128, 96], F32, kind="ExternalInput").ap()
    constb_in = nc.dram_tensor("constb", [128, 5248], BF16, kind="ExternalInput").ap()
    trig_in = nc.dram_tensor("trigN", [128, 4096], BF16, kind="ExternalInput").ap()
    offs_in = nc.dram_tensor("offs", [1, 16], I32, kind="ExternalInput").ap()

    out_T = nc.dram_tensor("out_T", [DM, ROWS], F32, kind="ExternalOutput").ap()

    x1T = nc.dram_tensor("x1T", [DM, ROWS], F32, kind="Internal").ap()
    x2T = nc.dram_tensor("x2T", [DM, ROWS], F32, kind="Internal").ap()
    sc_loc = nc.dram_tensor("sc_loc", [128, 128], F32, kind="Internal").ap()
    sc_all = nc.dram_tensor("sc_all", [1024, 128], F32, kind="Internal").ap()
    w1q_loc = nc.dram_tensor("w1q_loc", [8 * NT * 128, 128], BF16, kind="Internal").ap()
    w1q_all = nc.dram_tensor("w1q_all", [FT * NT * 128, 128], BF16, kind="Internal").ap()
    w2q_loc = nc.dram_tensor("w2q_loc", [2 * FT * 128, 128], BF16, kind="Internal").ap()
    w2q_all = nc.dram_tensor("w2q_all", [16 * FT * 128, 128], BF16, kind="Internal").ap()
    qkv_in = nc.dram_tensor("qkv_in", [4, 6656, 512], BF16, kind="Internal").ap()
    qkv_out = nc.dram_tensor("qkv_out", [4, 13312, 512], BF16, kind="Internal").ap()
    o_in = nc.dram_tensor("o_in", [NG, 1024, 512], BF16, kind="Internal").ap()
    o_out = nc.dram_tensor("o_out", [NG, 2048, 512], BF16, kind="Internal").ap()

    es = ExitStack()

    def SBp(name, shape, dt):
        return es.enter_context(nc.sbuf_tensor(name, shape, dt))

    constf = SBp("constf_s", [128, 96], F32)
    constb = SBp("constb_s", [128, 5248], BF16)
    trig = SBp("trig_s", [128, 4096], BF16)
    offs_s = SBp("offs_s", [1, 16], I32)
    onesrow_f = SBp("onesrow_f", [1, 128], F32)
    thr_s = SBp("thr_s", [1, 4], F32)
    thrv = SBp("thrv", [128, 2], F32)
    nthrv = SBp("nthrv", [128, 2], F32)
    stat = SBp("stat", [1, 2048], F32)
    statb = SBp("statb", [1, 1024], BF16)

    pbA = es.enter_context(nc.psum_tensor("pbA", [128, 1024], F32))
    pb2_t = es.enter_context(nc.psum_tensor("pb2", [128, 512], F32))
    pb3_t = es.enter_context(nc.psum_tensor("pb3", [128, 512], F32))
    pbB = es.enter_context(nc.psum_tensor("pbB", [128, 1024], F32))
    pb6_t = es.enter_context(nc.psum_tensor("pb6", [128, 512], F32))
    pb7_t = es.enter_context(nc.psum_tensor("pb7", [128, 512], F32))
    pb = [pbA[:, 0:512], pbA[:, 512:1024], pb2_t.ap(), pb3_t.ap(),
          pbB[:, 0:512], pbB[:, 512:1024], pb6_t.ap(), pb7_t.ap()]

    sems = {}
    cnt = {}
    for s in ["dW", "dX", "dG", "dS", "CC", "PE", "AC", "VE", "OUT"]:
        sems[s] = es.enter_context(nc.semaphore(s))
        cnt[s] = 0

    def bump(s, by=1):
        cnt[s] += by
        return cnt[s]

    blk = nc.Block()
    blk.__enter__()

    W = {}

    ONES_F = constf[:, 0:1]
    B2COL = lambda dmt: constf[:, 1 + dmt: 2 + dmt]
    B1COL = lambda ft: constf[:, 17 + ft: 18 + ft]
    ONES_B = constb[:, 0:1]
    IDENT = constb[:, 2:130]
    MASKGLA = constb[:, 130:258]
    MASKDIAG = lambda j: constb[:, 258 + j * 512: 258 + (j + 1) * 512]
    ONESROW_B = constb[0:1, 2306:2434]
    HALFROW_B = constb[0:1, 4994:5122]
    ONESROW512_B = constb[0:1, 2434:2946]
    B2ROW = lambda dmt: constb[0:1, 2946 + dmt * 128: 2946 + (dmt + 1) * 128]

    def phase_prologue(pes):
        wslab = pes.enter_context(nc.sbuf_tensor("wslab", [128, 16384], F32))
        wqslab = pes.enter_context(nc.sbuf_tensor("wqslab", [128, 16384], BF16))
        tmpq = pes.enter_context(nc.sbuf_tensor("tmpq", [128, 8192], BF16))
        sc_s = pes.enter_context(nc.sbuf_tensor("sc_s", [128, 128], F32))
        sc8 = pes.enter_context(nc.sbuf_tensor("sc8", [128, 16], F32))
        scacc = pes.enter_context(nc.sbuf_tensor("scacc", [128, 2], F32))

        def _(sync):
            sync.dma_start(out=constf[:], in_=constf_in).then_inc(sems["dX"], 16)
            sync.dma_start(out=constb[:], in_=constb_in).then_inc(sems["dX"], 16)
            sync.dma_start(out=trig[:], in_=trig_in).then_inc(sems["dX"], 16)
            sync.dma_start(out=offs_s[:], in_=offs_in).then_inc(sems["dX"], 16)
            sync.dma_start(out=wslab[:].rearrange("p (f d c) -> p f d c", f=8, d=NT),
                           in_=w1s_in.rearrange("f d p c -> p f d c")).then_inc(sems["dX"], 16)
        blk.sync(_)
        bump("dX", 16 * 5)
        W["p_loads"] = cnt["dX"]

        def _(v):
            v.wait_ge(sems["dX"], W["p_loads"])
            v.memset(onesrow_f[:], 1.0)
            v.memset(sc_s[:], 0.0)
            v.reduce_sum(sc_s[:, 0:1], wslab[:], AX.X, apply_absolute_value=True).then_inc(sems["VE"], 1)
        blk.vector(_)
        W["p_w1red"] = bump("VE")

        def _(sync):
            sync.wait_ge(sems["VE"], W["p_w1red"])
            sync.dma_start(out=wslab[:].rearrange("p (f d c) -> p f d c", f=2, d=FT),
                           in_=w2s_in.rearrange("f d p c -> p f d c")).then_inc(sems["dX"], 16)
        blk.sync(_)
        W["p_w2load"] = bump("dX", 16)

        def _(v):
            v.wait_ge(sems["dX"], W["p_w2load"])
            v.reduce_sum(sc_s[:, 1:2], wslab[:], AX.X, apply_absolute_value=True).then_inc(sems["VE"], 1)
        blk.vector(_)
        W["p_w2red"] = bump("VE")

        def _(sync):
            sync.wait_ge(sems["VE"], W["p_w2red"])
            sync.dma_start(out=sc_loc, in_=sc_s[:]).then_inc(sems["dX"], 16)
        blk.sync(_)
        W["p_scst"] = bump("dX", 16)

        def _(g):
            g.wait_ge(sems["dX"], W["p_scst"])
            g.collective_compute("AllGather", ALU.bypass, replica_groups=[list(range(8))],
                                 ins=[sc_loc], outs=[sc_all]).then_inc(sems["CC"], 1)
        blk.gpsimd(_)
        W["p_scAG"] = bump("CC")

        def _(sync):
            sync.wait_ge(sems["CC"], W["p_scAG"])
            sync.dma_start(out=sc8[:].rearrange("p (k c) -> p k c", k=8), in_=sc_all[:, 0:2].rearrange("(k p) c -> p k c", p=128)).then_inc(sems["dX"], 16)
            sync.dma_start(out=wslab[:].rearrange("p (f d c) -> p f d c", f=8, d=NT),
                           in_=w1s_in.rearrange("f d p c -> p f d c")).then_inc(sems["dX"], 16)
        blk.sync(_)
        W["p_sc8"] = bump("dX", 16)
        W["p_w1reload"] = bump("dX", 16)

        def _(v):
            v.wait_ge(sems["dX"], W["p_sc8"])
            v.reduce_sum(scacc[:], sc8[:].rearrange("p (k c) -> p c k", k=8), AX.X).then_inc(sems["VE"], 1)
        blk.vector(_)
        W["p_scsum"] = bump("VE")

        def _(t):
            t.wait_ge(sems["VE"], W["p_scsum"])
            t.matmul(pb[6][0:1, 0:2], ONES_F, scacc[:], start=True, stop=True).then_inc(sems["PE"], 1)
        blk.tensor(_)
        W["p_tot"] = bump("PE")

        def _(sc):
            sc.wait_ge(sems["PE"], W["p_tot"])
            sc.mul(thr_s[0:1, 0:2], pb[6][0:1, 0:2], 0.7 / (DFF * DM)).then_inc(sems["AC"], 1)
        blk.scalar(_)
        W["p_thr"] = bump("AC")

        def _(t):
            t.wait_ge(sems["AC"], W["p_thr"])
            t.matmul(pb[6][:, 0:1], onesrow_f[:], thr_s[0:1, 0:1], start=True, stop=True)
            t.matmul(pb[6][:, 1:2], onesrow_f[:], thr_s[0:1, 1:2], start=True, stop=True).then_inc(sems["PE"], 1)
        blk.tensor(_)
        W["p_thrb"] = bump("PE")

        def _(sc):
            sc.wait_ge(sems["PE"], W["p_thrb"])
            sc.copy(thrv[:], pb[6][:, 0:2])
            sc.mul(nthrv[:], pb[6][:, 0:2], -1.0).then_inc(sems["AC"], 1)
        blk.scalar(_)
        W["p_thrv"] = bump("AC")

        for wi in range(2):
            wld = W["p_w1reload"] if wi == 0 else W["p_w2reload"]
            wq_loc_t = w1q_loc if wi == 0 else w2q_loc
            wq_all_t = w1q_all if wi == 0 else w2q_all
            dpar = NT if wi == 0 else FT

            def _(v, wld=wld, wi=wi):
                v.wait_ge(sems["dX"], wld)
                v.wait_ge(sems["AC"], W["p_thrv"])
                for hh in range(2):
                    sl = slice(hh * 8192, (hh + 1) * 8192)
                    v.tensor_scalar(tmpq[:], wslab[:, sl], nthrv[:, wi:wi + 1], None, ALU.is_le)
                    ins = v.scalar_tensor_tensor(wqslab[:, sl], wslab[:, sl], thrv[:, wi:wi + 1],
                                                 tmpq[:], ALU.is_ge, ALU.subtract)
                ins.then_inc(sems["VE"], 1)
            blk.vector(_)
            W[f"p_q{wi}"] = bump("VE")

            def _(sync, wi=wi, wq_loc_t=wq_loc_t, dpar=dpar):
                sync.wait_ge(sems["VE"], W[f"p_q{wi}"])
                sync.dma_start(out=wq_loc_t.rearrange("(f d p) c -> p f d c", p=128, d=dpar),
                               in_=wqslab[:].rearrange("p (f d c) -> p f d c", d=dpar, c=128)).then_inc(sems["dX"], 16)
                if wi == 0:
                    sync.dma_start(out=wslab[:].rearrange("p (f d c) -> p f d c", f=2, d=FT),
                           in_=w2s_in.rearrange("f d p c -> p f d c")).then_inc(sems["dX"], 16)
            blk.sync(_)
            W[f"p_qst{wi}"] = bump("dX", 16)
            if wi == 0:
                W["p_w2reload"] = bump("dX", 16)

            def _(g, wi=wi, wq_loc_t=wq_loc_t, wq_all_t=wq_all_t):
                g.wait_ge(sems["dX"], W[f"p_qst{wi}"])
                g.wait_ge(sems["CC"], cnt["CC"])
                g.collective_compute("AllGather", ALU.bypass, replica_groups=[list(range(8))],
                                     ins=[wq_loc_t], outs=[wq_all_t]).then_inc(sems["CC"], 1)
            blk.gpsimd(_)
            W[f"p_wAG{wi}"] = bump("CC")

    def emit_ln(tag, F, free_after, pre_bf16=False, xc_sem="dX"):
        xc, xcb, xsq, xn, scr = F["xc"], F["xcb"], F["xsq"], F["xn"], F["scr"]

        def _(v):
            v.wait_ge(sems[xc_sem], W[tag + "_xc"])
            v.wait_ge(sems["VE"], free_after)
            if pre_bf16:
                v.engine_nop().then_inc(sems["VE"], 1)
            else:
                v.tensor_copy(xcb[:, 0:NT * BLKR], xc[:]).then_inc(sems["VE"], 1)
        blk.vector(_)
        ve_cast = bump("VE")

        def _(sc):
            sc.wait_ge(sems["VE"], ve_cast)
            sc.activation(xsq[:], xc[:], AF.Square).then_inc(sems["AC"], 1)
        blk.scalar(_)
        ac_sq = bump("AC")

        def _(t):
            t.wait_ge(sems["VE"], ve_cast)
            t.wait_ge(sems["AC"], ac_sq)
            for dt in range(NT):
                t.matmul(pb[4][0:1, :], ONES_B, xcb[:, dt * 512:(dt + 1) * 512],
                         start=(dt == 0), stop=(dt == NT - 1))
            for dt in range(NT):
                ins = t.matmul(pb[5][0:1, :], ONES_B, xsq[:, dt * 512:(dt + 1) * 512],
                               start=(dt == 0), stop=(dt == NT - 1))
            ins.then_inc(sems["PE"], 1)
        blk.tensor(_)
        pe_sums = bump("PE")

        def _(sc):
            sc.wait_ge(sems["PE"], pe_sums)
            sc.mul(stat[0:1, 0:512], pb[4][0:1, :], -1.0 / DM)
            sc.mul(stat[0:1, 512:1024], pb[5][0:1, :], 1.0 / DM).then_inc(sems["AC"], 1)
        blk.scalar(_)
        ac_stat = bump("AC")

        def _(v):
            v.wait_ge(sems["AC"], ac_stat)
            v.tensor_mul(stat[0:1, 1024:1536], stat[0:1, 0:512], stat[0:1, 0:512])
            v.tensor_sub(stat[0:1, 1536:2048], stat[0:1, 512:1024], stat[0:1, 1024:1536]).then_inc(sems["VE"], 1)
        blk.vector(_)
        ve_var = bump("VE")

        def _(sc):
            sc.wait_ge(sems["VE"], ve_var)
            sc.activation(stat[0:1, 1024:1536], stat[0:1, 1536:2048], AF.Ln, bias=constf[0:1, 82:83])
            sc.activation(stat[0:1, 1536:2048], stat[0:1, 1024:1536], AF.Exp, scale=-0.5)
            sc.copy(statb[0:1, 0:512], stat[0:1, 0:512])
            sc.copy(statb[0:1, 512:1024], stat[0:1, 1536:2048]).then_inc(sems["AC"], 1)
        blk.scalar(_)
        ac_rstd = bump("AC")

        def _(t):
            t.wait_ge(sems["AC"], ac_rstd)
            t.matmul(pb[4][:], ONESROW_B, statb[0:1, 0:512], start=True, stop=True)
            t.matmul(pb[5][:], ONESROW_B, statb[0:1, 512:1024], start=True, stop=True).then_inc(sems["PE"], 1)
        blk.tensor(_)
        pe_bc = bump("PE")

        def _(v):
            v.wait_ge(sems["PE"], pe_bc)
            for dt in range(NT):
                v.tensor_add(scr[:], xc[:, dt * 512:(dt + 1) * 512], pb[4][:])
                ins = v.tensor_mul(xn[:, dt * 512:(dt + 1) * 512], scr[:], pb[5][:])
            ins.then_inc(sems["VE"], 1)
        blk.vector(_)
        W[tag + "_apply"] = bump("VE")

    def phase_ffn(name, F, xin, xout, out_sem, free0, dx_gate=0):
        for b in range(NB):
            tag = f"{name}_b{b}"
            prev_free = W.get(f"{name}_b{b-1}_evacall", free0)

            def _(sync, b=b, prev_free=prev_free, dx_gate=dx_gate if b == 0 else 0, xin=xin, F=F):
                sync.wait_ge(sems["VE"], prev_free)
                if dx_gate:
                    sync.wait_ge(sems["dX"], dx_gate)
                sync.dma_start(
                    out=F["xc"][:].rearrange("p (dt r) -> p dt r", dt=NT),
                    in_=xin.rearrange("(dt p) r -> p dt r", p=128)[:, :, ds(b * BLKR, BLKR)],
                ).then_inc(sems["dX"], 16)
            blk.sync(_)
            W[tag + "_xc"] = bump("dX", 16)

            emit_ln(tag, F, prev_free)

            gelu_base = cnt["AC"]
            dw_w1_base = cnt["dW"]

            def _(sync, gelu_base=gelu_base, F=F):
                for ft in range(FT):
                    if ft >= 2:
                        sync.wait_ge(sems["AC"], gelu_base + ft - 1)
                    sync.dma_start(
                        out=F["w1b"][:, (ft % 2) * DM:(ft % 2) * DM + DM].rearrange("p (d c) -> p d c", c=128),
                        in_=w1q_all[ft * DM:(ft + 1) * DM, :].rearrange("(d p) c -> p d c", p=128),
                    ).then_inc(sems["dW"], 16)
            blk.sync(_)
            bump("dW", 16 * FT)

            pe_h_base = cnt["PE"]

            def _(t, tag=tag, gelu_base=gelu_base, dw_w1_base=dw_w1_base, F=F):
                t.wait_ge(sems["VE"], W[tag + "_apply"])
                for ft in range(FT):
                    t.wait_ge(sems["dW"], dw_w1_base + (ft + 1) * 16)
                    if ft >= 2:
                        t.wait_ge(sems["AC"], gelu_base + ft - 1)
                    for dt in range(NT):
                        ins = t.matmul(pb[ft % 2][:],
                                       F["w1b"][:, (ft % 2) * DM + dt * 128:(ft % 2) * DM + (dt + 1) * 128],
                                       F["xn"][:, dt * 512:(dt + 1) * 512],
                                       start=(dt == 0), stop=(dt == NT - 1))
                    ins.then_inc(sems["PE"], 1)
            blk.tensor(_)
            bump("PE", FT)

            def _(sc, pe_h_base=pe_h_base, F=F):
                for ft in range(FT):
                    sc.wait_ge(sems["PE"], pe_h_base + ft + 1)
                    sc.activation(F["hT"][:, ft * 512:(ft + 1) * 512], pb[ft % 2][:],
                                  AF.Gelu, bias=B1COL(ft)).then_inc(sems["AC"], 1)
            blk.scalar(_)
            bump("AC", FT)
            ac_gelu_all = cnt["AC"]
            assert ac_gelu_all == gelu_base + FT

            evac_base = cnt["VE"]
            dw_w2_base = cnt["dW"]
            prev_evacall = W.get(f"{name}_b{b-1}_evacall", 0)

            def _(sync, evac_base=evac_base, F=F):
                for dmt in range(16):
                    if dmt >= 2:
                        sync.wait_ge(sems["VE"], evac_base + dmt - 1)
                    sync.dma_start(
                        out=F["w2b"][:, (dmt % 2) * DFF:(dmt % 2) * DFF + DFF].rearrange("p (f c) -> p f c", c=128),
                        in_=w2q_all[dmt * DFF:(dmt + 1) * DFF, :].rearrange("(f p) c -> p f c", p=128),
                    ).then_inc(sems["dW"], 16)
            blk.sync(_)
            bump("dW", 16 * 16)

            pe_o_base = cnt["PE"]

            def _(t, ac_gelu_all=ac_gelu_all, evac_base=evac_base,
                  dw_w2_base=dw_w2_base, prev_evacall=prev_evacall, F=F):
                t.wait_ge(sems["AC"], ac_gelu_all)
                if prev_evacall:
                    t.wait_ge(sems["VE"], prev_evacall)
                for dmt in range(16):
                    t.wait_ge(sems["dW"], dw_w2_base + (dmt + 1) * 16)
                    if dmt >= 2:
                        t.wait_ge(sems["VE"], evac_base + dmt - 1)
                    for ft in range(FT):
                        t.matmul(pb[2 + dmt % 2][:],
                                 F["w2b"][:, (dmt % 2) * DFF + ft * 128:(dmt % 2) * DFF + (ft + 1) * 128],
                                 F["hT"][:, ft * 512:(ft + 1) * 512],
                                 start=(ft == 0), stop=False)
                    ins = t.matmul(pb[2 + dmt % 2][:], B2ROW(dmt), ONESROW512_B,
                                   start=False, stop=True)
                    ins.then_inc(sems["PE"], 1)
            blk.tensor(_)
            bump("PE", 16)

            xst_base = cnt[out_sem]

            def _(v, pe_o_base=pe_o_base, xst_base=xst_base, out_sem=out_sem, F=F):
                for dmt in range(16):
                    v.wait_ge(sems["PE"], pe_o_base + dmt + 1)
                    if dmt >= 2:
                        v.wait_ge(sems[out_sem], xst_base + (dmt - 1) * 16)
                    v.scalar_tensor_tensor(F["xo"][:, (dmt % 2) * 512:(dmt % 2) * 512 + 512],
                                           pb[2 + dmt % 2][:], 0.5,
                                           F["xc"][:, dmt * 512:(dmt + 1) * 512],
                                           ALU.mult, ALU.add).then_inc(sems["VE"], 1)
            blk.vector(_)
            bump("VE", 16)
            W[tag + "_evacall"] = cnt["VE"]

            def _(g, b=b, evac_base=evac_base, xout=xout, out_sem=out_sem, F=F):
                for dmt in range(16):
                    g.wait_ge(sems["VE"], evac_base + dmt + 1)
                    g.dma_start(
                        out=xout.rearrange("(dt p) r -> p dt r", p=128)[:, dmt:dmt + 1, ds(b * BLKR, BLKR)],
                        in_=F["xo"][:, (dmt % 2) * 512:(dmt % 2) * 512 + 512].rearrange("p (o r) -> p o r", o=1),
                    ).then_inc(sems[out_sem], 16)
            blk.gpsimd(_)
            bump(out_sem, 16 * 16)
            W[tag + "_stored"] = cnt[out_sem]
        return W[f"{name}_b{NB-1}_evacall"]

    def alloc_ffn_bufs(stack, pfx):
        F = {}
        F["xc"] = stack.enter_context(nc.sbuf_tensor(pfx + "xc", [128, NT * BLKR], F32))
        F["xn"] = stack.enter_context(nc.sbuf_tensor(pfx + "xn", [128, NT * BLKR], BF16))
        F["hT"] = stack.enter_context(nc.sbuf_tensor(pfx + "hT", [128, FT * BLKR], BF16))
        F["w1b"] = stack.enter_context(nc.sbuf_tensor(pfx + "w1b", [128, 2 * DM], BF16))
        F["w2b"] = stack.enter_context(nc.sbuf_tensor(pfx + "w2b", [128, 2 * DFF], BF16))
        F["scr"] = stack.enter_context(nc.sbuf_tensor(pfx + "scr", [128, 512], F32))
        F["scr2"] = stack.enter_context(nc.sbuf_tensor(pfx + "scr2", [128, 2 * 512], F32))
        F["xo"] = stack.enter_context(nc.sbuf_tensor(pfx + "xo", [128, 2 * 512], F32))
        F["xsq"] = F["xn"]
        F["xcb"] = F["hT"]
        return F



    BLK_SHIP = 3328
    BSQR, BQS, BSKR, BKS, BGL, BV = 0, 1024, 2048, 2304, 2560, 2816
    CSQR, CQS, CSKR, CKS, CGL, CV = 0, 4096, 8192, 9216, 10240, 11264
    PoolE = mybir.EngineType.Pool

    def phase_attn(free0_ve, pe_free, dx_x1_stored):
        cc_qkv_base = cnt["CC"]

        snd = ExitStack()
        SA = lambda n, s, d: snd.enter_context(nc.sbuf_tensor(n, s, d))
        axc = SA("axc", [128, NT * BLKR], BF16)
        axn = SA("axn", [128, NT * BLKR], BF16)
        ascr = SA("ascr", [128, 512], F32)
        wnb = SA("wnb", [128, 2 * NT * 512], BF16)
        qnat = SA("qnat", [128, 4 * 2048], BF16)
        rnat = SA("rnat", [128, 4 * 2048], BF16)
        qsn = SA("qsn", [128, 4 * 2048], BF16)
        eqb = SA("eqb", [128, 4 * 2048], BF16)
        rsc = SA("rsc", [128, 4096], BF16)
        srec = SA("srec", [128, 128], F32)
        knat = SA("knat", [128, 4 * 512], BF16)
        krn = SA("krn", [128, 4 * 512], BF16)
        ksn = SA("ksn", [128, 4 * 512], BF16)
        ekb = SA("ekb", [128, 4 * 512], BF16)
        ktrb = SA("ktrb", [128, 4 * 512], BF16)
        glb = SA("glb", [128, 4 * 512], BF16)
        dsc = SA("dsc", [128, 2048], F32)
        shs = SA("shs", [128, 4 * 512], BF16)
        AF_ = {"xc": axc, "xcb": axc, "xsq": axn, "xn": axn, "scr": ascr}
        qrecv = SA("qrecv", [128, 80], F32)

        for c in range(NB):
            tag = f"at_c{c}"
            prev_apply = W.get(f"at_c{c-1}_apply", free0_ve)
            prev_done = W.get(f"at_c{c-1}_shipdone", 0)

            def _(g, c=c, prev_apply=prev_apply, prev_done=prev_done):
                g.wait_ge(sems["dX"], dx_x1_stored[c])
                g.wait_ge(sems["VE"], prev_apply)
                if c == 0:
                    g.wait_ge(sems["PE"], pe_free)
                if prev_done:
                    g.wait_ge(sems["dX"], prev_done)
                ins = g.dma_start(
                    out=axc[:].rearrange("p (dt r) -> p dt r", dt=NT),
                    in_=x1T.rearrange("(dt p) r -> p dt r", p=128)[:, :, ds(c * BLKR, BLKR)],
                )
                ins.then_inc(sems["dG"], 16)
                W[f"at_c{c}_xc"] = bump("dG", 16)
            blk.gpsimd(_)

            emit_ln(tag, AF_, prev_apply, pre_bf16=True, xc_sem="dG")

            pe_proj_base = cnt["PE"]
            ac_ev_base = cnt["AC"]
            dg_wn = []

            def _(g, pe_proj_base=pe_proj_base):
                for og in range(6):
                    if og >= 2:
                        g.wait_ge(sems["PE"], pe_proj_base + (og - 1) * 4)
                    ins = g.dma_start(
                        out=wnb[:, (og % 2) * 8192:(og % 2) * 8192 + 8192].rearrange("p (dt c) -> p dt c", dt=NT),
                        in_=wqkv_in[og].rearrange("dt p c -> p dt c"),
                    )
                    ins.then_inc(sems["dG"], 16)
                    dg_wn.append(bump("dG", 16))
            blk.gpsimd(_)

            def _(t, tag=tag, ac_ev_base=ac_ev_base):
                t.wait_ge(sems["VE"], W[tag + "_apply"])
                for og in range(6):
                    t.wait_ge(sems["dG"], dg_wn[og])
                    for rs in range(4):
                        i = og * 4 + rs
                        if i >= 2:
                            t.wait_ge(sems["AC"], ac_ev_base + i - 1)
                        for dt in range(NT):
                            ins = t.matmul(pb[i % 2][:],
                                           axn[:, dt * 512 + rs * 128: dt * 512 + (rs + 1) * 128],
                                           wnb[:, (og % 2) * 8192 + dt * 512:(og % 2) * 8192 + (dt + 1) * 512],
                                           start=(dt == 0), stop=(dt == NT - 1))
                        ins.then_inc(sems["PE"], 1)
                        bump("PE")
            blk.tensor(_)

            dx_vship_base = cnt["dX"]
            ac_evs = []

            def _(sc, pe_proj_base=pe_proj_base, dx_vship_base=dx_vship_base):
                for og in range(6):
                    for rs in range(4):
                        i = og * 4 + rs
                        sc.wait_ge(sems["PE"], pe_proj_base + i + 1)
                        if og < 4:
                            ins = sc.copy(qnat[:, rs * 2048 + og * 512: rs * 2048 + (og + 1) * 512], pb[i % 2][:])
                        elif og == 4:
                            ins = sc.copy(knat[:, rs * 512:(rs + 1) * 512], pb[i % 2][:])
                        else:
                            if rs >= 2:
                                sc.wait_ge(sems["dX"], dx_vship_base + (rs - 1) * 16)
                            ins = sc.copy(shs[:, (rs % 2) * 512:(rs % 2) * 512 + 512], pb[i % 2][:])
                        ins.then_inc(sems["AC"], 1)
                        ac_evs.append(bump("AC"))
            blk.scalar(_)

            def _(sync, c=c):
                for rs in range(4):
                    sync.wait_ge(sems["AC"], ac_evs[20 + rs])
                    for b_ in range(2):
                        row = b_ * BLK_SHIP + BV + rs * 128
                        ins = sync.dma_start(
                            out=qkv_in[c][row: row + 128, 0:256],
                            in_=shs[:, (rs % 2) * 512 + b_ * 256:(rs % 2) * 512 + b_ * 256 + 256])
                        ins.then_inc(sems["dX"], 16)
                        bump("dX", 16)
                        ins = sync.dma_start(
                            out=qkv_in[c][row: row + 128, 256:512],
                            in_=shs[:, (rs % 2) * 512 + (1 - b_) * 256:(rs % 2) * 512 + (1 - b_) * 256 + 256])
                        ins.then_inc(sems["dX"], 16)
                        bump("dX", 16)
            blk.sync(_)

            def _(sc):
                sc.wait_ge(sems["AC"], ac_evs[15])
                ins = sc.activation(eqb[:], qnat[:], AF.Exp)
                ins.then_inc(sems["AC"], 1)
                W[tag + "_eq"] = bump("AC")
                sc.wait_ge(sems["AC"], ac_evs[19])
                ins = sc.activation(ekb[:], knat[:], AF.Exp)
                ins.then_inc(sems["AC"], 1)
                W[tag + "_ek"] = bump("AC")
            blk.scalar(_)

            def _(v, c=c, tag=tag):
                v.wait_ge(sems["AC"], ac_evs[19])
                for rs in range(4):
                    tile = c * 4 + rs
                    cosNq = trig[:, tile * 64: tile * 64 + 64].rearrange("p (o d) -> p o d", o=1).broadcast_to([128, 16, 64])
                    sinNq = trig[:, 2048 + tile * 64: 2048 + tile * 64 + 64].rearrange("p (o d) -> p o d", o=1).broadcast_to([128, 16, 64])
                    qv = qnat[:, rs * 2048:(rs + 1) * 2048].rearrange("p (h d) -> p h d", d=128)
                    rv = rnat[:, rs * 2048:(rs + 1) * 2048].rearrange("p (h d) -> p h d", d=128)
                    s0 = rsc[:, 0:1024].rearrange("p (h d) -> p h d", d=64)
                    s1 = rsc[:, 1024:2048].rearrange("p (h d) -> p h d", d=64)
                    s2 = rsc[:, 2048:3072].rearrange("p (h d) -> p h d", d=64)
                    s3 = rsc[:, 3072:4096].rearrange("p (h d) -> p h d", d=64)
                    v.tensor_mul(s0, qv[:, :, 0:64], cosNq)
                    v.tensor_mul(s1, qv[:, :, 64:128], sinNq)
                    v.tensor_mul(s2, qv[:, :, 64:128], cosNq)
                    ins = v.tensor_mul(s3, qv[:, :, 0:64], sinNq)
                    ins.then_inc(sems["VE"], 1)
                    v.wait_ge(sems["VE"], bump("VE"))
                    v.tensor_sub(rv[:, :, 0:64], s0, s1)
                    v.tensor_add(rv[:, :, 64:128], s2, s3)
                    cosNk = trig[:, tile * 64: tile * 64 + 64].rearrange("p (o d) -> p o d", o=1).broadcast_to([128, 4, 64])
                    sinNk = trig[:, 2048 + tile * 64: 2048 + tile * 64 + 64].rearrange("p (o d) -> p o d", o=1).broadcast_to([128, 4, 64])
                    kv_ = knat[:, rs * 512:(rs + 1) * 512].rearrange("p (h d) -> p h d", d=128)
                    kr = krn[:, rs * 512:(rs + 1) * 512].rearrange("p (h d) -> p h d", d=128)
                    k0 = rsc[:, 0:256].rearrange("p (h d) -> p h d", d=64)
                    k1 = rsc[:, 256:512].rearrange("p (h d) -> p h d", d=64)
                    k2 = rsc[:, 512:768].rearrange("p (h d) -> p h d", d=64)
                    k3 = rsc[:, 768:1024].rearrange("p (h d) -> p h d", d=64)
                    v.tensor_mul(k0, kv_[:, :, 0:64], cosNk)
                    v.tensor_mul(k1, kv_[:, :, 64:128], sinNk)
                    v.tensor_mul(k2, kv_[:, :, 64:128], cosNk)
                    ins = v.tensor_mul(k3, kv_[:, :, 0:64], sinNk)
                    ins.then_inc(sems["VE"], 1)
                    v.wait_ge(sems["VE"], bump("VE"))
                    v.tensor_sub(kr[:, :, 0:64], k0, k1)
                    ins = v.tensor_add(kr[:, :, 64:128], k2, k3)
                    ins.then_inc(sems["VE"], 1)
                    bump("VE")
                v.wait_ge(sems["AC"], W[tag + "_ek"])
                for rs in range(4):
                    v.reduce_sum(srec[:, rs * 16:(rs + 1) * 16],
                                 eqb[:, rs * 2048:(rs + 1) * 2048].rearrange("p (h d) -> p h d", d=128), AX.X)
                for rs in range(4):
                    ins = v.reduce_sum(srec[:, 64 + rs * 4:64 + (rs + 1) * 4],
                                       ekb[:, rs * 512:(rs + 1) * 512].rearrange("p (h d) -> p h d", d=128), AX.X)
                ins.then_inc(sems["VE"], 1)
                v.wait_ge(sems["VE"], bump("VE"))
                ins = v.reciprocal(qrecv[:], srec[:, 0:80])
                ins.then_inc(sems["VE"], 1)
                v.wait_ge(sems["VE"], bump("VE"))
                for rs in range(4):
                    v.tensor_mul(qsn[:, rs * 2048:(rs + 1) * 2048].rearrange("p (h d) -> p h d", d=128),
                                 eqb[:, rs * 2048:(rs + 1) * 2048].rearrange("p (h d) -> p h d", d=128),
                                 qrecv[:, rs * 16:(rs + 1) * 16].rearrange("p (h o) -> p h o", o=1).broadcast_to([128, 16, 128]))
                for rs in range(4):
                    ins = v.tensor_mul(ksn[:, rs * 512:(rs + 1) * 512].rearrange("p (h d) -> p h d", d=128),
                                       ekb[:, rs * 512:(rs + 1) * 512].rearrange("p (h d) -> p h d", d=128),
                                       qrecv[:, 64 + rs * 4:64 + (rs + 1) * 4].rearrange("p (h o) -> p h o", o=1).broadcast_to([128, 4, 128]))
                ins.then_inc(sems["VE"], 1)
                W[tag + "_derive"] = bump("VE")
            blk.vector(_)

            tlist = [(rnat, BSQR, 16, 2048), (qsn, BQS, 16, 2048),
                     (krn, BSKR, 4, 512), (ksn, BKS, 4, 512), (knat, -1, 4, 512)]
            pe_tr_base = cnt["PE"]
            ac_tr_base = cnt["AC"]
            dx_sh_base = cnt["dX"]

            def _(t, tag=tag, ac_tr_base=ac_tr_base):
                t.wait_ge(sems["VE"], W[tag + "_derive"])
                gi = 0
                for (buf, shiprow, nodt, w) in tlist:
                    for odt in range(nodt):
                        if gi >= 2:
                            t.wait_ge(sems["AC"], ac_tr_base + gi - 1)
                        for rs in range(4):
                            ins = t.matmul(pb[2 + gi % 2][:, rs * 128:(rs + 1) * 128],
                                           buf[:, rs * w + odt * 128: rs * w + (odt + 1) * 128],
                                           IDENT, start=(rs == 0), stop=(rs == 3))
                        ins.then_inc(sems["PE"], 1)
                        bump("PE")
                        gi += 1
            blk.tensor(_)

            def _(sc, tag=tag, pe_tr_base=pe_tr_base, dx_sh_base=dx_sh_base):
                gi = 0
                si = 0
                for (buf, shiprow, nodt, w) in tlist:
                    for odt in range(nodt):
                        sc.wait_ge(sems["PE"], pe_tr_base + gi + 1)
                        if shiprow >= 0:
                            if si < 4:
                                sc.wait_ge(sems["dX"], dx_sh_base)
                            else:
                                sc.wait_ge(sems["dX"], dx_sh_base + (si - 3) * 16)
                            ins = sc.copy(shs[:, (si % 4) * 512:(si % 4) * 512 + 512], pb[2 + gi % 2][:])
                            si += 1
                        else:
                            ins = sc.copy(ktrb[:, odt * 512:(odt + 1) * 512], pb[2 + gi % 2][:])
                        ins.then_inc(sems["AC"], 1)
                        bump("AC")
                        gi += 1
            blk.scalar(_)

            def _(sync, c=c, ac_tr_base=ac_tr_base):
                gi = 0
                si = 0
                for (buf, shiprow, nodt, w) in tlist:
                    for odt in range(nodt):
                        if shiprow >= 0:
                            sync.wait_ge(sems["AC"], ac_tr_base + gi + 1)
                            b_ = odt // (nodt // 2)
                            lo = odt % (nodt // 2)
                            row = b_ * BLK_SHIP + shiprow + lo * 128
                            ins = sync.dma_start(
                                out=qkv_in[c][row: row + 128, :],
                                in_=shs[:, (si % 4) * 512:(si % 4) * 512 + 512])
                            ins.then_inc(sems["dX"], 16)
                            bump("dX", 16)
                            si += 1
                        gi += 1
            blk.sync(_)
            ac_ktr_done = cnt["AC"]

            ve_gl0 = cnt["VE"] + 1

            def _(sc, ac_ktr_done=ac_ktr_done, ve_gl0=ve_gl0):
                sc.wait_ge(sems["AC"], ac_ktr_done)
                for odt in range(4):
                    if odt >= 1:
                        sc.wait_ge(sems["VE"], ve_gl0 + odt - 1)
                    sc.activation(dsc[:, 0:512], ktrb[:, odt * 512:(odt + 1) * 512], AF.Abs)
                    sc.activation(dsc[:, 512:1024], dsc[:, 0:512], AF.Exp, scale=-1.0)
                    ins = sc.activation(dsc[:, 0:512], dsc[:, 512:1024], AF.Ln, bias=1.0)
                    ins.then_inc(sems["AC"], 1)
                    bump("AC")
            blk.scalar(_)
            ac_gl_base = ac_ktr_done

            def _(v, tag=tag, ac_gl_base=ac_gl_base):
                for odt in range(4):
                    v.wait_ge(sems["AC"], ac_gl_base + odt + 1)
                    v.tensor_scalar_min(dsc[:, 1024:1536], ktrb[:, odt * 512:(odt + 1) * 512], 0.0)
                    ins = v.tensor_sub(glb[:, odt * 512:(odt + 1) * 512], dsc[:, 1024:1536], dsc[:, 0:512])
                    ins.then_inc(sems["VE"], 1)
                    bump("VE")
            blk.vector(_)
            ve_gl_base = cnt["VE"] - 4

            def _(sync, c=c, ve_gl_base=ve_gl_base):
                for odt in range(4):
                    sync.wait_ge(sems["VE"], ve_gl_base + odt + 1)
                    b_ = odt // 2
                    row = b_ * BLK_SHIP + BGL + (odt % 2) * 128
                    ins = sync.dma_start(out=qkv_in[c][row: row + 128, :],
                                         in_=glb[:, odt * 512:(odt + 1) * 512])
                    ins.then_inc(sems["dX"], 16)
                    bump("dX", 16)
            blk.sync(_)
            W[tag + "_shipdone"] = cnt["dX"]

            def _(g, c=c, tag=tag):
                g.wait_ge(sems["dX"], W[tag + "_shipdone"])
                g.wait_ge(sems["CC"], cnt["CC"])
                ins = g.collective_compute("AllGather", ALU.bypass,
                                           replica_groups=[[0, 1], [2, 3], [4, 5], [6, 7]],
                                           ins=[qkv_in[c]], outs=[qkv_out[c]])
                ins.then_inc(sems["CC"], 1)
                bump("CC")
            blk.gpsimd(_)

        snd_pe_end = cnt["PE"]
        snd_ve_end = cnt["VE"]
        snd_ac_end = cnt["AC"]
        snd.close()

        rcv = ExitStack()
        RA = lambda n, s, d: rcv.enter_context(nc.sbuf_tensor(n, s, d))
        stg = RA("stg", [128, 26 * 512], BF16)
        skrT = RA("skrT", [128, KVH * SEQ], BF16)
        vful = RA("vful", [128, KVH * 32 * 128], BF16)
        Acum = RA("Acum", [128, KVH * 512], F32)
        eAb = RA("eAb", [128, KVH * 512], F32)
        eAn = RA("eAn", [128, KVH * 512], F32)
        kt_ = RA("kt_", [128, KVH * 512], BF16)
        ktn = RA("ktn", [128, KVH * 128], BF16)
        zf = RA("zf", [128, 128], BF16)
        Sst = RA("Sst", [128, KVH * 128], F32)
        Ssc = RA("Ssc", [128, KVH * 128], F32)
        Sb = RA("Sb", [128, KVH * 128], BF16)
        pgb = RA("pgb", [128, 128], BF16)
        o05 = RA("o05", [128, H * 512], BF16)
        oTb = RA("oTb", [128, H * 512], BF16)
        Pring = RA("Pring", [128, 2 * 1024], BF16)
        hdall = RA("hdall", [1, H * 512], BF16)
        hdb = RA("hdb", [1, H * 512], BF16)
        yraw = RA("yraw", [128, H * 512], BF16)
        qt_ = yraw
        yscr = RA("yscr", [128, 512], F32)
        ost = RA("ost", [128, 16 * 512], BF16)
        wob = RA("wob", [128, 2 * 2048], BF16)
        x1r = RA("x1r", [128, 2 * 512], F32)
        x2o = RA("x2o", [128, 2 * 512], F32)

        def _(v):
            v.wait_ge(sems["VE"], snd_ve_end)
            v.wait_ge(sems["PE"], snd_pe_end)
            v.wait_ge(sems["AC"], snd_ac_end)
            v.memset(zf[:], 0.0)
            ins = v.memset(Sst[:], 0.0)
            ins.then_inc(sems["VE"], 1)
            W["rcv_init"] = bump("VE")
        blk.vector(_)

        dg0 = cnt["dG"]
        for g_ in range(NG):
            W[f"g{g_}_staged"] = dg0 + (g_ + 1) * 16
        dg_p3 = dg0 + NG * 16
        for c_ in range(NB):
            W[f"wo_c{c_}_ost"] = dg_p3 + (c_ * 17 + 1) * 16
            for dmt_ in range(16):
                W[f"wo_c{c_}_wo{dmt_}"] = dg_p3 + (c_ * 17 + 1 + dmt_ + 1) * 16
        out20 = cnt["OUT"]
        for c_ in range(NB):
            for dmt_ in range(16):
                W[f"wo_c{c_}_st{dmt_}"] = out20 + (c_ * 16 + dmt_ + 1) * 16

        g_pe_end = {-1: snd_pe_end}
        g_ve_end = {-1: W["rcv_init"]}
        g_ac_end = {-1: snd_ac_end}
        g_ship = {-1: 0}

        for g in range(NG):
            slot, step = g // 4, g % 4
            src2d = qkv_out[step]


            def _(v, g=g):
                v.wait_ge(sems["dG"], W[f"g{g}_staged"])
                v.tensor_copy(skrT[:].rearrange("p (kv r) -> p kv r", kv=KVH)[:, :, g * 512:(g + 1) * 512],
                              stg[:, CSKR: CSKR + 1024].rearrange("p (kv r) -> p kv r", kv=KVH))
                for kv in range(KVH):
                    v.tensor_copy(
                        vful[:, kv * 4096 + g * 512: kv * 4096 + (g + 1) * 512].rearrange("p (rs d) -> p rs d", d=128),
                        stg[:, CV:].rearrange("p (rs c) -> p rs c", c=512)[:, :, kv * 128:(kv + 1) * 128])
                for kv in range(KVH):
                    for j in range(4):
                        ins = v.tensor_tensor_scan(Acum[:, kv * 512 + j * 128: kv * 512 + (j + 1) * 128],
                                                   stg[:, CGL + kv * 512 + j * 128: CGL + kv * 512 + (j + 1) * 128],
                                                   zf[:], 0.0, ALU.add, ALU.add)
                ins.then_inc(sems["VE"], 1)
                W[f"g{g}_scan"] = bump("VE")
            blk.vector(_)

            def _(sc, g=g):
                sc.wait_ge(sems["VE"], W[f"g{g}_scan"])
                sc.activation(eAb[:], Acum[:], AF.Exp, scale=1.0 / 16.0)
                ins = sc.activation(eAn[:], Acum[:], AF.Exp, scale=-1.0 / 16.0)
                ins.then_inc(sems["AC"], 1)
                W[f"g{g}_eA"] = bump("AC")
            blk.scalar(_)

            def _(v, g=g):
                v.wait_ge(sems["AC"], W[f"g{g}_eA"])
                for h in range(H):
                    v.tensor_mul(qt_[:, h * 512:(h + 1) * 512], stg[:, CQS + h * 512: CQS + (h + 1) * 512],
                                 eAb[:, (h // 4) * 512:(h // 4 + 1) * 512])
                for kv in range(KVH):
                    ins = v.tensor_mul(kt_[:, kv * 512:(kv + 1) * 512], stg[:, CKS + kv * 512: CKS + (kv + 1) * 512],
                                       eAn[:, kv * 512:(kv + 1) * 512])
                ins.then_inc(sems["VE"], 1)
                W[f"g{g}_qtkt"] = bump("VE")
            blk.vector(_)

            for j in range(4):
                jt = f"g{g}j{j}"
                def _(sc, g=g, j=j, jt=jt):
                    if j > 0:
                        sc.wait_ge(sems["VE"], W[f"g{g}j{j-1}_supd"])
                    elif g > 0:
                        sc.wait_ge(sems["VE"], W[f"g{g-1}j3_supd"])
                    else:
                        sc.wait_ge(sems["VE"], W["rcv_init"])
                    if g == 0 and j == 0:
                        pass
                    sc.wait_ge(sems["VE"], W[f"g{g}_qtkt"])
                    if j == 0 and g > 0:
                        sc.wait_ge(sems["VE"], W[f"g{g-1}_oT"])
                    ins = sc.copy(Sb[:], Sst[:])
                    ins.then_inc(sems["AC"], 1)
                    W[jt + "_sb"] = bump("AC")
                blk.scalar(_)

                ac_ktn0 = cnt["AC"] + 1

                def _(t, g=g, j=j, jt=jt, ac_ktn0=ac_ktn0):
                    t.wait_ge(sems["VE"], W[f"g{g}_qtkt"])
                    if j > 0:
                        t.wait_ge(sems["VE"], W[f"g{g}j{j-1}_supd"])
                    elif g > 0:
                        t.wait_ge(sems["VE"], W[f"g{g-1}j3_supd"])
                    for kv in range(KVH):
                        if kv == 1:
                            t.wait_ge(sems["AC"], ac_ktn0)
                        ins = t.matmul(pb[6][:, 0:128],
                                       kt_[:, kv * 512 + j * 128: kv * 512 + (j + 1) * 128],
                                       IDENT, start=True, stop=True)
                        ins.then_inc(sems["PE"], 1)
                        W[jt + f"_ktr{kv}"] = bump("PE")
                blk.tensor(_)

                def _(sc, jt=jt):
                    for kv in range(KVH):
                        sc.wait_ge(sems["PE"], W[jt + f"_ktr{kv}"])
                        ins = sc.copy(ktn[:, kv * 128:(kv + 1) * 128], pb[6][:, 0:128])
                        ins.then_inc(sems["AC"], 1)
                        W[jt + f"_ktn{kv}"] = bump("AC")
                blk.scalar(_)

                def _(t, g=g, j=j, jt=jt):
                    t.wait_ge(sems["AC"], W[jt + "_ktn1"])
                    for kv in range(KVH):
                        ins = t.matmul(pb[6][:, 128 + kv * 128: 128 + (kv + 1) * 128],
                                       ktn[:, kv * 128:(kv + 1) * 128],
                                       vful[:, kv * 4096 + (g * 4 + j) * 128: kv * 4096 + (g * 4 + j) * 128 + 128],
                                       start=True, stop=True)
                        ins.then_inc(sems["PE"], 1)
                        W[jt + f"_u0{kv}"] = bump("PE")
                blk.tensor(_)

                for h in range(H):
                    kv = h // 4
                    def _(t, g=g, j=j, h=h, kv=kv, jt=jt):
                        if h >= 2:
                            t.wait_ge(sems["VE"], W[jt + f"_pg{h-2}"])
                        ins = t.matmul(pb[4 + h % 2][:, 0:128],
                                       kt_[:, kv * 512 + j * 128: kv * 512 + (j + 1) * 128],
                                       qt_[:, h * 512 + j * 128: h * 512 + (j + 1) * 128],
                                       start=True, stop=True)
                        ins.then_inc(sems["PE"], 1)
                        W[jt + f"_pgm{h}"] = bump("PE")
                    blk.tensor(_)

                    def _(v, h=h, jt=jt):
                        v.wait_ge(sems["PE"], W[jt + f"_pgm{h}"])
                        if h >= 1:
                            v.wait_ge(sems["PE"], W[jt + f"_om{h-1}"])
                        ins = v.tensor_mul(pgb[:], pb[4 + h % 2][:, 0:128], MASKGLA)
                        ins.then_inc(sems["VE"], 1)
                        W[jt + f"_pg{h}"] = bump("VE")
                    blk.vector(_)

                    def _(t, g=g, j=j, h=h, kv=kv, jt=jt):
                        t.wait_ge(sems["VE"], W[jt + f"_pg{h}"])
                        t.wait_ge(sems["AC"], W[jt + "_sb"])
                        if h >= 2:
                            t.wait_ge(sems["AC"], W[jt + f"_oev{h-2}"])
                        t.matmul(pb[2 + h % 2][:, 0:128],
                                 vful[:, kv * 4096 + (g * 4 + j) * 128: kv * 4096 + (g * 4 + j) * 128 + 128],
                                 pgb[:], start=True, stop=False)
                        ins = t.matmul(pb[2 + h % 2][:, 0:128],
                                       Sb[:, kv * 128:(kv + 1) * 128],
                                       qt_[:, h * 512 + j * 128: h * 512 + (j + 1) * 128],
                                       start=False, stop=True)
                        ins.then_inc(sems["PE"], 1)
                        W[jt + f"_om{h}"] = bump("PE")
                    blk.tensor(_)

                    def _(sc, g=g, j=j, h=h, jt=jt):
                        sc.wait_ge(sems["PE"], W[jt + f"_om{h}"])
                        ins = sc.mul(o05[:, h * 512 + j * 128: h * 512 + (j + 1) * 128],
                                     pb[2 + h % 2][:, 0:128], 0.5)
                        ins.then_inc(sems["AC"], 1)
                        W[jt + f"_oev{h}"] = bump("AC")
                    blk.scalar(_)

                def _(v, g=g, j=j, jt=jt):
                    v.wait_ge(sems["PE"], W[jt + "_u01"])
                    v.wait_ge(sems["AC"], W[jt + "_sb"])
                    for kv in range(KVH):
                        eac = eAb[:, kv * 512 + j * 128 + 127: kv * 512 + j * 128 + 128]
                        v.tensor_add(Ssc[:, kv * 128:(kv + 1) * 128], Sst[:, kv * 128:(kv + 1) * 128],
                                     pb[6][:, 128 + kv * 128: 128 + (kv + 1) * 128])
                        ins = v.tensor_scalar(Sst[:, kv * 128:(kv + 1) * 128], Ssc[:, kv * 128:(kv + 1) * 128],
                                              eac, None, ALU.mult)
                        ins.then_inc(sems["VE"], 1)
                        v.wait_ge(sems["VE"], bump("VE"))
                    W[f"g{g}j{j}_supd"] = cnt["VE"]
                blk.vector(_)

            K = 4 * g + 4
            KP = K // 2
            for h in range(H):
                kv = h // 4
                ht = f"g{g}h{h}"
                pe0 = cnt["PE"]
                ac0 = cnt["AC"]
                ve0 = cnt["VE"]
                order = []
                for kt in range(min(4, K)):
                    order.append(("sc", kt))
                for p in range(2, KP):
                    order.append(("av", 2 * p - 4))
                    order.append(("av", 2 * p - 3))
                    order.append(("sc", 2 * p))
                    order.append(("sc", 2 * p + 1))
                order += [("av", K - 4), ("av", K - 3), ("av", K - 2), ("av", K - 1)] if K >= 4 else []
                POS = {it: i + 1 for i, it in enumerate(order)}
                pe_sc_f = lambda kt, pe0=pe0, POS=POS: pe0 + POS[("sc", kt)]
                pe_av_f = lambda kt, pe0=pe0, POS=POS: pe0 + POS[("av", kt)]
                ac_ex_f = lambda p, ac0=ac0: ac0 + p + 1
                ve_mk_f = lambda kt, g=g, ve0=ve0: ve0 + (kt - 4 * g) + 1

                def _(t, g=g, h=h, kv=kv, K=K, ht=ht, order=order):
                    if h == 0:
                        t.wait_ge(sems["AC"], W[f"g{g}j3_oev{H-1}"])
                        t.wait_ge(sems["VE"], W[f"g{g}j3_pg{H-1}"])
                    else:
                        t.wait_ge(sems["VE"], W[f"g{g}h{h-1}_dev"])
                        t.wait_ge(sems["AC"], W[f"g{g}h{h-1}_yev"])
                    for (kind, kt) in order:
                        p = kt // 2
                        if kind == "sc":
                            if p >= 2:
                                t.wait_ge(sems["AC"], ac_ex_f(p - 2))
                            bankpair = pbA if p % 2 == 0 else pbB
                            ins = t.matmul(bankpair[:, (kt % 2) * 512:(kt % 2) * 512 + 512],
                                           skrT[:, kv * SEQ + kt * 128: kv * SEQ + (kt + 1) * 128],
                                           stg[:, CSQR + h * 512: CSQR + (h + 1) * 512],
                                           start=True, stop=True)
                            ins.then_inc(sems["PE"], 1)
                            assert bump("PE") == pe_sc_f(kt)
                        else:
                            if kt >= 4 * g:
                                t.wait_ge(sems["VE"], ve_mk_f(kt))
                            else:
                                t.wait_ge(sems["AC"], ac_ex_f(p))
                            slot = (p % 2) * 1024 + (kt % 2) * 512
                            t.matmul(pb[2][:],
                                     vful[:, kv * 4096 + kt * 128: kv * 4096 + (kt + 1) * 128],
                                     Pring[:, slot: slot + 512],
                                     start=(kt == 0), stop=(kt == K - 1))
                            ins = t.matmul(pb[3][0:1, :], ONES_B,
                                           Pring[:, slot: slot + 512],
                                           start=(kt == 0), stop=(kt == K - 1))
                            ins.then_inc(sems["PE"], 1)
                            assert bump("PE") == pe_av_f(kt)
                blk.tensor(_)

                def _(sc, g=g, h=h, K=K, KP=KP, ht=ht):
                    for p in range(KP):
                        sc.wait_ge(sems["PE"], pe_sc_f(2 * p + 1))
                        if p >= 2:
                            sc.wait_ge(sems["PE"], pe_av_f(2 * (p - 2) + 1))
                        bankpair = pbA if p % 2 == 0 else pbB
                        ins = sc.activation(Pring[:, (p % 2) * 1024:(p % 2) * 1024 + 1024],
                                            bankpair[:], AF.Exp, scale=float(HD) ** -0.5)
                        ins.then_inc(sems["AC"], 1)
                        assert bump("AC") == ac_ex_f(p)
                    sc.wait_ge(sems["PE"], pe_av_f(K - 1))
                    ins = sc.copy(yraw[:, h * 512:(h + 1) * 512], pb[2][:])
                    ins.then_inc(sems["AC"], 1)
                    W[ht + "_yev"] = bump("AC")
                blk.scalar(_)

                def _(v, g=g, h=h, K=K, ht=ht):
                    for kt in range(4 * g, K):
                        p = kt // 2
                        slot = (p % 2) * 1024 + (kt % 2) * 512
                        v.wait_ge(sems["AC"], ac_ex_f(p))
                        ins = v.tensor_mul(Pring[:, slot: slot + 512],
                                           Pring[:, slot: slot + 512],
                                           MASKDIAG(kt - 4 * g))
                        ins.then_inc(sems["VE"], 1)
                        assert bump("VE") == ve_mk_f(kt)
                    v.wait_ge(sems["PE"], pe_av_f(K - 1))
                    ins = v.tensor_copy(hdall[0:1, h * 512:(h + 1) * 512], pb[3][0:1, :])
                    ins.then_inc(sems["VE"], 1)
                    W[ht + "_dev"] = bump("VE")
                blk.vector(_)

            def _(v, g=g):
                v.wait_ge(sems["AC"], W[f"g{g}h{H-1}_yev"])
                v.wait_ge(sems["VE"], W[f"g{g}h{H-1}_dev"])
                with nc.allow_low_precision(reason="bf16 softmax denominator reciprocal"):
                    ins = v.reciprocal(hdb[0:1, :], hdall[0:1, :])
                ins.then_inc(sems["VE"], 1)
                W[f"g{g}_hdb"] = bump("VE")
            blk.vector(_)

            ve_nm0 = cnt["VE"]

            def _(t, g=g, ve_nm0=ve_nm0):
                t.wait_ge(sems["VE"], W[f"g{g}_hdb"])
                for h in range(H):
                    if h >= 2:
                        t.wait_ge(sems["VE"], ve_nm0 + 2 * (h - 1))
                    ins = t.matmul(pb[6 + h % 2][:], HALFROW_B, hdb[0:1, h * 512:(h + 1) * 512],
                                   start=True, stop=True)
                    ins.then_inc(sems["PE"], 1)
                    W[f"g{g}nb{h}"] = bump("PE")
            blk.tensor(_)

            def _(v, g=g, ve_nm0=ve_nm0):
                for h in range(H):
                    v.wait_ge(sems["PE"], W[f"g{g}nb{h}"])
                    if g > 0 and h == 0:
                        v.wait_ge(sems["dX"], g_ship[g - 1])
                    ins = v.tensor_mul(yscr[:], yraw[:, h * 512:(h + 1) * 512], pb[6 + h % 2][:])
                    ins.then_inc(sems["VE"], 1)
                    bump("VE")
                    ins = v.tensor_add(oTb[:, h * 512:(h + 1) * 512], yscr[:],
                                       o05[:, h * 512:(h + 1) * 512])
                    ins.then_inc(sems["VE"], 1)
                    bump("VE")
            blk.vector(_)
            W[f"g{g}_oT"] = cnt["VE"]

            def _(sync, g=g):
                sync.wait_ge(sems["VE"], W[f"g{g}_oT"])
                ins = sync.dma_start(out=o_in[g].rearrange("(h p) r -> p h r", p=128),
                                     in_=oTb[:].rearrange("p (h r) -> p h r", h=H))
                ins.then_inc(sems["dX"], 16)
                g_ship[g] = bump("dX", 16)
            blk.sync(_)


            g_pe_end[g] = cnt["PE"]
            g_ve_end[g] = cnt["VE"]
            g_ac_end[g] = cnt["AC"]

        o_cc_base = cnt["CC"]

        o_out_flat = o_out.rearrange("g row c -> (g row) c")
        ve_p3 = cnt["VE"]
        pe_p3 = cnt["PE"]
        for c_ in range(NB):
            for dmt_ in range(16):
                W[f"wo_c{c_}_ev{dmt_}"] = ve_p3 + c_ * 16 + dmt_ + 1
                W[f"wo_c{c_}_m{dmt_}"] = pe_p3 + c_ * 16 + dmt_ + 1
            W[f"wo_c{c_}_pe_end"] = pe_p3 + (c_ + 1) * 16
        for c in range(NB):
            wt = f"wo_c{c}"

            def _(sync, c=c, wt=wt):
                for dmt in range(16):
                    if dmt >= 2:
                        sync.wait_ge(sems["VE"], W[wt + f"_ev{dmt-2}"])
                    elif c > 0:
                        sync.wait_ge(sems["VE"], W[f"wo_c{c-1}_ev{dmt+14}"])
                    ins = sync.dma_start(
                        out=x1r[:, (dmt % 2) * 512:(dmt % 2) * 512 + 512].rearrange("p (o r) -> p o r", o=1),
                        in_=x1T.rearrange("(dt p) r -> p dt r", p=128)[:, dmt:dmt + 1, c * 512:(c + 1) * 512])
                    ins.then_inc(sems["dX"], 16)
                    W[wt + f"_x1{dmt}"] = bump("dX", 16)
            blk.sync(_)

            def _(t, c=c, wt=wt):
                t.wait_ge(sems["dG"], W[wt + "_ost"])
                if c == 0:
                    t.wait_ge(sems["AC"], g_ac_end[NG - 1])
                    t.wait_ge(sems["VE"], g_ve_end[NG - 1])
                for dmt in range(16):
                    t.wait_ge(sems["dG"], W[wt + f"_wo{dmt}"])
                    if dmt >= 2:
                        t.wait_ge(sems["VE"], W[wt + f"_ev{dmt-2}"])
                    for ot in range(16):
                        ins = t.matmul(pb[dmt % 2][:],
                                       wob[:, (dmt % 2) * 2048 + ot * 128:(dmt % 2) * 2048 + (ot + 1) * 128],
                                       ost[:, ot * 512:(ot + 1) * 512],
                                       start=(ot == 0), stop=(ot == 15))
                    ins.then_inc(sems["PE"], 1)
                    assert bump("PE") == W[wt + f"_m{dmt}"]
                assert cnt["PE"] == W[wt + "_pe_end"]
            blk.tensor(_)

            def _(v, c=c, wt=wt):
                for dmt in range(16):
                    v.wait_ge(sems["PE"], W[wt + f"_m{dmt}"])
                    v.wait_ge(sems["dX"], W[wt + f"_x1{dmt}"])
                    if dmt >= 2:
                        v.wait_ge(sems["OUT"], W[wt + f"_st{dmt-2}"])
                    elif c > 0:
                        v.wait_ge(sems["OUT"], W[f"wo_c{c-1}_st{dmt+14}"])
                    ins = v.tensor_add(x2o[:, (dmt % 2) * 512:(dmt % 2) * 512 + 512],
                                       pb[dmt % 2][:], x1r[:, (dmt % 2) * 512:(dmt % 2) * 512 + 512])
                    ins.then_inc(sems["VE"], 1)
                    assert bump("VE") == W[wt + f"_ev{dmt}"]
            blk.vector(_)


        qkv_flat = qkv_out.rearrange("s r c -> (s r c)")
        o_flat1 = o_out.rearrange("g row c -> (g row c)")

        def _(gp):
            gp.wait_ge(sems["dX"], W["p_loads"])
            REG = [nc.values_load(offs_s[0:1, i:i + 1], engines=[PoolE],
                                  skip_runtime_bounds_check=True) for i in range(12)]
            for g in range(NG):
                slot, step = g // 4, g % 4
                gp.wait_ge(sems["CC"], cc_qkv_base + step + 1)
                gp.wait_ge(sems["PE"], g_pe_end[g - 1])
                gp.wait_ge(sems["VE"], g_ve_end[g - 1])
                gp.wait_ge(sems["AC"], g_ac_end[g - 1])
                gp.dma_start(
                    out=stg[:].rearrange("p (grp r) -> p grp r", r=512),
                    in_=qkv_flat[ds(REG[g], 26 * 128 * 512)]
                    .rearrange("(grp p r) -> p grp r", p=128, r=512)).then_inc(sems["dG"], 16)
                bump("dG", 16)
                assert cnt["dG"] == W[f"g{g}_staged"]
                if g > 0:
                    gp.wait_ge(sems["dX"], g_ship[g - 1])
                    gp.wait_ge(sems["CC"], cnt["CC"])
                    gp.collective_compute("AllGather", ALU.bypass,
                                          replica_groups=[[0, 1], [2, 3], [4, 5], [6, 7]],
                                          ins=[o_in[g - 1]], outs=[o_out[g - 1]]).then_inc(sems["CC"], 1)
                    assert bump("CC") == o_cc_base + g
            gp.wait_ge(sems["dX"], g_ship[NG - 1])
            gp.wait_ge(sems["CC"], cnt["CC"])
            gp.collective_compute("AllGather", ALU.bypass,
                                  replica_groups=[[0, 1], [2, 3], [4, 5], [6, 7]],
                                  ins=[o_in[NG - 1]], outs=[o_out[NG - 1]]).then_inc(sems["CC"], 1)
            assert bump("CC") == o_cc_base + NG

            for c in range(NB):
                wt = f"wo_c{c}"
                gp.wait_ge(sems["CC"], o_cc_base + min(5 + c, NG))
                if c > 0:
                    gp.wait_ge(sems["PE"], W[f"wo_c{c-1}_pe_end"])
                gp.dma_start(out=ost[:].rearrange("p (ot r) -> p ot r", ot=16),
                             in_=o_flat1[ds(REG[8 + c], 2048 * 512)].rearrange("(ot p r) -> p ot r", p=128, r=512)
                             ).then_inc(sems["dG"], 16)
                assert bump("dG", 16) == W[wt + "_ost"]
                for dmt in range(16):
                    if dmt >= 2:
                        gp.wait_ge(sems["PE"], W[wt + f"_m{dmt-2}"])
                        gp.wait_ge(sems["VE"], W[wt + f"_ev{dmt-2}"])
                        gp.dma_start(
                            out=x2T.rearrange("(dt p) r -> p dt r", p=128)[:, dmt - 2:dmt - 1, c * 512:(c + 1) * 512],
                            in_=x2o[:, (dmt % 2) * 512:(dmt % 2) * 512 + 512].rearrange("p (o r) -> p o r", o=1)
                        ).then_inc(sems["OUT"], 16)
                        assert bump("OUT", 16) == W[wt + f"_st{dmt-2}"]
                    gp.dma_start(out=wob[:, (dmt % 2) * 2048:(dmt % 2) * 2048 + 2048].rearrange("p (ot cc) -> p ot cc", ot=16),
                                 in_=wo_in[dmt].rearrange("ot p cc -> p ot cc")).then_inc(sems["dG"], 16)
                    assert bump("dG", 16) == W[wt + f"_wo{dmt}"]
                for dmt in range(14, 16):
                    gp.wait_ge(sems["VE"], W[wt + f"_ev{dmt}"])
                    gp.dma_start(
                        out=x2T.rearrange("(dt p) r -> p dt r", p=128)[:, dmt:dmt + 1, c * 512:(c + 1) * 512],
                        in_=x2o[:, (dmt % 2) * 512:(dmt % 2) * 512 + 512].rearrange("p (o r) -> p o r", o=1)
                    ).then_inc(sems["OUT"], 16)
                    assert bump("OUT", 16) == W[wt + f"_st{dmt}"]
        blk.gpsimd(_)

        rcv.close()
        return cnt["VE"], max(cnt["dX"], 0), cnt["OUT"]

    with ExitStack() as pes:
        phase_prologue(pes)

    def _(t):
        t.wait_ge(sems["CC"], W["p_wAG1"])
    blk.tensor(_)
    def _(sync):
        sync.wait_ge(sems["CC"], W["p_wAG1"])
    blk.sync(_)

    if N_SUBLAYERS == 1:
        with ExitStack() as fes:
            F = alloc_ffn_bufs(fes, "f1")
            phase_ffn("f1", F, xT_in, out_T, "OUT", free0=0)
        final_out = cnt["OUT"]
    else:
        with ExitStack() as fes:
            F = alloc_ffn_bufs(fes, "f1")
            phase_ffn("f1", F, xT_in, x1T, "dS", free0=0)
        f1_pe_end = cnt["PE"]
        f1_stored = [W[f"f1_b{b}_stored"] for b in range(NB)]
        at_ve_end, at_dx_end, at_out_end = phase_attn(W[f"f1_b{NB-1}_evacall"], f1_pe_end, f1_stored)

        def _(sync, at_out_end=at_out_end):
            sync.wait_ge(sems["OUT"], at_out_end)
        blk.sync(_)

        with ExitStack() as fes:
            F3 = alloc_ffn_bufs(fes, "f3")
            phase_ffn("f3", F3, x2T, out_T, "OUT", free0=at_ve_end, dx_gate=at_dx_end)
        final_out = cnt["OUT"]

    def _(sync, final_out=final_out):
        sync.wait_ge(sems["OUT"], final_out)
    blk.sync(_)

    blk.__exit__(None, None, None)
    es.close()
    return nc



def _host_prep(inputs):
    x = np.ascontiguousarray(np.asarray(inputs["L0_attn_hidden_states"], dtype=np.float32))
    Wq = np.asarray(inputs["L0_attn_Wq"], dtype=np.float32)
    Wk = np.asarray(inputs["L0_attn_Wk"], dtype=np.float32)
    Wv = np.asarray(inputs["L0_attn_Wv"], dtype=np.float32)
    Wo = np.asarray(inputs["L0_attn_Wo"], dtype=np.float32)
    w1 = np.asarray(inputs["L0_ffn_w1"], dtype=np.float32)
    b1 = np.asarray(inputs["L0_ffn_b1"], dtype=np.float32)
    w2 = np.asarray(inputs["L0_ffn_w2"], dtype=np.float32)
    b2 = np.asarray(inputs["L0_ffn_b2"], dtype=np.float32)
    import ml_dtypes
    bf = ml_dtypes.bfloat16

    w1t = np.ascontiguousarray(w1.reshape(FT, 128, NT, 128).transpose(0, 2, 3, 1))
    w2t = np.ascontiguousarray(w2.reshape(16, 128, FT, 128).transpose(0, 2, 3, 1))
    wqkv_cat = np.concatenate([Wq, Wk, Wv], axis=1)
    wqkvN = np.ascontiguousarray(wqkv_cat.reshape(NT, 128, 6, 512).transpose(2, 0, 1, 3))
    woT = np.ascontiguousarray(Wo.reshape(16, 128, 16, 128).transpose(2, 0, 1, 3))

    constf = np.zeros((128, 96), np.float32)
    constf[:, 0] = 1.0
    constf[:, 1:17] = b2.reshape(16, 128).T
    constf[:, 17:81] = b1.reshape(FT, 128).T
    constf[0, 82] = EPS

    constb = np.zeros((128, 5248), np.float32)
    constb[:, 0] = 1.0
    constb[:, 2:130] = np.eye(128)
    s_idx = np.arange(128)[:, None]
    q_idx512 = np.arange(512)[None, :]
    t_idx = np.arange(128)[None, :]
    constb[:, 130:258] = (t_idx >= s_idx).astype(np.float32)
    for j in range(4):
        constb[:, 258 + j * 512: 258 + (j + 1) * 512] = (q_idx512 >= (s_idx + 128 * j)).astype(np.float32)
    constb[0, 2306:2434] = 1.0
    constb[0, 2434:2946] = 1.0
    constb[0, 2946:4994] = 0.5 * b2
    constb[0, 4994:5122] = 0.5
    constb = constb.astype(bf)

    inv_freq = 1.0 / (10000.0 ** (np.arange(0, HD, 2, dtype=np.float64) / HD))
    pos = np.arange(SEQ, dtype=np.float64)
    fr = pos[:, None] * inv_freq[None, :]
    cosn = np.cos(fr).astype(np.float32).reshape(32, 128, 64).transpose(1, 0, 2).reshape(128, 2048)
    sinn = np.sin(fr).astype(np.float32).reshape(32, 128, 64).transpose(1, 0, 2).reshape(128, 2048)
    trig = np.concatenate([cosn, sinn], axis=1).astype(bf)

    per_core = []
    for c in range(8):
        b_, r = c // 2, c % 2
        xT = np.ascontiguousarray(x[b_, r * ROWS:(r + 1) * ROWS, :].T)
        offs = np.zeros((1, 16), np.int32)
        for g_ in range(8):
            slot_, step_ = g_ // 4, g_ % 4
            offs[0, g_] = (step_ * 13312 + slot_ * 6656 + r * 3328) * 512
        for lc in range(4):
            offs[0, 8 + lc] = ((4 * r + lc) * 2048) * 512
        per_core.append({
            "xT_in": xT,
            "w1s": np.ascontiguousarray(w1t[c * 8:(c + 1) * 8]),
            "w2s": np.ascontiguousarray(w2t[c * 2:(c + 1) * 2]),
            "wqkvN": wqkvN,
            "woT": woT,
            "constf": constf,
            "constb": constb,
            "trigN": trig,
            "offs": offs,
        })
    return per_core


def kernel(**inputs):
    per_core = _host_prep(inputs)
    nc = build_nc()
    res = run_bass_kernel_spmd(nc, per_core, core_ids=list(range(8)))
    out = np.empty((4, SEQ, DM), np.float32)
    for c in range(8):
        b_, r = c // 2, c % 2
        out[b_, r * ROWS:(r + 1) * ROWS, :] = res.results[c]["out_T"].T
    return out


# version 04
# speedup vs baseline: 1.0442x; 1.0179x over previous
import numpy as np
from contextlib import ExitStack

import concourse.bass as bass
import concourse.mybir as mybir
from concourse.bass_utils import run_bass_kernel_spmd

F32 = mybir.dt.float32
BF16 = mybir.dt.bfloat16
I32 = mybir.dt.int32
AF = mybir.ActivationFunctionType
ALU = mybir.AluOpType
AX = mybir.AxisListType
ds = bass.ds

DM = 2048
DFF = 8192
NT = DM // 128
FT = DFF // 128
ROWS = 2048
NB = 4
BLKR = 512
SEQ = 4096
NG = 8
H = 8
KVH = 2
HD = 128
EPS = 1e-5

N_SUBLAYERS = 3


def build_nc():
    nc = bass.Bass("TRN2", target_bir_lowering=False, detect_race_conditions=False)

    xT_in = nc.dram_tensor("xT_in", [DM, ROWS], F32, kind="ExternalInput").ap()
    w1s_in = nc.dram_tensor("w1s", [8, NT, 128, 128], F32, kind="ExternalInput").ap()
    w2s_in = nc.dram_tensor("w2s", [2, FT, 128, 128], F32, kind="ExternalInput").ap()
    wqkv_in = nc.dram_tensor("wqkvN", [6, NT, 128, 512], F32, kind="ExternalInput").ap()
    wo_in = nc.dram_tensor("woT", [16, 16, 128, 128], F32, kind="ExternalInput").ap()
    constf_in = nc.dram_tensor("constf", [128, 96], F32, kind="ExternalInput").ap()
    constb_in = nc.dram_tensor("constb", [128, 5248], BF16, kind="ExternalInput").ap()
    trig_in = nc.dram_tensor("trigN", [128, 4096], BF16, kind="ExternalInput").ap()
    offs_in = nc.dram_tensor("offs", [1, 16], I32, kind="ExternalInput").ap()

    out_T = nc.dram_tensor("out_T", [DM, ROWS], F32, kind="ExternalOutput").ap()

    x1T = nc.dram_tensor("x1T", [DM, ROWS], F32, kind="Internal").ap()
    x2T = nc.dram_tensor("x2T", [DM, ROWS], F32, kind="Internal").ap()
    sc_loc = nc.dram_tensor("sc_loc", [128, 128], F32, kind="Internal").ap()
    sc_all = nc.dram_tensor("sc_all", [1024, 128], F32, kind="Internal").ap()
    w1q_loc = nc.dram_tensor("w1q_loc", [8 * NT * 128, 128], BF16, kind="Internal").ap()
    w1q_all = nc.dram_tensor("w1q_all", [FT * NT * 128, 128], BF16, kind="Internal").ap()
    w2q_loc = nc.dram_tensor("w2q_loc", [2 * FT * 128, 128], BF16, kind="Internal").ap()
    w2q_all = nc.dram_tensor("w2q_all", [16 * FT * 128, 128], BF16, kind="Internal").ap()
    qkv_in = nc.dram_tensor("qkv_in", [4, 6656, 512], BF16, kind="Internal").ap()
    qkv_out = nc.dram_tensor("qkv_out", [4, 13312, 512], BF16, kind="Internal").ap()
    o_in = nc.dram_tensor("o_in", [NG, 1024, 512], BF16, kind="Internal").ap()
    o_out = nc.dram_tensor("o_out", [NG, 2048, 512], BF16, kind="Internal").ap()

    es = ExitStack()

    def SBp(name, shape, dt):
        return es.enter_context(nc.sbuf_tensor(name, shape, dt))

    constf = SBp("constf_s", [128, 96], F32)
    constb = SBp("constb_s", [128, 5248], BF16)
    trig = SBp("trig_s", [128, 4096], BF16)
    offs_s = SBp("offs_s", [1, 16], I32)
    onesrow_f = SBp("onesrow_f", [1, 128], F32)
    thr_s = SBp("thr_s", [1, 4], F32)
    thrv = SBp("thrv", [128, 2], F32)
    nthrv = SBp("nthrv", [128, 2], F32)
    stat = SBp("stat", [1, 2048], F32)
    statb = SBp("statb", [1, 1024], BF16)

    pbA = es.enter_context(nc.psum_tensor("pbA", [128, 1024], F32))
    pb2_t = es.enter_context(nc.psum_tensor("pb2", [128, 512], F32))
    pb3_t = es.enter_context(nc.psum_tensor("pb3", [128, 512], F32))
    pbB = es.enter_context(nc.psum_tensor("pbB", [128, 1024], F32))
    pb6_t = es.enter_context(nc.psum_tensor("pb6", [128, 512], F32))
    pb7_t = es.enter_context(nc.psum_tensor("pb7", [128, 512], F32))
    pb = [pbA[:, 0:512], pbA[:, 512:1024], pb2_t.ap(), pb3_t.ap(),
          pbB[:, 0:512], pbB[:, 512:1024], pb6_t.ap(), pb7_t.ap()]

    sems = {}
    cnt = {}
    for s in ["dW", "dX", "dG", "dS", "CC", "PE", "AC", "VE", "OUT"]:
        sems[s] = es.enter_context(nc.semaphore(s))
        cnt[s] = 0

    def bump(s, by=1):
        cnt[s] += by
        return cnt[s]

    blk = nc.Block()
    blk.__enter__()

    W = {}

    ONES_F = constf[:, 0:1]
    B2COL = lambda dmt: constf[:, 1 + dmt: 2 + dmt]
    B1COL = lambda ft: constf[:, 17 + ft: 18 + ft]
    ONES_B = constb[:, 0:1]
    IDENT = constb[:, 2:130]
    MASKGLA = constb[:, 130:258]
    MASKDIAG = lambda j: constb[:, 258 + j * 512: 258 + (j + 1) * 512]
    ONESROW_B = constb[0:1, 2306:2434]
    HALFROW_B = constb[0:1, 4994:5122]
    ONESROW512_B = constb[0:1, 2434:2946]
    B2ROW = lambda dmt: constb[0:1, 2946 + dmt * 128: 2946 + (dmt + 1) * 128]

    def phase_prologue(pes):
        wslab = pes.enter_context(nc.sbuf_tensor("wslab", [128, 16384], F32))
        wslab2 = pes.enter_context(nc.sbuf_tensor("wslab2", [128, 16384], F32))
        wqslab = pes.enter_context(nc.sbuf_tensor("wqslab", [128, 16384], BF16))
        tmpq = pes.enter_context(nc.sbuf_tensor("tmpq", [128, 8192], BF16))
        sc_s = pes.enter_context(nc.sbuf_tensor("sc_s", [128, 128], F32))
        sc8 = pes.enter_context(nc.sbuf_tensor("sc8", [128, 16], F32))
        scacc = pes.enter_context(nc.sbuf_tensor("scacc", [128, 2], F32))

        def _(sync):
            sync.dma_start(out=constf[:], in_=constf_in).then_inc(sems["dX"], 16)
            sync.dma_start(out=constb[:], in_=constb_in).then_inc(sems["dX"], 16)
            sync.dma_start(out=trig[:], in_=trig_in).then_inc(sems["dX"], 16)
            sync.dma_start(out=offs_s[:], in_=offs_in).then_inc(sems["dX"], 16)
            sync.dma_start(out=wslab[:].rearrange("p (f d c) -> p f d c", f=8, d=NT),
                           in_=w1s_in.rearrange("f d p c -> p f d c")).then_inc(sems["dX"], 16)
        blk.sync(_)
        bump("dX", 16 * 5)
        W["p_loads"] = cnt["dX"]

        def _(sync):
            sync.dma_start(out=wslab2[:].rearrange("p (f d c) -> p f d c", f=2, d=FT),
                           in_=w2s_in.rearrange("f d p c -> p f d c")).then_inc(sems["dX"], 16)
        blk.sync(_)
        W["p_w2load"] = bump("dX", 16)

        def _(v):
            v.wait_ge(sems["dX"], W["p_loads"])
            v.memset(onesrow_f[:], 1.0)
            v.memset(sc_s[:], 0.0)
            v.reduce_sum(sc_s[:, 0:1], wslab[:], AX.X, apply_absolute_value=True)
            v.wait_ge(sems["dX"], W["p_w2load"])
            ins = v.reduce_sum(sc_s[:, 1:2], wslab2[:], AX.X, apply_absolute_value=True)
            ins.then_inc(sems["VE"], 1)
        blk.vector(_)
        W["p_red"] = bump("VE")

        def _(sync):
            sync.wait_ge(sems["VE"], W["p_red"])
            sync.dma_start(out=sc_loc, in_=sc_s[:]).then_inc(sems["dX"], 16)
        blk.sync(_)
        W["p_scst"] = bump("dX", 16)

        def _(g):
            g.wait_ge(sems["dX"], W["p_scst"])
            g.collective_compute("AllGather", ALU.bypass, replica_groups=[list(range(8))],
                                 ins=[sc_loc], outs=[sc_all]).then_inc(sems["CC"], 1)
        blk.gpsimd(_)
        W["p_scAG"] = bump("CC")

        def _(sync):
            sync.wait_ge(sems["CC"], W["p_scAG"])
            sync.dma_start(out=sc8[:].rearrange("p (k c) -> p k c", k=8),
                           in_=sc_all[:, 0:2].rearrange("(k p) c -> p k c", p=128)).then_inc(sems["dX"], 16)
        blk.sync(_)
        W["p_sc8"] = bump("dX", 16)

        def _(v):
            v.wait_ge(sems["dX"], W["p_sc8"])
            v.reduce_sum(scacc[:], sc8[:].rearrange("p (k c) -> p c k", k=8), AX.X).then_inc(sems["VE"], 1)
        blk.vector(_)
        W["p_scsum"] = bump("VE")

        def _(t):
            t.wait_ge(sems["VE"], W["p_scsum"])
            t.matmul(pb[6][0:1, 0:2], ONES_F, scacc[:], start=True, stop=True).then_inc(sems["PE"], 1)
        blk.tensor(_)
        W["p_tot"] = bump("PE")

        def _(sc):
            sc.wait_ge(sems["PE"], W["p_tot"])
            sc.mul(thr_s[0:1, 0:2], pb[6][0:1, 0:2], 0.7 / (DFF * DM)).then_inc(sems["AC"], 1)
        blk.scalar(_)
        W["p_thr"] = bump("AC")

        def _(t):
            t.wait_ge(sems["AC"], W["p_thr"])
            t.matmul(pb[6][:, 0:1], onesrow_f[:], thr_s[0:1, 0:1], start=True, stop=True)
            t.matmul(pb[6][:, 1:2], onesrow_f[:], thr_s[0:1, 1:2], start=True, stop=True).then_inc(sems["PE"], 1)
        blk.tensor(_)
        W["p_thrb"] = bump("PE")

        def _(sc):
            sc.wait_ge(sems["PE"], W["p_thrb"])
            sc.copy(thrv[:], pb[6][:, 0:2])
            sc.mul(nthrv[:], pb[6][:, 0:2], -1.0).then_inc(sems["AC"], 1)
        blk.scalar(_)
        W["p_thrv"] = bump("AC")

        for wi in range(2):
            slab = wslab if wi == 0 else wslab2
            wq_loc_t = w1q_loc if wi == 0 else w2q_loc
            wq_all_t = w1q_all if wi == 0 else w2q_all
            dpar = NT if wi == 0 else FT

            def _(v, wi=wi, slab=slab):
                v.wait_ge(sems["AC"], W["p_thrv"])
                if wi == 1:
                    v.wait_ge(sems["dX"], W["p_qst0"])
                for hh in range(2):
                    sl = slice(hh * 8192, (hh + 1) * 8192)
                    v.tensor_scalar(tmpq[:], slab[:, sl], nthrv[:, wi:wi + 1], None, ALU.is_le)
                    ins = v.scalar_tensor_tensor(wqslab[:, sl], slab[:, sl], thrv[:, wi:wi + 1],
                                                 tmpq[:], ALU.is_ge, ALU.subtract)
                ins.then_inc(sems["VE"], 1)
            blk.vector(_)
            W[f"p_q{wi}"] = bump("VE")

            def _(sync, wi=wi, wq_loc_t=wq_loc_t, dpar=dpar):
                sync.wait_ge(sems["VE"], W[f"p_q{wi}"])
                sync.dma_start(out=wq_loc_t.rearrange("(f d p) c -> p f d c", p=128, d=dpar),
                               in_=wqslab[:].rearrange("p (f d c) -> p f d c", d=dpar, c=128)).then_inc(sems["dX"], 16)
            blk.sync(_)
            W[f"p_qst{wi}"] = bump("dX", 16)

            def _(g, wi=wi, wq_loc_t=wq_loc_t, wq_all_t=wq_all_t):
                g.wait_ge(sems["dX"], W[f"p_qst{wi}"])
                g.wait_ge(sems["CC"], cnt["CC"])
                g.collective_compute("AllGather", ALU.bypass, replica_groups=[list(range(8))],
                                     ins=[wq_loc_t], outs=[wq_all_t]).then_inc(sems["CC"], 1)
            blk.gpsimd(_)
            W[f"p_wAG{wi}"] = bump("CC")

    def emit_ln(tag, F, free_after, pre_bf16=False, xc_sem="dX"):
        xc, xcb, xsq, xn, scr = F["xc"], F["xcb"], F["xsq"], F["xn"], F["scr"]

        def _(v):
            v.wait_ge(sems[xc_sem], W[tag + "_xc"])
            v.wait_ge(sems["VE"], free_after)
            if pre_bf16:
                v.engine_nop().then_inc(sems["VE"], 1)
            else:
                v.tensor_copy(xcb[:, 0:NT * BLKR], xc[:]).then_inc(sems["VE"], 1)
        blk.vector(_)
        ve_cast = bump("VE")

        def _(sc):
            sc.wait_ge(sems["VE"], ve_cast)
            sc.activation(xsq[:], xc[:], AF.Square).then_inc(sems["AC"], 1)
        blk.scalar(_)
        ac_sq = bump("AC")

        def _(t):
            t.wait_ge(sems["VE"], ve_cast)
            t.wait_ge(sems["AC"], ac_sq)
            for dt in range(NT):
                t.matmul(pb[4][0:1, :], ONES_B, xcb[:, dt * 512:(dt + 1) * 512],
                         start=(dt == 0), stop=(dt == NT - 1))
            for dt in range(NT):
                ins = t.matmul(pb[5][0:1, :], ONES_B, xsq[:, dt * 512:(dt + 1) * 512],
                               start=(dt == 0), stop=(dt == NT - 1))
            ins.then_inc(sems["PE"], 1)
        blk.tensor(_)
        pe_sums = bump("PE")

        def _(sc):
            sc.wait_ge(sems["PE"], pe_sums)
            sc.mul(stat[0:1, 0:512], pb[4][0:1, :], -1.0 / DM)
            sc.mul(stat[0:1, 512:1024], pb[5][0:1, :], 1.0 / DM).then_inc(sems["AC"], 1)
        blk.scalar(_)
        ac_stat = bump("AC")

        def _(v):
            v.wait_ge(sems["AC"], ac_stat)
            v.tensor_mul(stat[0:1, 1024:1536], stat[0:1, 0:512], stat[0:1, 0:512])
            v.tensor_sub(stat[0:1, 1536:2048], stat[0:1, 512:1024], stat[0:1, 1024:1536]).then_inc(sems["VE"], 1)
        blk.vector(_)
        ve_var = bump("VE")

        def _(sc):
            sc.wait_ge(sems["VE"], ve_var)
            sc.activation(stat[0:1, 1024:1536], stat[0:1, 1536:2048], AF.Ln, bias=constf[0:1, 82:83])
            sc.activation(stat[0:1, 1536:2048], stat[0:1, 1024:1536], AF.Exp, scale=-0.5)
            sc.copy(statb[0:1, 0:512], stat[0:1, 0:512])
            sc.copy(statb[0:1, 512:1024], stat[0:1, 1536:2048]).then_inc(sems["AC"], 1)
        blk.scalar(_)
        ac_rstd = bump("AC")

        def _(t):
            t.wait_ge(sems["AC"], ac_rstd)
            t.matmul(pb[4][:], ONESROW_B, statb[0:1, 0:512], start=True, stop=True)
            t.matmul(pb[5][:], ONESROW_B, statb[0:1, 512:1024], start=True, stop=True).then_inc(sems["PE"], 1)
        blk.tensor(_)
        pe_bc = bump("PE")

        def _(v):
            v.wait_ge(sems["PE"], pe_bc)
            for dt in range(NT):
                v.tensor_add(scr[:], xc[:, dt * 512:(dt + 1) * 512], pb[4][:])
                ins = v.tensor_mul(xn[:, dt * 512:(dt + 1) * 512], scr[:], pb[5][:])
            ins.then_inc(sems["VE"], 1)
        blk.vector(_)
        W[tag + "_apply"] = bump("VE")

    def phase_ffn(name, F, xin, xout, out_sem, free0, dx_gate=0):
        for b in range(NB):
            tag = f"{name}_b{b}"
            prev_free = W.get(f"{name}_b{b-1}_evacall", free0)

            def _(sync, b=b, prev_free=prev_free, dx_gate=dx_gate if b == 0 else 0, xin=xin, F=F):
                sync.wait_ge(sems["VE"], prev_free)
                if dx_gate:
                    sync.wait_ge(sems["dX"], dx_gate)
                sync.dma_start(
                    out=F["xc"][:].rearrange("p (dt r) -> p dt r", dt=NT),
                    in_=xin.rearrange("(dt p) r -> p dt r", p=128)[:, :, ds(b * BLKR, BLKR)],
                ).then_inc(sems["dX"], 16)
            blk.sync(_)
            W[tag + "_xc"] = bump("dX", 16)

            emit_ln(tag, F, prev_free)

            gelu_base = cnt["AC"]
            dw_w1_base = cnt["dW"]

            def _(sync, gelu_base=gelu_base, F=F):
                for ft in range(FT):
                    if ft >= 2:
                        sync.wait_ge(sems["AC"], gelu_base + ft - 1)
                    sync.dma_start(
                        out=F["w1b"][:, (ft % 2) * DM:(ft % 2) * DM + DM].rearrange("p (d c) -> p d c", c=128),
                        in_=w1q_all[ft * DM:(ft + 1) * DM, :].rearrange("(d p) c -> p d c", p=128),
                    ).then_inc(sems["dW"], 16)
            blk.sync(_)
            bump("dW", 16 * FT)

            pe_h_base = cnt["PE"]

            def _(t, tag=tag, gelu_base=gelu_base, dw_w1_base=dw_w1_base, F=F):
                t.wait_ge(sems["VE"], W[tag + "_apply"])
                for ft in range(FT):
                    t.wait_ge(sems["dW"], dw_w1_base + (ft + 1) * 16)
                    if ft >= 2:
                        t.wait_ge(sems["AC"], gelu_base + ft - 1)
                    for dt in range(NT):
                        ins = t.matmul(pb[ft % 2][:],
                                       F["w1b"][:, (ft % 2) * DM + dt * 128:(ft % 2) * DM + (dt + 1) * 128],
                                       F["xn"][:, dt * 512:(dt + 1) * 512],
                                       start=(dt == 0), stop=(dt == NT - 1))
                    ins.then_inc(sems["PE"], 1)
            blk.tensor(_)
            bump("PE", FT)

            def _(sc, pe_h_base=pe_h_base, F=F):
                for ft in range(FT):
                    sc.wait_ge(sems["PE"], pe_h_base + ft + 1)
                    sc.activation(F["hT"][:, ft * 512:(ft + 1) * 512], pb[ft % 2][:],
                                  AF.Gelu, bias=B1COL(ft)).then_inc(sems["AC"], 1)
            blk.scalar(_)
            bump("AC", FT)
            ac_gelu_all = cnt["AC"]
            assert ac_gelu_all == gelu_base + FT

            evac_base = cnt["VE"]
            dw_w2_base = cnt["dW"]
            prev_evacall = W.get(f"{name}_b{b-1}_evacall", 0)

            def _(sync, evac_base=evac_base, F=F, b=b, name=name):
                if b == 0 and name == "f1":
                    sync.wait_ge(sems["CC"], W["p_wAG1"])
                for dmt in range(16):
                    if dmt >= 2:
                        sync.wait_ge(sems["VE"], evac_base + dmt - 1)
                    sync.dma_start(
                        out=F["w2b"][:, (dmt % 2) * DFF:(dmt % 2) * DFF + DFF].rearrange("p (f c) -> p f c", c=128),
                        in_=w2q_all[dmt * DFF:(dmt + 1) * DFF, :].rearrange("(f p) c -> p f c", p=128),
                    ).then_inc(sems["dW"], 16)
            blk.sync(_)
            bump("dW", 16 * 16)

            pe_o_base = cnt["PE"]

            def _(t, ac_gelu_all=ac_gelu_all, evac_base=evac_base,
                  dw_w2_base=dw_w2_base, prev_evacall=prev_evacall, F=F):
                t.wait_ge(sems["AC"], ac_gelu_all)
                if prev_evacall:
                    t.wait_ge(sems["VE"], prev_evacall)
                for dmt in range(16):
                    t.wait_ge(sems["dW"], dw_w2_base + (dmt + 1) * 16)
                    if dmt >= 2:
                        t.wait_ge(sems["VE"], evac_base + dmt - 1)
                    for ft in range(FT):
                        t.matmul(pb[2 + dmt % 2][:],
                                 F["w2b"][:, (dmt % 2) * DFF + ft * 128:(dmt % 2) * DFF + (ft + 1) * 128],
                                 F["hT"][:, ft * 512:(ft + 1) * 512],
                                 start=(ft == 0), stop=False)
                    ins = t.matmul(pb[2 + dmt % 2][:], B2ROW(dmt), ONESROW512_B,
                                   start=False, stop=True)
                    ins.then_inc(sems["PE"], 1)
            blk.tensor(_)
            bump("PE", 16)

            xst_base = cnt[out_sem]

            def _(v, pe_o_base=pe_o_base, xst_base=xst_base, out_sem=out_sem, F=F):
                for dmt in range(16):
                    v.wait_ge(sems["PE"], pe_o_base + dmt + 1)
                    if dmt >= 2:
                        v.wait_ge(sems[out_sem], xst_base + (dmt - 1) * 16)
                    v.scalar_tensor_tensor(F["xo"][:, (dmt % 2) * 512:(dmt % 2) * 512 + 512],
                                           pb[2 + dmt % 2][:], 0.5,
                                           F["xc"][:, dmt * 512:(dmt + 1) * 512],
                                           ALU.mult, ALU.add).then_inc(sems["VE"], 1)
            blk.vector(_)
            bump("VE", 16)
            W[tag + "_evacall"] = cnt["VE"]

            def _(g, b=b, evac_base=evac_base, xout=xout, out_sem=out_sem, F=F):
                for dmt in range(16):
                    g.wait_ge(sems["VE"], evac_base + dmt + 1)
                    g.dma_start(
                        out=xout.rearrange("(dt p) r -> p dt r", p=128)[:, dmt:dmt + 1, ds(b * BLKR, BLKR)],
                        in_=F["xo"][:, (dmt % 2) * 512:(dmt % 2) * 512 + 512].rearrange("p (o r) -> p o r", o=1),
                    ).then_inc(sems[out_sem], 16)
            blk.gpsimd(_)
            bump(out_sem, 16 * 16)
            W[tag + "_stored"] = cnt[out_sem]
        return W[f"{name}_b{NB-1}_evacall"]

    def alloc_ffn_bufs(stack, pfx):
        F = {}
        F["xc"] = stack.enter_context(nc.sbuf_tensor(pfx + "xc", [128, NT * BLKR], F32))
        F["xn"] = stack.enter_context(nc.sbuf_tensor(pfx + "xn", [128, NT * BLKR], BF16))
        F["hT"] = stack.enter_context(nc.sbuf_tensor(pfx + "hT", [128, FT * BLKR], BF16))
        F["w1b"] = stack.enter_context(nc.sbuf_tensor(pfx + "w1b", [128, 2 * DM], BF16))
        F["w2b"] = stack.enter_context(nc.sbuf_tensor(pfx + "w2b", [128, 2 * DFF], BF16))
        F["scr"] = stack.enter_context(nc.sbuf_tensor(pfx + "scr", [128, 512], F32))
        F["scr2"] = stack.enter_context(nc.sbuf_tensor(pfx + "scr2", [128, 2 * 512], F32))
        F["xo"] = stack.enter_context(nc.sbuf_tensor(pfx + "xo", [128, 2 * 512], F32))
        F["xsq"] = F["xn"]
        F["xcb"] = F["hT"]
        return F



    BLK_SHIP = 3328
    BSQR, BQS, BSKR, BKS, BGL, BV = 0, 1024, 2048, 2304, 2560, 2816
    CSQR, CQS, CSKR, CKS, CGL, CV = 0, 4096, 8192, 9216, 10240, 11264
    PoolE = mybir.EngineType.Pool

    def phase_attn(free0_ve, pe_free, dx_x1_stored):
        cc_qkv_base = cnt["CC"]

        snd = ExitStack()
        SA = lambda n, s, d: snd.enter_context(nc.sbuf_tensor(n, s, d))
        axc = SA("axc", [128, NT * BLKR], BF16)
        axn = SA("axn", [128, NT * BLKR], BF16)
        ascr = SA("ascr", [128, 512], F32)
        wnb = SA("wnb", [128, 2 * NT * 512], BF16)
        qnat = SA("qnat", [128, 4 * 2048], BF16)
        rnat = SA("rnat", [128, 4 * 2048], BF16)
        qsn = SA("qsn", [128, 4 * 2048], BF16)
        eqb = SA("eqb", [128, 4 * 2048], BF16)
        rsc = SA("rsc", [128, 4096], BF16)
        srec = SA("srec", [128, 128], F32)
        knat = SA("knat", [128, 4 * 512], BF16)
        krn = SA("krn", [128, 4 * 512], BF16)
        ksn = SA("ksn", [128, 4 * 512], BF16)
        ekb = SA("ekb", [128, 4 * 512], BF16)
        ktrb = SA("ktrb", [128, 4 * 512], BF16)
        glb = SA("glb", [128, 4 * 512], BF16)
        dsc = SA("dsc", [128, 2048], F32)
        shs = SA("shs", [128, 4 * 512], BF16)
        AF_ = {"xc": axc, "xcb": axc, "xsq": axn, "xn": axn, "scr": ascr}
        qrecv = SA("qrecv", [128, 80], F32)

        for c in range(NB):
            tag = f"at_c{c}"
            prev_apply = W.get(f"at_c{c-1}_apply", free0_ve)
            prev_done = W.get(f"at_c{c-1}_shipdone", 0)

            def _(g, c=c, prev_apply=prev_apply, prev_done=prev_done):
                g.wait_ge(sems["dS"], dx_x1_stored[c])
                g.wait_ge(sems["VE"], prev_apply)
                if c == 0:
                    g.wait_ge(sems["PE"], pe_free)
                if prev_done:
                    g.wait_ge(sems["dX"], prev_done)
                ins = g.dma_start(
                    out=axc[:].rearrange("p (dt r) -> p dt r", dt=NT),
                    in_=x1T.rearrange("(dt p) r -> p dt r", p=128)[:, :, ds(c * BLKR, BLKR)],
                )
                ins.then_inc(sems["dG"], 16)
                W[f"at_c{c}_xc"] = bump("dG", 16)
            blk.gpsimd(_)

            emit_ln(tag, AF_, prev_apply, pre_bf16=True, xc_sem="dG")

            pe_proj_base = cnt["PE"]
            ac_ev_base = cnt["AC"]
            dg_wn = []

            def _(g, pe_proj_base=pe_proj_base):
                for og in range(6):
                    if og >= 2:
                        g.wait_ge(sems["PE"], pe_proj_base + (og - 1) * 4)
                    ins = g.dma_start(
                        out=wnb[:, (og % 2) * 8192:(og % 2) * 8192 + 8192].rearrange("p (dt c) -> p dt c", dt=NT),
                        in_=wqkv_in[og].rearrange("dt p c -> p dt c"),
                    )
                    ins.then_inc(sems["dG"], 16)
                    dg_wn.append(bump("dG", 16))
            blk.gpsimd(_)

            def _(t, tag=tag, ac_ev_base=ac_ev_base):
                t.wait_ge(sems["VE"], W[tag + "_apply"])
                for og in range(6):
                    t.wait_ge(sems["dG"], dg_wn[og])
                    for rs in range(4):
                        i = og * 4 + rs
                        if i >= 2:
                            t.wait_ge(sems["AC"], ac_ev_base + i - 1)
                        for dt in range(NT):
                            ins = t.matmul(pb[i % 2][:],
                                           axn[:, dt * 512 + rs * 128: dt * 512 + (rs + 1) * 128],
                                           wnb[:, (og % 2) * 8192 + dt * 512:(og % 2) * 8192 + (dt + 1) * 512],
                                           start=(dt == 0), stop=(dt == NT - 1))
                        ins.then_inc(sems["PE"], 1)
                        bump("PE")
            blk.tensor(_)

            dx_vship_base = cnt["dX"]
            ac_evs = []

            def _(sc, pe_proj_base=pe_proj_base, dx_vship_base=dx_vship_base):
                for og in range(6):
                    for rs in range(4):
                        i = og * 4 + rs
                        sc.wait_ge(sems["PE"], pe_proj_base + i + 1)
                        if og < 4:
                            ins = sc.copy(qnat[:, rs * 2048 + og * 512: rs * 2048 + (og + 1) * 512], pb[i % 2][:])
                        elif og == 4:
                            ins = sc.copy(knat[:, rs * 512:(rs + 1) * 512], pb[i % 2][:])
                        else:
                            if rs >= 2:
                                sc.wait_ge(sems["dX"], dx_vship_base + (rs - 1) * 16)
                            ins = sc.copy(shs[:, (rs % 2) * 512:(rs % 2) * 512 + 512], pb[i % 2][:])
                        ins.then_inc(sems["AC"], 1)
                        ac_evs.append(bump("AC"))
            blk.scalar(_)

            def _(sync, c=c):
                for rs in range(4):
                    sync.wait_ge(sems["AC"], ac_evs[20 + rs])
                    for b_ in range(2):
                        row = b_ * BLK_SHIP + BV + rs * 128
                        ins = sync.dma_start(
                            out=qkv_in[c][row: row + 128, 0:256],
                            in_=shs[:, (rs % 2) * 512 + b_ * 256:(rs % 2) * 512 + b_ * 256 + 256])
                        ins.then_inc(sems["dX"], 16)
                        bump("dX", 16)
                        ins = sync.dma_start(
                            out=qkv_in[c][row: row + 128, 256:512],
                            in_=shs[:, (rs % 2) * 512 + (1 - b_) * 256:(rs % 2) * 512 + (1 - b_) * 256 + 256])
                        ins.then_inc(sems["dX"], 16)
                        bump("dX", 16)
            blk.sync(_)

            def _(sc):
                sc.wait_ge(sems["AC"], ac_evs[15])
                ins = sc.activation(eqb[:], qnat[:], AF.Exp)
                ins.then_inc(sems["AC"], 1)
                W[tag + "_eq"] = bump("AC")
                sc.wait_ge(sems["AC"], ac_evs[19])
                ins = sc.activation(ekb[:], knat[:], AF.Exp)
                ins.then_inc(sems["AC"], 1)
                W[tag + "_ek"] = bump("AC")
            blk.scalar(_)

            def _(v, c=c, tag=tag):
                v.wait_ge(sems["AC"], ac_evs[19])
                for rs in range(4):
                    tile = c * 4 + rs
                    cosNq = trig[:, tile * 64: tile * 64 + 64].rearrange("p (o d) -> p o d", o=1).broadcast_to([128, 16, 64])
                    sinNq = trig[:, 2048 + tile * 64: 2048 + tile * 64 + 64].rearrange("p (o d) -> p o d", o=1).broadcast_to([128, 16, 64])
                    qv = qnat[:, rs * 2048:(rs + 1) * 2048].rearrange("p (h d) -> p h d", d=128)
                    rv = rnat[:, rs * 2048:(rs + 1) * 2048].rearrange("p (h d) -> p h d", d=128)
                    s0 = rsc[:, 0:1024].rearrange("p (h d) -> p h d", d=64)
                    s1 = rsc[:, 1024:2048].rearrange("p (h d) -> p h d", d=64)
                    s2 = rsc[:, 2048:3072].rearrange("p (h d) -> p h d", d=64)
                    s3 = rsc[:, 3072:4096].rearrange("p (h d) -> p h d", d=64)
                    v.tensor_mul(s0, qv[:, :, 0:64], cosNq)
                    v.tensor_mul(s1, qv[:, :, 64:128], sinNq)
                    v.tensor_mul(s2, qv[:, :, 64:128], cosNq)
                    ins = v.tensor_mul(s3, qv[:, :, 0:64], sinNq)
                    ins.then_inc(sems["VE"], 1)
                    v.wait_ge(sems["VE"], bump("VE"))
                    v.tensor_sub(rv[:, :, 0:64], s0, s1)
                    v.tensor_add(rv[:, :, 64:128], s2, s3)
                    cosNk = trig[:, tile * 64: tile * 64 + 64].rearrange("p (o d) -> p o d", o=1).broadcast_to([128, 4, 64])
                    sinNk = trig[:, 2048 + tile * 64: 2048 + tile * 64 + 64].rearrange("p (o d) -> p o d", o=1).broadcast_to([128, 4, 64])
                    kv_ = knat[:, rs * 512:(rs + 1) * 512].rearrange("p (h d) -> p h d", d=128)
                    kr = krn[:, rs * 512:(rs + 1) * 512].rearrange("p (h d) -> p h d", d=128)
                    k0 = rsc[:, 0:256].rearrange("p (h d) -> p h d", d=64)
                    k1 = rsc[:, 256:512].rearrange("p (h d) -> p h d", d=64)
                    k2 = rsc[:, 512:768].rearrange("p (h d) -> p h d", d=64)
                    k3 = rsc[:, 768:1024].rearrange("p (h d) -> p h d", d=64)
                    v.tensor_mul(k0, kv_[:, :, 0:64], cosNk)
                    v.tensor_mul(k1, kv_[:, :, 64:128], sinNk)
                    v.tensor_mul(k2, kv_[:, :, 64:128], cosNk)
                    ins = v.tensor_mul(k3, kv_[:, :, 0:64], sinNk)
                    ins.then_inc(sems["VE"], 1)
                    v.wait_ge(sems["VE"], bump("VE"))
                    v.tensor_sub(kr[:, :, 0:64], k0, k1)
                    ins = v.tensor_add(kr[:, :, 64:128], k2, k3)
                    ins.then_inc(sems["VE"], 1)
                    bump("VE")
                v.wait_ge(sems["AC"], W[tag + "_ek"])
                for rs in range(4):
                    v.reduce_sum(srec[:, rs * 16:(rs + 1) * 16],
                                 eqb[:, rs * 2048:(rs + 1) * 2048].rearrange("p (h d) -> p h d", d=128), AX.X)
                for rs in range(4):
                    ins = v.reduce_sum(srec[:, 64 + rs * 4:64 + (rs + 1) * 4],
                                       ekb[:, rs * 512:(rs + 1) * 512].rearrange("p (h d) -> p h d", d=128), AX.X)
                ins.then_inc(sems["VE"], 1)
                v.wait_ge(sems["VE"], bump("VE"))
                ins = v.reciprocal(qrecv[:], srec[:, 0:80])
                ins.then_inc(sems["VE"], 1)
                v.wait_ge(sems["VE"], bump("VE"))
                for rs in range(4):
                    v.tensor_mul(qsn[:, rs * 2048:(rs + 1) * 2048].rearrange("p (h d) -> p h d", d=128),
                                 eqb[:, rs * 2048:(rs + 1) * 2048].rearrange("p (h d) -> p h d", d=128),
                                 qrecv[:, rs * 16:(rs + 1) * 16].rearrange("p (h o) -> p h o", o=1).broadcast_to([128, 16, 128]))
                for rs in range(4):
                    ins = v.tensor_mul(ksn[:, rs * 512:(rs + 1) * 512].rearrange("p (h d) -> p h d", d=128),
                                       ekb[:, rs * 512:(rs + 1) * 512].rearrange("p (h d) -> p h d", d=128),
                                       qrecv[:, 64 + rs * 4:64 + (rs + 1) * 4].rearrange("p (h o) -> p h o", o=1).broadcast_to([128, 4, 128]))
                ins.then_inc(sems["VE"], 1)
                W[tag + "_derive"] = bump("VE")
            blk.vector(_)

            tlist = [(rnat, BSQR, 16, 2048), (qsn, BQS, 16, 2048),
                     (krn, BSKR, 4, 512), (ksn, BKS, 4, 512), (knat, -1, 4, 512)]
            pe_tr_base = cnt["PE"]
            ac_tr_base = cnt["AC"]
            dx_sh_base = cnt["dX"]

            def _(t, tag=tag, ac_tr_base=ac_tr_base):
                t.wait_ge(sems["VE"], W[tag + "_derive"])
                gi = 0
                for (buf, shiprow, nodt, w) in tlist:
                    for odt in range(nodt):
                        if gi >= 2:
                            t.wait_ge(sems["AC"], ac_tr_base + gi - 1)
                        for rs in range(4):
                            ins = t.matmul(pb[2 + gi % 2][:, rs * 128:(rs + 1) * 128],
                                           buf[:, rs * w + odt * 128: rs * w + (odt + 1) * 128],
                                           IDENT, start=(rs == 0), stop=(rs == 3))
                        ins.then_inc(sems["PE"], 1)
                        bump("PE")
                        gi += 1
            blk.tensor(_)

            def _(sc, tag=tag, pe_tr_base=pe_tr_base, dx_sh_base=dx_sh_base):
                gi = 0
                si = 0
                for (buf, shiprow, nodt, w) in tlist:
                    for odt in range(nodt):
                        sc.wait_ge(sems["PE"], pe_tr_base + gi + 1)
                        if shiprow >= 0:
                            if si < 4:
                                sc.wait_ge(sems["dX"], dx_sh_base)
                            else:
                                sc.wait_ge(sems["dX"], dx_sh_base + (si - 3) * 16)
                            ins = sc.copy(shs[:, (si % 4) * 512:(si % 4) * 512 + 512], pb[2 + gi % 2][:])
                            si += 1
                        else:
                            ins = sc.copy(ktrb[:, odt * 512:(odt + 1) * 512], pb[2 + gi % 2][:])
                        ins.then_inc(sems["AC"], 1)
                        bump("AC")
                        gi += 1
            blk.scalar(_)

            def _(sync, c=c, ac_tr_base=ac_tr_base):
                gi = 0
                si = 0
                for (buf, shiprow, nodt, w) in tlist:
                    for odt in range(nodt):
                        if shiprow >= 0:
                            sync.wait_ge(sems["AC"], ac_tr_base + gi + 1)
                            b_ = odt // (nodt // 2)
                            lo = odt % (nodt // 2)
                            row = b_ * BLK_SHIP + shiprow + lo * 128
                            ins = sync.dma_start(
                                out=qkv_in[c][row: row + 128, :],
                                in_=shs[:, (si % 4) * 512:(si % 4) * 512 + 512])
                            ins.then_inc(sems["dX"], 16)
                            bump("dX", 16)
                            si += 1
                        gi += 1
            blk.sync(_)
            ac_ktr_done = cnt["AC"]

            ve_gl0 = cnt["VE"] + 1

            def _(sc, ac_ktr_done=ac_ktr_done, ve_gl0=ve_gl0):
                sc.wait_ge(sems["AC"], ac_ktr_done)
                for odt in range(4):
                    if odt >= 1:
                        sc.wait_ge(sems["VE"], ve_gl0 + odt - 1)
                    sc.activation(dsc[:, 0:512], ktrb[:, odt * 512:(odt + 1) * 512], AF.Abs)
                    sc.activation(dsc[:, 512:1024], dsc[:, 0:512], AF.Exp, scale=-1.0)
                    ins = sc.activation(dsc[:, 0:512], dsc[:, 512:1024], AF.Ln, bias=1.0)
                    ins.then_inc(sems["AC"], 1)
                    bump("AC")
            blk.scalar(_)
            ac_gl_base = ac_ktr_done

            def _(v, tag=tag, ac_gl_base=ac_gl_base):
                for odt in range(4):
                    v.wait_ge(sems["AC"], ac_gl_base + odt + 1)
                    v.tensor_scalar_min(dsc[:, 1024:1536], ktrb[:, odt * 512:(odt + 1) * 512], 0.0)
                    ins = v.tensor_sub(glb[:, odt * 512:(odt + 1) * 512], dsc[:, 1024:1536], dsc[:, 0:512])
                    ins.then_inc(sems["VE"], 1)
                    bump("VE")
            blk.vector(_)
            ve_gl_base = cnt["VE"] - 4

            def _(sync, c=c, ve_gl_base=ve_gl_base):
                for odt in range(4):
                    sync.wait_ge(sems["VE"], ve_gl_base + odt + 1)
                    b_ = odt // 2
                    row = b_ * BLK_SHIP + BGL + (odt % 2) * 128
                    ins = sync.dma_start(out=qkv_in[c][row: row + 128, :],
                                         in_=glb[:, odt * 512:(odt + 1) * 512])
                    ins.then_inc(sems["dX"], 16)
                    bump("dX", 16)
            blk.sync(_)
            W[tag + "_shipdone"] = cnt["dX"]

            def _(g, c=c, tag=tag):
                g.wait_ge(sems["dX"], W[tag + "_shipdone"])
                g.wait_ge(sems["CC"], cnt["CC"])
                ins = g.collective_compute("AllGather", ALU.bypass,
                                           replica_groups=[[0, 1], [2, 3], [4, 5], [6, 7]],
                                           ins=[qkv_in[c]], outs=[qkv_out[c]])
                ins.then_inc(sems["CC"], 1)
                bump("CC")
            blk.gpsimd(_)

        snd_pe_end = cnt["PE"]
        snd_ve_end = cnt["VE"]
        snd_ac_end = cnt["AC"]
        snd.close()

        rcv = ExitStack()
        RA = lambda n, s, d: rcv.enter_context(nc.sbuf_tensor(n, s, d))
        stg = RA("stg", [128, 26 * 512], BF16)
        skrT = RA("skrT", [128, KVH * SEQ], BF16)
        vful = RA("vful", [128, KVH * 32 * 128], BF16)
        Acum = RA("Acum", [128, KVH * 512], F32)
        eAb = RA("eAb", [128, KVH * 512], F32)
        eAn = RA("eAn", [128, KVH * 512], F32)
        kt_ = RA("kt_", [128, KVH * 512], BF16)
        ktn = RA("ktn", [128, KVH * 128], BF16)
        zf = RA("zf", [128, 128], BF16)
        Sst = RA("Sst", [128, KVH * 128], F32)
        Ssc = RA("Ssc", [128, KVH * 128], F32)
        Sb = RA("Sb", [128, KVH * 128], BF16)
        pgb = RA("pgb", [128, 128], BF16)
        o05 = RA("o05", [128, H * 512], BF16)
        oTb = RA("oTb", [128, H * 512], BF16)
        Pring = RA("Pring", [128, 2 * 1024], BF16)
        hdall = RA("hdall", [1, H * 512], BF16)
        hdb = RA("hdb", [1, H * 512], BF16)
        yraw = RA("yraw", [128, H * 512], BF16)
        qt_ = yraw
        yscr = RA("yscr", [128, 512], F32)
        ost = RA("ost", [128, 16 * 512], BF16)
        wob = RA("wob", [128, 2 * 2048], BF16)
        x1r = RA("x1r", [128, 2 * 512], F32)
        x2o = RA("x2o", [128, 2 * 512], F32)

        def _(v):
            v.wait_ge(sems["VE"], snd_ve_end)
            v.wait_ge(sems["PE"], snd_pe_end)
            v.wait_ge(sems["AC"], snd_ac_end)
            v.memset(zf[:], 0.0)
            ins = v.memset(Sst[:], 0.0)
            ins.then_inc(sems["VE"], 1)
            W["rcv_init"] = bump("VE")
        blk.vector(_)

        dg0 = cnt["dG"]
        for g_ in range(NG):
            W[f"g{g_}_staged"] = dg0 + (g_ + 1) * 16
        dg_p3 = dg0 + NG * 16
        for c_ in range(NB):
            W[f"wo_c{c_}_ost"] = dg_p3 + (c_ * 17 + 1) * 16
            for dmt_ in range(16):
                W[f"wo_c{c_}_wo{dmt_}"] = dg_p3 + (c_ * 17 + 1 + dmt_ + 1) * 16
        out20 = cnt["OUT"]
        for c_ in range(NB):
            for dmt_ in range(16):
                W[f"wo_c{c_}_st{dmt_}"] = out20 + (c_ * 16 + dmt_ + 1) * 16

        g_pe_end = {-1: snd_pe_end}
        g_ve_end = {-1: W["rcv_init"]}
        g_ac_end = {-1: snd_ac_end}
        g_ship = {-1: 0}

        for g in range(NG):
            slot, step = g // 4, g % 4
            src2d = qkv_out[step]


            def _(v, g=g):
                v.wait_ge(sems["dG"], W[f"g{g}_staged"])
                v.tensor_copy(skrT[:].rearrange("p (kv r) -> p kv r", kv=KVH)[:, :, g * 512:(g + 1) * 512],
                              stg[:, CSKR: CSKR + 1024].rearrange("p (kv r) -> p kv r", kv=KVH))
                for kv in range(KVH):
                    v.tensor_copy(
                        vful[:, kv * 4096 + g * 512: kv * 4096 + (g + 1) * 512].rearrange("p (rs d) -> p rs d", d=128),
                        stg[:, CV:].rearrange("p (rs c) -> p rs c", c=512)[:, :, kv * 128:(kv + 1) * 128])
                for kv in range(KVH):
                    for j in range(4):
                        ins = v.tensor_tensor_scan(Acum[:, kv * 512 + j * 128: kv * 512 + (j + 1) * 128],
                                                   stg[:, CGL + kv * 512 + j * 128: CGL + kv * 512 + (j + 1) * 128],
                                                   zf[:], 0.0, ALU.add, ALU.add)
                ins.then_inc(sems["VE"], 1)
                W[f"g{g}_scan"] = bump("VE")
            blk.vector(_)

            def _(sc, g=g):
                sc.wait_ge(sems["VE"], W[f"g{g}_scan"])
                sc.activation(eAb[:], Acum[:], AF.Exp, scale=1.0 / 16.0)
                ins = sc.activation(eAn[:], Acum[:], AF.Exp, scale=-1.0 / 16.0)
                ins.then_inc(sems["AC"], 1)
                W[f"g{g}_eA"] = bump("AC")
            blk.scalar(_)

            def _(v, g=g):
                v.wait_ge(sems["AC"], W[f"g{g}_eA"])
                for h in range(H):
                    v.tensor_mul(qt_[:, h * 512:(h + 1) * 512], stg[:, CQS + h * 512: CQS + (h + 1) * 512],
                                 eAb[:, (h // 4) * 512:(h // 4 + 1) * 512])
                for kv in range(KVH):
                    ins = v.tensor_mul(kt_[:, kv * 512:(kv + 1) * 512], stg[:, CKS + kv * 512: CKS + (kv + 1) * 512],
                                       eAn[:, kv * 512:(kv + 1) * 512])
                ins.then_inc(sems["VE"], 1)
                W[f"g{g}_qtkt"] = bump("VE")
            blk.vector(_)

            for j in range(4):
                jt = f"g{g}j{j}"
                def _(sc, g=g, j=j, jt=jt):
                    if j > 0:
                        sc.wait_ge(sems["VE"], W[f"g{g}j{j-1}_supd"])
                    elif g > 0:
                        sc.wait_ge(sems["VE"], W[f"g{g-1}j3_supd"])
                    else:
                        sc.wait_ge(sems["VE"], W["rcv_init"])
                    if g == 0 and j == 0:
                        pass
                    sc.wait_ge(sems["VE"], W[f"g{g}_qtkt"])
                    if j == 0 and g > 0:
                        sc.wait_ge(sems["VE"], W[f"g{g-1}_oT"])
                    ins = sc.copy(Sb[:], Sst[:])
                    ins.then_inc(sems["AC"], 1)
                    W[jt + "_sb"] = bump("AC")
                blk.scalar(_)

                ac_ktn0 = cnt["AC"] + 1

                def _(t, g=g, j=j, jt=jt, ac_ktn0=ac_ktn0):
                    t.wait_ge(sems["VE"], W[f"g{g}_qtkt"])
                    if j > 0:
                        t.wait_ge(sems["VE"], W[f"g{g}j{j-1}_supd"])
                    elif g > 0:
                        t.wait_ge(sems["VE"], W[f"g{g-1}j3_supd"])
                    for kv in range(KVH):
                        if kv == 1:
                            t.wait_ge(sems["AC"], ac_ktn0)
                        ins = t.matmul(pb[6][:, 0:128],
                                       kt_[:, kv * 512 + j * 128: kv * 512 + (j + 1) * 128],
                                       IDENT, start=True, stop=True)
                        ins.then_inc(sems["PE"], 1)
                        W[jt + f"_ktr{kv}"] = bump("PE")
                blk.tensor(_)

                def _(sc, jt=jt):
                    for kv in range(KVH):
                        sc.wait_ge(sems["PE"], W[jt + f"_ktr{kv}"])
                        ins = sc.copy(ktn[:, kv * 128:(kv + 1) * 128], pb[6][:, 0:128])
                        ins.then_inc(sems["AC"], 1)
                        W[jt + f"_ktn{kv}"] = bump("AC")
                blk.scalar(_)

                def _(t, g=g, j=j, jt=jt):
                    t.wait_ge(sems["AC"], W[jt + "_ktn1"])
                    for kv in range(KVH):
                        ins = t.matmul(pb[6][:, 128 + kv * 128: 128 + (kv + 1) * 128],
                                       ktn[:, kv * 128:(kv + 1) * 128],
                                       vful[:, kv * 4096 + (g * 4 + j) * 128: kv * 4096 + (g * 4 + j) * 128 + 128],
                                       start=True, stop=True)
                        ins.then_inc(sems["PE"], 1)
                        W[jt + f"_u0{kv}"] = bump("PE")
                blk.tensor(_)

                for h in range(H):
                    kv = h // 4
                    def _(t, g=g, j=j, h=h, kv=kv, jt=jt):
                        if h >= 2:
                            t.wait_ge(sems["VE"], W[jt + f"_pg{h-2}"])
                        ins = t.matmul(pb[4 + h % 2][:, 0:128],
                                       kt_[:, kv * 512 + j * 128: kv * 512 + (j + 1) * 128],
                                       qt_[:, h * 512 + j * 128: h * 512 + (j + 1) * 128],
                                       start=True, stop=True)
                        ins.then_inc(sems["PE"], 1)
                        W[jt + f"_pgm{h}"] = bump("PE")
                    blk.tensor(_)

                    def _(v, h=h, jt=jt):
                        v.wait_ge(sems["PE"], W[jt + f"_pgm{h}"])
                        if h >= 1:
                            v.wait_ge(sems["PE"], W[jt + f"_om{h-1}"])
                        ins = v.tensor_mul(pgb[:], pb[4 + h % 2][:, 0:128], MASKGLA)
                        ins.then_inc(sems["VE"], 1)
                        W[jt + f"_pg{h}"] = bump("VE")
                    blk.vector(_)

                    def _(t, g=g, j=j, h=h, kv=kv, jt=jt):
                        t.wait_ge(sems["VE"], W[jt + f"_pg{h}"])
                        t.wait_ge(sems["AC"], W[jt + "_sb"])
                        if h >= 2:
                            t.wait_ge(sems["AC"], W[jt + f"_oev{h-2}"])
                        t.matmul(pb[2 + h % 2][:, 0:128],
                                 vful[:, kv * 4096 + (g * 4 + j) * 128: kv * 4096 + (g * 4 + j) * 128 + 128],
                                 pgb[:], start=True, stop=False)
                        ins = t.matmul(pb[2 + h % 2][:, 0:128],
                                       Sb[:, kv * 128:(kv + 1) * 128],
                                       qt_[:, h * 512 + j * 128: h * 512 + (j + 1) * 128],
                                       start=False, stop=True)
                        ins.then_inc(sems["PE"], 1)
                        W[jt + f"_om{h}"] = bump("PE")
                    blk.tensor(_)

                    def _(sc, g=g, j=j, h=h, jt=jt):
                        sc.wait_ge(sems["PE"], W[jt + f"_om{h}"])
                        ins = sc.mul(o05[:, h * 512 + j * 128: h * 512 + (j + 1) * 128],
                                     pb[2 + h % 2][:, 0:128], 0.5)
                        ins.then_inc(sems["AC"], 1)
                        W[jt + f"_oev{h}"] = bump("AC")
                    blk.scalar(_)

                def _(v, g=g, j=j, jt=jt):
                    v.wait_ge(sems["PE"], W[jt + "_u01"])
                    v.wait_ge(sems["AC"], W[jt + "_sb"])
                    for kv in range(KVH):
                        eac = eAb[:, kv * 512 + j * 128 + 127: kv * 512 + j * 128 + 128]
                        v.tensor_add(Ssc[:, kv * 128:(kv + 1) * 128], Sst[:, kv * 128:(kv + 1) * 128],
                                     pb[6][:, 128 + kv * 128: 128 + (kv + 1) * 128])
                        ins = v.tensor_scalar(Sst[:, kv * 128:(kv + 1) * 128], Ssc[:, kv * 128:(kv + 1) * 128],
                                              eac, None, ALU.mult)
                        ins.then_inc(sems["VE"], 1)
                        v.wait_ge(sems["VE"], bump("VE"))
                    W[f"g{g}j{j}_supd"] = cnt["VE"]
                blk.vector(_)

            K = 4 * g + 4
            KP = K // 2
            for h in range(H):
                kv = h // 4
                ht = f"g{g}h{h}"
                pe0 = cnt["PE"]
                ac0 = cnt["AC"]
                ve0 = cnt["VE"]
                order = []
                for kt in range(min(4, K)):
                    order.append(("sc", kt))
                for p in range(2, KP):
                    order.append(("av", 2 * p - 4))
                    order.append(("av", 2 * p - 3))
                    order.append(("sc", 2 * p))
                    order.append(("sc", 2 * p + 1))
                order += [("av", K - 4), ("av", K - 3), ("av", K - 2), ("av", K - 1)] if K >= 4 else []
                POS = {it: i + 1 for i, it in enumerate(order)}
                pe_sc_f = lambda kt, pe0=pe0, POS=POS: pe0 + POS[("sc", kt)]
                pe_av_f = lambda kt, pe0=pe0, POS=POS: pe0 + POS[("av", kt)]
                ac_ex_f = lambda p, ac0=ac0: ac0 + p + 1
                ve_mk_f = lambda kt, g=g, ve0=ve0: ve0 + (kt - 4 * g) + 1

                def _(t, g=g, h=h, kv=kv, K=K, ht=ht, order=order):
                    if h == 0:
                        t.wait_ge(sems["AC"], W[f"g{g}j3_oev{H-1}"])
                        t.wait_ge(sems["VE"], W[f"g{g}j3_pg{H-1}"])
                    else:
                        t.wait_ge(sems["VE"], W[f"g{g}h{h-1}_dev"])
                        t.wait_ge(sems["AC"], W[f"g{g}h{h-1}_yev"])
                    for (kind, kt) in order:
                        p = kt // 2
                        if kind == "sc":
                            if p >= 2:
                                t.wait_ge(sems["AC"], ac_ex_f(p - 2))
                            bankpair = pbA if p % 2 == 0 else pbB
                            ins = t.matmul(bankpair[:, (kt % 2) * 512:(kt % 2) * 512 + 512],
                                           skrT[:, kv * SEQ + kt * 128: kv * SEQ + (kt + 1) * 128],
                                           stg[:, CSQR + h * 512: CSQR + (h + 1) * 512],
                                           start=True, stop=True)
                            ins.then_inc(sems["PE"], 1)
                            assert bump("PE") == pe_sc_f(kt)
                        else:
                            if kt >= 4 * g:
                                t.wait_ge(sems["VE"], ve_mk_f(kt))
                            else:
                                t.wait_ge(sems["AC"], ac_ex_f(p))
                            slot = (p % 2) * 1024 + (kt % 2) * 512
                            t.matmul(pb[2][:],
                                     vful[:, kv * 4096 + kt * 128: kv * 4096 + (kt + 1) * 128],
                                     Pring[:, slot: slot + 512],
                                     start=(kt == 0), stop=(kt == K - 1))
                            ins = t.matmul(pb[3][0:1, :], ONES_B,
                                           Pring[:, slot: slot + 512],
                                           start=(kt == 0), stop=(kt == K - 1))
                            ins.then_inc(sems["PE"], 1)
                            assert bump("PE") == pe_av_f(kt)
                blk.tensor(_)

                def _(sc, g=g, h=h, K=K, KP=KP, ht=ht):
                    for p in range(KP):
                        sc.wait_ge(sems["PE"], pe_sc_f(2 * p + 1))
                        if p >= 2:
                            sc.wait_ge(sems["PE"], pe_av_f(2 * (p - 2) + 1))
                        bankpair = pbA if p % 2 == 0 else pbB
                        ins = sc.activation(Pring[:, (p % 2) * 1024:(p % 2) * 1024 + 1024],
                                            bankpair[:], AF.Exp, scale=float(HD) ** -0.5)
                        ins.then_inc(sems["AC"], 1)
                        assert bump("AC") == ac_ex_f(p)
                    sc.wait_ge(sems["PE"], pe_av_f(K - 1))
                    ins = sc.copy(yraw[:, h * 512:(h + 1) * 512], pb[2][:])
                    ins.then_inc(sems["AC"], 1)
                    W[ht + "_yev"] = bump("AC")
                blk.scalar(_)

                def _(v, g=g, h=h, K=K, ht=ht):
                    for kt in range(4 * g, K):
                        p = kt // 2
                        slot = (p % 2) * 1024 + (kt % 2) * 512
                        v.wait_ge(sems["AC"], ac_ex_f(p))
                        ins = v.tensor_mul(Pring[:, slot: slot + 512],
                                           Pring[:, slot: slot + 512],
                                           MASKDIAG(kt - 4 * g))
                        ins.then_inc(sems["VE"], 1)
                        assert bump("VE") == ve_mk_f(kt)
                    v.wait_ge(sems["PE"], pe_av_f(K - 1))
                    ins = v.tensor_copy(hdall[0:1, h * 512:(h + 1) * 512], pb[3][0:1, :])
                    ins.then_inc(sems["VE"], 1)
                    W[ht + "_dev"] = bump("VE")
                blk.vector(_)

            def _(v, g=g):
                v.wait_ge(sems["AC"], W[f"g{g}h{H-1}_yev"])
                v.wait_ge(sems["VE"], W[f"g{g}h{H-1}_dev"])
                with nc.allow_low_precision(reason="bf16 softmax denominator reciprocal"):
                    ins = v.reciprocal(hdb[0:1, :], hdall[0:1, :])
                ins.then_inc(sems["VE"], 1)
                W[f"g{g}_hdb"] = bump("VE")
            blk.vector(_)

            ve_nm0 = cnt["VE"]

            def _(t, g=g, ve_nm0=ve_nm0):
                t.wait_ge(sems["VE"], W[f"g{g}_hdb"])
                for h in range(H):
                    if h >= 2:
                        t.wait_ge(sems["VE"], ve_nm0 + 2 * (h - 1))
                    ins = t.matmul(pb[6 + h % 2][:], HALFROW_B, hdb[0:1, h * 512:(h + 1) * 512],
                                   start=True, stop=True)
                    ins.then_inc(sems["PE"], 1)
                    W[f"g{g}nb{h}"] = bump("PE")
            blk.tensor(_)

            def _(v, g=g, ve_nm0=ve_nm0):
                for h in range(H):
                    v.wait_ge(sems["PE"], W[f"g{g}nb{h}"])
                    if g > 0 and h == 0:
                        v.wait_ge(sems["dX"], g_ship[g - 1])
                    ins = v.tensor_mul(yscr[:], yraw[:, h * 512:(h + 1) * 512], pb[6 + h % 2][:])
                    ins.then_inc(sems["VE"], 1)
                    bump("VE")
                    ins = v.tensor_add(oTb[:, h * 512:(h + 1) * 512], yscr[:],
                                       o05[:, h * 512:(h + 1) * 512])
                    ins.then_inc(sems["VE"], 1)
                    bump("VE")
            blk.vector(_)
            W[f"g{g}_oT"] = cnt["VE"]

            def _(sync, g=g):
                sync.wait_ge(sems["VE"], W[f"g{g}_oT"])
                ins = sync.dma_start(out=o_in[g].rearrange("(h p) r -> p h r", p=128),
                                     in_=oTb[:].rearrange("p (h r) -> p h r", h=H))
                ins.then_inc(sems["dX"], 16)
                g_ship[g] = bump("dX", 16)
            blk.sync(_)


            g_pe_end[g] = cnt["PE"]
            g_ve_end[g] = cnt["VE"]
            g_ac_end[g] = cnt["AC"]

        o_cc_base = cnt["CC"]

        o_out_flat = o_out.rearrange("g row c -> (g row) c")
        ve_p3 = cnt["VE"]
        pe_p3 = cnt["PE"]
        for c_ in range(NB):
            for dmt_ in range(16):
                W[f"wo_c{c_}_ev{dmt_}"] = ve_p3 + c_ * 16 + dmt_ + 1
                W[f"wo_c{c_}_m{dmt_}"] = pe_p3 + c_ * 16 + dmt_ + 1
            W[f"wo_c{c_}_pe_end"] = pe_p3 + (c_ + 1) * 16
        for c in range(NB):
            wt = f"wo_c{c}"

            def _(sync, c=c, wt=wt):
                for dmt in range(16):
                    if dmt >= 2:
                        sync.wait_ge(sems["VE"], W[wt + f"_ev{dmt-2}"])
                    elif c > 0:
                        sync.wait_ge(sems["VE"], W[f"wo_c{c-1}_ev{dmt+14}"])
                    ins = sync.dma_start(
                        out=x1r[:, (dmt % 2) * 512:(dmt % 2) * 512 + 512].rearrange("p (o r) -> p o r", o=1),
                        in_=x1T.rearrange("(dt p) r -> p dt r", p=128)[:, dmt:dmt + 1, c * 512:(c + 1) * 512])
                    ins.then_inc(sems["dX"], 16)
                    W[wt + f"_x1{dmt}"] = bump("dX", 16)
            blk.sync(_)

            def _(t, c=c, wt=wt):
                t.wait_ge(sems["dG"], W[wt + "_ost"])
                if c == 0:
                    t.wait_ge(sems["AC"], g_ac_end[NG - 1])
                    t.wait_ge(sems["VE"], g_ve_end[NG - 1])
                for dmt in range(16):
                    t.wait_ge(sems["dG"], W[wt + f"_wo{dmt}"])
                    if dmt >= 2:
                        t.wait_ge(sems["VE"], W[wt + f"_ev{dmt-2}"])
                    for ot in range(16):
                        ins = t.matmul(pb[dmt % 2][:],
                                       wob[:, (dmt % 2) * 2048 + ot * 128:(dmt % 2) * 2048 + (ot + 1) * 128],
                                       ost[:, ot * 512:(ot + 1) * 512],
                                       start=(ot == 0), stop=(ot == 15))
                    ins.then_inc(sems["PE"], 1)
                    assert bump("PE") == W[wt + f"_m{dmt}"]
                assert cnt["PE"] == W[wt + "_pe_end"]
            blk.tensor(_)

            def _(v, c=c, wt=wt):
                for dmt in range(16):
                    v.wait_ge(sems["PE"], W[wt + f"_m{dmt}"])
                    v.wait_ge(sems["dX"], W[wt + f"_x1{dmt}"])
                    if dmt >= 2:
                        v.wait_ge(sems["OUT"], W[wt + f"_st{dmt-2}"])
                    elif c > 0:
                        v.wait_ge(sems["OUT"], W[f"wo_c{c-1}_st{dmt+14}"])
                    ins = v.tensor_add(x2o[:, (dmt % 2) * 512:(dmt % 2) * 512 + 512],
                                       pb[dmt % 2][:], x1r[:, (dmt % 2) * 512:(dmt % 2) * 512 + 512])
                    ins.then_inc(sems["VE"], 1)
                    assert bump("VE") == W[wt + f"_ev{dmt}"]
            blk.vector(_)


        qkv_flat = qkv_out.rearrange("s r c -> (s r c)")
        o_flat1 = o_out.rearrange("g row c -> (g row c)")

        def _(gp):
            gp.wait_ge(sems["dX"], W["p_loads"])
            REG = [nc.values_load(offs_s[0:1, i:i + 1], engines=[PoolE],
                                  skip_runtime_bounds_check=True) for i in range(12)]
            for g in range(NG):
                slot, step = g // 4, g % 4
                gp.wait_ge(sems["CC"], cc_qkv_base + step + 1)
                gp.wait_ge(sems["PE"], g_pe_end[g - 1])
                gp.wait_ge(sems["VE"], g_ve_end[g - 1])
                gp.wait_ge(sems["AC"], g_ac_end[g - 1])
                gp.dma_start(
                    out=stg[:].rearrange("p (grp r) -> p grp r", r=512),
                    in_=qkv_flat[ds(REG[g], 26 * 128 * 512)]
                    .rearrange("(grp p r) -> p grp r", p=128, r=512)).then_inc(sems["dG"], 16)
                bump("dG", 16)
                assert cnt["dG"] == W[f"g{g}_staged"]
                if g > 0:
                    gp.wait_ge(sems["dX"], g_ship[g - 1])
                    gp.wait_ge(sems["CC"], cnt["CC"])
                    gp.collective_compute("AllGather", ALU.bypass,
                                          replica_groups=[[0, 1], [2, 3], [4, 5], [6, 7]],
                                          ins=[o_in[g - 1]], outs=[o_out[g - 1]]).then_inc(sems["CC"], 1)
                    assert bump("CC") == o_cc_base + g
            gp.wait_ge(sems["dX"], g_ship[NG - 1])
            gp.wait_ge(sems["CC"], cnt["CC"])
            gp.collective_compute("AllGather", ALU.bypass,
                                  replica_groups=[[0, 1], [2, 3], [4, 5], [6, 7]],
                                  ins=[o_in[NG - 1]], outs=[o_out[NG - 1]]).then_inc(sems["CC"], 1)
            assert bump("CC") == o_cc_base + NG

            for c in range(NB):
                wt = f"wo_c{c}"
                gp.wait_ge(sems["CC"], o_cc_base + min(5 + c, NG))
                if c > 0:
                    gp.wait_ge(sems["PE"], W[f"wo_c{c-1}_pe_end"])
                gp.dma_start(out=ost[:].rearrange("p (ot r) -> p ot r", ot=16),
                             in_=o_flat1[ds(REG[8 + c], 2048 * 512)].rearrange("(ot p r) -> p ot r", p=128, r=512)
                             ).then_inc(sems["dG"], 16)
                assert bump("dG", 16) == W[wt + "_ost"]
                for dmt in range(16):
                    if dmt >= 2:
                        gp.wait_ge(sems["PE"], W[wt + f"_m{dmt-2}"])
                        gp.wait_ge(sems["VE"], W[wt + f"_ev{dmt-2}"])
                        gp.dma_start(
                            out=x2T.rearrange("(dt p) r -> p dt r", p=128)[:, dmt - 2:dmt - 1, c * 512:(c + 1) * 512],
                            in_=x2o[:, (dmt % 2) * 512:(dmt % 2) * 512 + 512].rearrange("p (o r) -> p o r", o=1)
                        ).then_inc(sems["OUT"], 16)
                        assert bump("OUT", 16) == W[wt + f"_st{dmt-2}"]
                    gp.dma_start(out=wob[:, (dmt % 2) * 2048:(dmt % 2) * 2048 + 2048].rearrange("p (ot cc) -> p ot cc", ot=16),
                                 in_=wo_in[dmt].rearrange("ot p cc -> p ot cc")).then_inc(sems["dG"], 16)
                    assert bump("dG", 16) == W[wt + f"_wo{dmt}"]
                for dmt in range(14, 16):
                    gp.wait_ge(sems["VE"], W[wt + f"_ev{dmt}"])
                    gp.dma_start(
                        out=x2T.rearrange("(dt p) r -> p dt r", p=128)[:, dmt:dmt + 1, c * 512:(c + 1) * 512],
                        in_=x2o[:, (dmt % 2) * 512:(dmt % 2) * 512 + 512].rearrange("p (o r) -> p o r", o=1)
                    ).then_inc(sems["OUT"], 16)
                    assert bump("OUT", 16) == W[wt + f"_st{dmt}"]
        blk.gpsimd(_)

        rcv.close()
        return cnt["VE"], max(cnt["dX"], 0), cnt["OUT"]

    with ExitStack() as pes:
        phase_prologue(pes)

    def _(t):
        t.wait_ge(sems["CC"], W["p_wAG0"])
    blk.tensor(_)
    def _(sync):
        sync.wait_ge(sems["CC"], W["p_wAG0"])
    blk.sync(_)

    if N_SUBLAYERS == 1:
        with ExitStack() as fes:
            F = alloc_ffn_bufs(fes, "f1")
            phase_ffn("f1", F, xT_in, out_T, "OUT", free0=0)
        final_out = cnt["OUT"]
    else:
        with ExitStack() as fes:
            F = alloc_ffn_bufs(fes, "f1")
            phase_ffn("f1", F, xT_in, x1T, "dS", free0=0)
        f1_pe_end = cnt["PE"]
        f1_stored = [W[f"f1_b{b}_stored"] for b in range(NB)]
        at_ve_end, at_dx_end, at_out_end = phase_attn(W[f"f1_b{NB-1}_evacall"], f1_pe_end, f1_stored)

        def _(sync, at_out_end=at_out_end):
            sync.wait_ge(sems["OUT"], at_out_end)
        blk.sync(_)

        with ExitStack() as fes:
            F3 = alloc_ffn_bufs(fes, "f3")
            phase_ffn("f3", F3, x2T, out_T, "OUT", free0=at_ve_end, dx_gate=at_dx_end)
        final_out = cnt["OUT"]

    def _(sync, final_out=final_out):
        sync.wait_ge(sems["OUT"], final_out)
    blk.sync(_)

    blk.__exit__(None, None, None)
    es.close()
    return nc



def _host_prep(inputs):
    x = np.ascontiguousarray(np.asarray(inputs["L0_attn_hidden_states"], dtype=np.float32))
    Wq = np.asarray(inputs["L0_attn_Wq"], dtype=np.float32)
    Wk = np.asarray(inputs["L0_attn_Wk"], dtype=np.float32)
    Wv = np.asarray(inputs["L0_attn_Wv"], dtype=np.float32)
    Wo = np.asarray(inputs["L0_attn_Wo"], dtype=np.float32)
    w1 = np.asarray(inputs["L0_ffn_w1"], dtype=np.float32)
    b1 = np.asarray(inputs["L0_ffn_b1"], dtype=np.float32)
    w2 = np.asarray(inputs["L0_ffn_w2"], dtype=np.float32)
    b2 = np.asarray(inputs["L0_ffn_b2"], dtype=np.float32)
    import ml_dtypes
    bf = ml_dtypes.bfloat16

    w1t = np.ascontiguousarray(w1.reshape(FT, 128, NT, 128).transpose(0, 2, 3, 1))
    w2t = np.ascontiguousarray(w2.reshape(16, 128, FT, 128).transpose(0, 2, 3, 1))
    wqkv_cat = np.concatenate([Wq, Wk, Wv], axis=1)
    wqkvN = np.ascontiguousarray(wqkv_cat.reshape(NT, 128, 6, 512).transpose(2, 0, 1, 3))
    woT = np.ascontiguousarray(Wo.reshape(16, 128, 16, 128).transpose(2, 0, 1, 3))

    constf = np.zeros((128, 96), np.float32)
    constf[:, 0] = 1.0
    constf[:, 1:17] = b2.reshape(16, 128).T
    constf[:, 17:81] = b1.reshape(FT, 128).T
    constf[0, 82] = EPS

    constb = np.zeros((128, 5248), np.float32)
    constb[:, 0] = 1.0
    constb[:, 2:130] = np.eye(128)
    s_idx = np.arange(128)[:, None]
    q_idx512 = np.arange(512)[None, :]
    t_idx = np.arange(128)[None, :]
    constb[:, 130:258] = (t_idx >= s_idx).astype(np.float32)
    for j in range(4):
        constb[:, 258 + j * 512: 258 + (j + 1) * 512] = (q_idx512 >= (s_idx + 128 * j)).astype(np.float32)
    constb[0, 2306:2434] = 1.0
    constb[0, 2434:2946] = 1.0
    constb[0, 2946:4994] = 0.5 * b2
    constb[0, 4994:5122] = 0.5
    constb = constb.astype(bf)

    inv_freq = 1.0 / (10000.0 ** (np.arange(0, HD, 2, dtype=np.float64) / HD))
    pos = np.arange(SEQ, dtype=np.float64)
    fr = pos[:, None] * inv_freq[None, :]
    cosn = np.cos(fr).astype(np.float32).reshape(32, 128, 64).transpose(1, 0, 2).reshape(128, 2048)
    sinn = np.sin(fr).astype(np.float32).reshape(32, 128, 64).transpose(1, 0, 2).reshape(128, 2048)
    trig = np.concatenate([cosn, sinn], axis=1).astype(bf)

    per_core = []
    for c in range(8):
        b_, r = c // 2, c % 2
        xT = np.ascontiguousarray(x[b_, r * ROWS:(r + 1) * ROWS, :].T)
        offs = np.zeros((1, 16), np.int32)
        for g_ in range(8):
            slot_, step_ = g_ // 4, g_ % 4
            offs[0, g_] = (step_ * 13312 + slot_ * 6656 + r * 3328) * 512
        for lc in range(4):
            offs[0, 8 + lc] = ((4 * r + lc) * 2048) * 512
        per_core.append({
            "xT_in": xT,
            "w1s": np.ascontiguousarray(w1t[c * 8:(c + 1) * 8]),
            "w2s": np.ascontiguousarray(w2t[c * 2:(c + 1) * 2]),
            "wqkvN": wqkvN,
            "woT": woT,
            "constf": constf,
            "constb": constb,
            "trigN": trig,
            "offs": offs,
        })
    return per_core


def kernel(**inputs):
    per_core = _host_prep(inputs)
    nc = build_nc()
    res = run_bass_kernel_spmd(nc, per_core, core_ids=list(range(8)))
    out = np.empty((4, SEQ, DM), np.float32)
    for c in range(8):
        b_, r = c // 2, c % 2
        out[b_, r * ROWS:(r + 1) * ROWS, :] = res.results[c]["out_T"].T
    return out


# version 05
# speedup vs baseline: 1.0726x; 1.0272x over previous
import numpy as np
from contextlib import ExitStack

import concourse.bass as bass
import concourse.mybir as mybir
from concourse.bass_utils import run_bass_kernel_spmd

F32 = mybir.dt.float32
BF16 = mybir.dt.bfloat16
I32 = mybir.dt.int32
AF = mybir.ActivationFunctionType
ALU = mybir.AluOpType
AX = mybir.AxisListType
ds = bass.ds

DM = 2048
DFF = 8192
NT = DM // 128
FT = DFF // 128
ROWS = 2048
NB = 4
BLKR = 512
SEQ = 4096
NG = 8
H = 8
KVH = 2
HD = 128
EPS = 1e-5

N_SUBLAYERS = 3


def build_nc():
    nc = bass.Bass("TRN2", target_bir_lowering=False, detect_race_conditions=False)

    xT_in = nc.dram_tensor("xT_in", [DM, ROWS], F32, kind="ExternalInput").ap()
    w1s_in = nc.dram_tensor("w1s", [8, NT, 128, 128], F32, kind="ExternalInput").ap()
    w2s_in = nc.dram_tensor("w2s", [2, FT, 128, 128], F32, kind="ExternalInput").ap()
    wqkv_in = nc.dram_tensor("wqkvN", [6, NT, 128, 512], F32, kind="ExternalInput").ap()
    wo_in = nc.dram_tensor("woT", [16, 16, 128, 128], F32, kind="ExternalInput").ap()
    constf_in = nc.dram_tensor("constf", [128, 96], F32, kind="ExternalInput").ap()
    constb_in = nc.dram_tensor("constb", [128, 5248], BF16, kind="ExternalInput").ap()
    trig_in = nc.dram_tensor("trigN", [128, 4096], BF16, kind="ExternalInput").ap()
    offs_in = nc.dram_tensor("offs", [1, 16], I32, kind="ExternalInput").ap()

    out_T = nc.dram_tensor("out_T", [DM, ROWS], F32, kind="ExternalOutput").ap()

    x1T = nc.dram_tensor("x1T", [DM, ROWS], F32, kind="Internal").ap()
    x2T = nc.dram_tensor("x2T", [DM, ROWS], F32, kind="Internal").ap()
    sc_loc = nc.dram_tensor("sc_loc", [128, 128], F32, kind="Internal").ap()
    sc_all = nc.dram_tensor("sc_all", [1024, 128], F32, kind="Internal").ap()
    w1q_loc = nc.dram_tensor("w1q_loc", [8 * NT * 128, 128], BF16, kind="Internal").ap()
    w1q_all = nc.dram_tensor("w1q_all", [FT * NT * 128, 128], BF16, kind="Internal").ap()
    w2q_loc = nc.dram_tensor("w2q_loc", [2 * FT * 128, 128], BF16, kind="Internal").ap()
    w2q_all = nc.dram_tensor("w2q_all", [16 * FT * 128, 128], BF16, kind="Internal").ap()
    qkv_in = nc.dram_tensor("qkv_in", [4, 6656, 512], BF16, kind="Internal").ap()
    qkv_out = nc.dram_tensor("qkv_out", [4, 13312, 512], BF16, kind="Internal").ap()
    o_in = nc.dram_tensor("o_in", [NG, 1024, 512], BF16, kind="Internal").ap()
    o_out = nc.dram_tensor("o_out", [NG, 2048, 512], BF16, kind="Internal").ap()

    es = ExitStack()

    def SBp(name, shape, dt):
        return es.enter_context(nc.sbuf_tensor(name, shape, dt))

    constf = SBp("constf_s", [128, 96], F32)
    constb = SBp("constb_s", [128, 5248], BF16)
    trig = SBp("trig_s", [128, 4096], BF16)
    offs_s = SBp("offs_s", [1, 16], I32)
    onesrow_f = SBp("onesrow_f", [1, 128], F32)
    thr_s = SBp("thr_s", [1, 4], F32)
    thrv = SBp("thrv", [128, 2], F32)
    nthrv = SBp("nthrv", [128, 2], F32)
    stat = SBp("stat", [1, 2048], F32)
    statb = SBp("statb", [1, 1024], BF16)

    pbA = es.enter_context(nc.psum_tensor("pbA", [128, 1024], F32))
    pb2_t = es.enter_context(nc.psum_tensor("pb2", [128, 512], F32))
    pb3_t = es.enter_context(nc.psum_tensor("pb3", [128, 512], F32))
    pbB = es.enter_context(nc.psum_tensor("pbB", [128, 1024], F32))
    pb6_t = es.enter_context(nc.psum_tensor("pb6", [128, 512], F32))
    pb7_t = es.enter_context(nc.psum_tensor("pb7", [128, 512], F32))
    pb = [pbA[:, 0:512], pbA[:, 512:1024], pb2_t.ap(), pb3_t.ap(),
          pbB[:, 0:512], pbB[:, 512:1024], pb6_t.ap(), pb7_t.ap()]

    sems = {}
    cnt = {}
    for s in ["dW", "dX", "dG", "dS", "CC", "PE", "AC", "VE", "OUT"]:
        sems[s] = es.enter_context(nc.semaphore(s))
        cnt[s] = 0

    def bump(s, by=1):
        cnt[s] += by
        return cnt[s]

    blk = nc.Block()
    blk.__enter__()

    W = {}

    ONES_F = constf[:, 0:1]
    B2COL = lambda dmt: constf[:, 1 + dmt: 2 + dmt]
    B1COL = lambda ft: constf[:, 17 + ft: 18 + ft]
    ONES_B = constb[:, 0:1]
    IDENT = constb[:, 2:130]
    MASKGLA = constb[:, 130:258]
    MASKDIAG = lambda j: constb[:, 258 + j * 512: 258 + (j + 1) * 512]
    ONESROW_B = constb[0:1, 2306:2434]
    HALFROW_B = constb[0:1, 4994:5122]
    ONESROW512_B = constb[0:1, 2434:2946]
    B2ROW = lambda dmt: constb[0:1, 2946 + dmt * 128: 2946 + (dmt + 1) * 128]

    def phase_prologue(pes):
        wslab = pes.enter_context(nc.sbuf_tensor("wslab", [128, 16384], F32))
        wslab2 = pes.enter_context(nc.sbuf_tensor("wslab2", [128, 16384], F32))
        wqslab = pes.enter_context(nc.sbuf_tensor("wqslab", [128, 16384], BF16))
        tmpq = pes.enter_context(nc.sbuf_tensor("tmpq", [128, 8192], BF16))
        sc_s = pes.enter_context(nc.sbuf_tensor("sc_s", [128, 128], F32))
        sc8 = pes.enter_context(nc.sbuf_tensor("sc8", [128, 16], F32))
        scacc = pes.enter_context(nc.sbuf_tensor("scacc", [128, 2], F32))

        def _(sync):
            sync.dma_start(out=constf[:], in_=constf_in).then_inc(sems["dX"], 16)
            sync.dma_start(out=constb[:], in_=constb_in).then_inc(sems["dX"], 16)
            sync.dma_start(out=trig[:], in_=trig_in).then_inc(sems["dX"], 16)
            sync.dma_start(out=offs_s[:], in_=offs_in).then_inc(sems["dX"], 16)
            sync.dma_start(out=wslab[:].rearrange("p (f d c) -> p f d c", f=8, d=NT),
                           in_=w1s_in.rearrange("f d p c -> p f d c")).then_inc(sems["dX"], 16)
        blk.sync(_)
        bump("dX", 16 * 5)
        W["p_loads"] = cnt["dX"]

        def _(sync):
            sync.dma_start(out=wslab2[:].rearrange("p (f d c) -> p f d c", f=2, d=FT),
                           in_=w2s_in.rearrange("f d p c -> p f d c")).then_inc(sems["dX"], 16)
        blk.sync(_)
        W["p_w2load"] = bump("dX", 16)

        def _(v):
            v.wait_ge(sems["dX"], W["p_loads"])
            v.memset(onesrow_f[:], 1.0)
            v.memset(sc_s[:], 0.0)
            v.reduce_sum(sc_s[:, 0:1], wslab[:], AX.X, apply_absolute_value=True)
            v.wait_ge(sems["dX"], W["p_w2load"])
            ins = v.reduce_sum(sc_s[:, 1:2], wslab2[:], AX.X, apply_absolute_value=True)
            ins.then_inc(sems["VE"], 1)
        blk.vector(_)
        W["p_red"] = bump("VE")

        def _(sync):
            sync.wait_ge(sems["VE"], W["p_red"])
            sync.dma_start(out=sc_loc, in_=sc_s[:]).then_inc(sems["dX"], 16)
        blk.sync(_)
        W["p_scst"] = bump("dX", 16)

        def _(g):
            g.wait_ge(sems["dX"], W["p_scst"])
            g.collective_compute("AllGather", ALU.bypass, replica_groups=[list(range(8))],
                                 ins=[sc_loc], outs=[sc_all]).then_inc(sems["CC"], 1)
        blk.gpsimd(_)
        W["p_scAG"] = bump("CC")

        def _(sync):
            sync.wait_ge(sems["CC"], W["p_scAG"])
            sync.dma_start(out=sc8[:].rearrange("p (k c) -> p k c", k=8),
                           in_=sc_all[:, 0:2].rearrange("(k p) c -> p k c", p=128)).then_inc(sems["dX"], 16)
        blk.sync(_)
        W["p_sc8"] = bump("dX", 16)

        def _(v):
            v.wait_ge(sems["dX"], W["p_sc8"])
            v.reduce_sum(scacc[:], sc8[:].rearrange("p (k c) -> p c k", k=8), AX.X).then_inc(sems["VE"], 1)
        blk.vector(_)
        W["p_scsum"] = bump("VE")

        def _(t):
            t.wait_ge(sems["VE"], W["p_scsum"])
            t.matmul(pb[6][0:1, 0:2], ONES_F, scacc[:], start=True, stop=True).then_inc(sems["PE"], 1)
        blk.tensor(_)
        W["p_tot"] = bump("PE")

        def _(sc):
            sc.wait_ge(sems["PE"], W["p_tot"])
            sc.mul(thr_s[0:1, 0:2], pb[6][0:1, 0:2], 0.7 / (DFF * DM)).then_inc(sems["AC"], 1)
        blk.scalar(_)
        W["p_thr"] = bump("AC")

        def _(t):
            t.wait_ge(sems["AC"], W["p_thr"])
            t.matmul(pb[6][:, 0:1], onesrow_f[:], thr_s[0:1, 0:1], start=True, stop=True)
            t.matmul(pb[6][:, 1:2], onesrow_f[:], thr_s[0:1, 1:2], start=True, stop=True).then_inc(sems["PE"], 1)
        blk.tensor(_)
        W["p_thrb"] = bump("PE")

        def _(sc):
            sc.wait_ge(sems["PE"], W["p_thrb"])
            sc.copy(thrv[:], pb[6][:, 0:2])
            sc.mul(nthrv[:], pb[6][:, 0:2], -1.0).then_inc(sems["AC"], 1)
        blk.scalar(_)
        W["p_thrv"] = bump("AC")

        for wi in range(2):
            slab = wslab if wi == 0 else wslab2
            wq_loc_t = w1q_loc if wi == 0 else w2q_loc
            wq_all_t = w1q_all if wi == 0 else w2q_all
            dpar = NT if wi == 0 else FT

            def _(v, wi=wi, slab=slab):
                v.wait_ge(sems["AC"], W["p_thrv"])
                if wi == 1:
                    v.wait_ge(sems["dX"], W["p_qst0"])
                for hh in range(2):
                    sl = slice(hh * 8192, (hh + 1) * 8192)
                    v.tensor_scalar(tmpq[:], slab[:, sl], nthrv[:, wi:wi + 1], None, ALU.is_le)
                    ins = v.scalar_tensor_tensor(wqslab[:, sl], slab[:, sl], thrv[:, wi:wi + 1],
                                                 tmpq[:], ALU.is_ge, ALU.subtract)
                ins.then_inc(sems["VE"], 1)
            blk.vector(_)
            W[f"p_q{wi}"] = bump("VE")

            def _(sync, wi=wi, wq_loc_t=wq_loc_t, dpar=dpar):
                sync.wait_ge(sems["VE"], W[f"p_q{wi}"])
                sync.dma_start(out=wq_loc_t.rearrange("(f d p) c -> p f d c", p=128, d=dpar),
                               in_=wqslab[:].rearrange("p (f d c) -> p f d c", d=dpar, c=128)).then_inc(sems["dX"], 16)
            blk.sync(_)
            W[f"p_qst{wi}"] = bump("dX", 16)

            def _(g, wi=wi, wq_loc_t=wq_loc_t, wq_all_t=wq_all_t):
                g.wait_ge(sems["dX"], W[f"p_qst{wi}"])
                g.wait_ge(sems["CC"], cnt["CC"])
                g.collective_compute("AllGather", ALU.bypass, replica_groups=[list(range(8))],
                                     ins=[wq_loc_t], outs=[wq_all_t]).then_inc(sems["CC"], 1)
            blk.gpsimd(_)
            W[f"p_wAG{wi}"] = bump("CC")

    def emit_ln(tag, F, free_after, pre_bf16=False, xc_sem="dX"):
        xc, xcb, xsq, xn, scr = F["xc"], F["xcb"], F["xsq"], F["xn"], F["scr"]

        def _(v):
            v.wait_ge(sems[xc_sem], W[tag + "_xc"])
            v.wait_ge(sems["VE"], free_after)
            if pre_bf16:
                v.engine_nop().then_inc(sems["VE"], 1)
            else:
                v.tensor_copy(xcb[:, 0:NT * BLKR], xc[:]).then_inc(sems["VE"], 1)
        blk.vector(_)
        ve_cast = bump("VE")

        def _(sc):
            sc.wait_ge(sems["VE"], ve_cast)
            sc.activation(xsq[:], xc[:], AF.Square).then_inc(sems["AC"], 1)
        blk.scalar(_)
        ac_sq = bump("AC")

        def _(t):
            t.wait_ge(sems["VE"], ve_cast)
            t.wait_ge(sems["AC"], ac_sq)
            for dt in range(NT):
                t.matmul(pb[4][0:1, :], ONES_B, xcb[:, dt * 512:(dt + 1) * 512],
                         start=(dt == 0), stop=(dt == NT - 1))
            for dt in range(NT):
                ins = t.matmul(pb[5][0:1, :], ONES_B, xsq[:, dt * 512:(dt + 1) * 512],
                               start=(dt == 0), stop=(dt == NT - 1))
            ins.then_inc(sems["PE"], 1)
        blk.tensor(_)
        pe_sums = bump("PE")

        def _(sc):
            sc.wait_ge(sems["PE"], pe_sums)
            sc.mul(stat[0:1, 0:512], pb[4][0:1, :], -1.0 / DM)
            sc.mul(stat[0:1, 512:1024], pb[5][0:1, :], 1.0 / DM).then_inc(sems["AC"], 1)
        blk.scalar(_)
        ac_stat = bump("AC")

        def _(v):
            v.wait_ge(sems["AC"], ac_stat)
            v.tensor_mul(stat[0:1, 1024:1536], stat[0:1, 0:512], stat[0:1, 0:512])
            v.tensor_sub(stat[0:1, 1536:2048], stat[0:1, 512:1024], stat[0:1, 1024:1536]).then_inc(sems["VE"], 1)
        blk.vector(_)
        ve_var = bump("VE")

        def _(sc):
            sc.wait_ge(sems["VE"], ve_var)
            sc.activation(stat[0:1, 1024:1536], stat[0:1, 1536:2048], AF.Ln, bias=constf[0:1, 82:83])
            sc.activation(stat[0:1, 1536:2048], stat[0:1, 1024:1536], AF.Exp, scale=-0.5)
            sc.copy(statb[0:1, 0:512], stat[0:1, 0:512])
            sc.copy(statb[0:1, 512:1024], stat[0:1, 1536:2048]).then_inc(sems["AC"], 1)
        blk.scalar(_)
        ac_rstd = bump("AC")

        def _(t):
            t.wait_ge(sems["AC"], ac_rstd)
            t.matmul(pb[4][:], ONESROW_B, statb[0:1, 0:512], start=True, stop=True)
            t.matmul(pb[5][:], ONESROW_B, statb[0:1, 512:1024], start=True, stop=True).then_inc(sems["PE"], 1)
        blk.tensor(_)
        pe_bc = bump("PE")

        def _(v):
            v.wait_ge(sems["PE"], pe_bc)
            for dt in range(NT):
                v.tensor_add(scr[:], xc[:, dt * 512:(dt + 1) * 512], pb[4][:])
                ins = v.tensor_mul(xn[:, dt * 512:(dt + 1) * 512], scr[:], pb[5][:])
            ins.then_inc(sems["VE"], 1)
        blk.vector(_)
        W[tag + "_apply"] = bump("VE")

    def phase_ffn(name, F, xin, xout, out_sem, free0, dx_gate=0):
        for b in range(NB):
            tag = f"{name}_b{b}"
            prev_free = W.get(f"{name}_b{b-1}_evacall", free0)

            def _(sync, b=b, prev_free=prev_free, dx_gate=dx_gate if b == 0 else 0, xin=xin, F=F):
                sync.wait_ge(sems["VE"], prev_free)
                if dx_gate:
                    sync.wait_ge(sems["dX"], dx_gate)
                sync.dma_start(
                    out=F["xc"][:].rearrange("p (dt r) -> p dt r", dt=NT),
                    in_=xin.rearrange("(dt p) r -> p dt r", p=128)[:, :, ds(b * BLKR, BLKR)],
                ).then_inc(sems["dX"], 16)
            blk.sync(_)
            W[tag + "_xc"] = bump("dX", 16)

            emit_ln(tag, F, prev_free)

            gelu_base = cnt["AC"]
            dw_w1_base = cnt["dW"]

            def _(sync, gelu_base=gelu_base, F=F):
                for ft in range(FT):
                    if ft >= 2:
                        sync.wait_ge(sems["AC"], gelu_base + ft - 1)
                    sync.dma_start(
                        out=F["w1b"][:, (ft % 2) * DM:(ft % 2) * DM + DM].rearrange("p (d c) -> p d c", c=128),
                        in_=w1q_all[ft * DM:(ft + 1) * DM, :].rearrange("(d p) c -> p d c", p=128),
                    ).then_inc(sems["dW"], 16)
            blk.sync(_)
            bump("dW", 16 * FT)

            pe_h_base = cnt["PE"]

            def _(t, tag=tag, gelu_base=gelu_base, dw_w1_base=dw_w1_base, F=F):
                t.wait_ge(sems["VE"], W[tag + "_apply"])
                for ft in range(FT):
                    t.wait_ge(sems["dW"], dw_w1_base + (ft + 1) * 16)
                    if ft >= 2:
                        t.wait_ge(sems["AC"], gelu_base + ft - 1)
                    for dt in range(NT):
                        ins = t.matmul(pb[ft % 2][:],
                                       F["w1b"][:, (ft % 2) * DM + dt * 128:(ft % 2) * DM + (dt + 1) * 128],
                                       F["xn"][:, dt * 512:(dt + 1) * 512],
                                       start=(dt == 0), stop=(dt == NT - 1))
                    ins.then_inc(sems["PE"], 1)
            blk.tensor(_)
            bump("PE", FT)

            def _(sc, pe_h_base=pe_h_base, F=F):
                for ft in range(FT):
                    sc.wait_ge(sems["PE"], pe_h_base + ft + 1)
                    sc.activation(F["hT"][:, ft * 512:(ft + 1) * 512], pb[ft % 2][:],
                                  AF.Gelu, bias=B1COL(ft)).then_inc(sems["AC"], 1)
            blk.scalar(_)
            bump("AC", FT)
            ac_gelu_all = cnt["AC"]
            assert ac_gelu_all == gelu_base + FT

            evac_base = cnt["VE"]
            dw_w2_base = cnt["dW"]
            prev_evacall = W.get(f"{name}_b{b-1}_evacall", 0)

            def _(sync, evac_base=evac_base, F=F, b=b, name=name):
                if b == 0 and name == "f1":
                    sync.wait_ge(sems["CC"], W["p_wAG1"])
                for dmt in range(16):
                    if dmt >= 2:
                        sync.wait_ge(sems["VE"], evac_base + dmt - 1)
                    sync.dma_start(
                        out=F["w2b"][:, (dmt % 2) * DFF:(dmt % 2) * DFF + DFF].rearrange("p (f c) -> p f c", c=128),
                        in_=w2q_all[dmt * DFF:(dmt + 1) * DFF, :].rearrange("(f p) c -> p f c", p=128),
                    ).then_inc(sems["dW"], 16)
            blk.sync(_)
            bump("dW", 16 * 16)

            pe_o_base = cnt["PE"]

            def _(t, ac_gelu_all=ac_gelu_all, evac_base=evac_base,
                  dw_w2_base=dw_w2_base, prev_evacall=prev_evacall, F=F):
                t.wait_ge(sems["AC"], ac_gelu_all)
                if prev_evacall:
                    t.wait_ge(sems["VE"], prev_evacall)
                for dmt in range(16):
                    t.wait_ge(sems["dW"], dw_w2_base + (dmt + 1) * 16)
                    if dmt >= 2:
                        t.wait_ge(sems["VE"], evac_base + dmt - 1)
                    for ft in range(FT):
                        t.matmul(pb[2 + dmt % 2][:],
                                 F["w2b"][:, (dmt % 2) * DFF + ft * 128:(dmt % 2) * DFF + (ft + 1) * 128],
                                 F["hT"][:, ft * 512:(ft + 1) * 512],
                                 start=(ft == 0), stop=False)
                    ins = t.matmul(pb[2 + dmt % 2][:], B2ROW(dmt), ONESROW512_B,
                                   start=False, stop=True)
                    ins.then_inc(sems["PE"], 1)
            blk.tensor(_)
            bump("PE", 16)

            xst_base = cnt[out_sem]

            def _(v, pe_o_base=pe_o_base, xst_base=xst_base, out_sem=out_sem, F=F):
                for dmt in range(16):
                    v.wait_ge(sems["PE"], pe_o_base + dmt + 1)
                    if dmt >= 2:
                        v.wait_ge(sems[out_sem], xst_base + (dmt - 1) * 16)
                    v.scalar_tensor_tensor(F["xo"][:, (dmt % 2) * 512:(dmt % 2) * 512 + 512],
                                           pb[2 + dmt % 2][:], 0.5,
                                           F["xc"][:, dmt * 512:(dmt + 1) * 512],
                                           ALU.mult, ALU.add).then_inc(sems["VE"], 1)
            blk.vector(_)
            bump("VE", 16)
            W[tag + "_evacall"] = cnt["VE"]

            def _(g, b=b, evac_base=evac_base, xout=xout, out_sem=out_sem, F=F):
                for dmt in range(16):
                    g.wait_ge(sems["VE"], evac_base + dmt + 1)
                    g.dma_start(
                        out=xout.rearrange("(dt p) r -> p dt r", p=128)[:, dmt:dmt + 1, ds(b * BLKR, BLKR)],
                        in_=F["xo"][:, (dmt % 2) * 512:(dmt % 2) * 512 + 512].rearrange("p (o r) -> p o r", o=1),
                    ).then_inc(sems[out_sem], 16)
            blk.gpsimd(_)
            bump(out_sem, 16 * 16)
            W[tag + "_stored"] = cnt[out_sem]
        return W[f"{name}_b{NB-1}_evacall"]

    def alloc_ffn_bufs(stack, pfx):
        F = {}
        F["xc"] = stack.enter_context(nc.sbuf_tensor(pfx + "xc", [128, NT * BLKR], F32))
        F["xn"] = stack.enter_context(nc.sbuf_tensor(pfx + "xn", [128, NT * BLKR], BF16))
        F["hT"] = stack.enter_context(nc.sbuf_tensor(pfx + "hT", [128, FT * BLKR], BF16))
        F["w1b"] = stack.enter_context(nc.sbuf_tensor(pfx + "w1b", [128, 2 * DM], BF16))
        F["w2b"] = stack.enter_context(nc.sbuf_tensor(pfx + "w2b", [128, 2 * DFF], BF16))
        F["scr"] = stack.enter_context(nc.sbuf_tensor(pfx + "scr", [128, 512], F32))
        F["scr2"] = stack.enter_context(nc.sbuf_tensor(pfx + "scr2", [128, 2 * 512], F32))
        F["xo"] = stack.enter_context(nc.sbuf_tensor(pfx + "xo", [128, 2 * 512], F32))
        F["xsq"] = F["xn"]
        F["xcb"] = F["hT"]
        return F



    BLK_SHIP = 3328
    BSQR, BQS, BSKR, BKS, BGL, BV = 0, 1024, 2048, 2304, 2560, 2816
    CSQR, CQS, CSKR, CKS, CGL, CV = 0, 4096, 8192, 9216, 10240, 11264
    PoolE = mybir.EngineType.Pool

    def phase_attn(free0_ve, pe_free, dx_x1_stored):
        cc_qkv_base = cnt["CC"]

        snd = ExitStack()
        SA = lambda n, s, d: snd.enter_context(nc.sbuf_tensor(n, s, d))
        axc = SA("axc", [128, NT * BLKR], BF16)
        axn = SA("axn", [128, NT * BLKR], BF16)
        ascr = SA("ascr", [128, 512], F32)
        wnb = SA("wnb", [128, 2 * NT * 512], BF16)
        qnat = SA("qnat", [128, 4 * 2048], BF16)
        rnat = SA("rnat", [128, 4 * 2048], BF16)
        qsn = SA("qsn", [128, 4 * 2048], BF16)
        eqb = SA("eqb", [128, 4 * 2048], BF16)
        rsc = SA("rsc", [128, 4096], BF16)
        srec = SA("srec", [128, 128], F32)
        knat = SA("knat", [128, 4 * 512], BF16)
        krn = SA("krn", [128, 4 * 512], BF16)
        ksn = SA("ksn", [128, 4 * 512], BF16)
        ekb = SA("ekb", [128, 4 * 512], BF16)
        ktrb = SA("ktrb", [128, 4 * 512], BF16)
        glb = SA("glb", [128, 4 * 512], BF16)
        dsc = SA("dsc", [128, 2048], F32)
        shs = SA("shs", [128, 4 * 512], BF16)
        AF_ = {"xc": axc, "xcb": axc, "xsq": axn, "xn": axn, "scr": ascr}
        qrecv = SA("qrecv", [128, 80], F32)

        for c in range(NB):
            tag = f"at_c{c}"
            prev_apply = W.get(f"at_c{c-1}_apply", free0_ve)
            prev_done = W.get(f"at_c{c-1}_shipdone", 0)

            def _(g, c=c, prev_apply=prev_apply, prev_done=prev_done):
                g.wait_ge(sems["dS"], dx_x1_stored[c])
                g.wait_ge(sems["VE"], prev_apply)
                if c == 0:
                    g.wait_ge(sems["PE"], pe_free)
                if prev_done:
                    g.wait_ge(sems["dX"], prev_done)
                ins = g.dma_start(
                    out=axc[:].rearrange("p (dt r) -> p dt r", dt=NT),
                    in_=x1T.rearrange("(dt p) r -> p dt r", p=128)[:, :, ds(c * BLKR, BLKR)],
                )
                ins.then_inc(sems["dG"], 16)
                W[f"at_c{c}_xc"] = bump("dG", 16)
            blk.gpsimd(_)

            emit_ln(tag, AF_, prev_apply, pre_bf16=True, xc_sem="dG")

            pe_proj_base = cnt["PE"]
            ac_ev_base = cnt["AC"]
            dg_wn = []

            def _(g, pe_proj_base=pe_proj_base):
                for og in range(6):
                    if og >= 2:
                        g.wait_ge(sems["PE"], pe_proj_base + (og - 1) * 4)
                    ins = g.dma_start(
                        out=wnb[:, (og % 2) * 8192:(og % 2) * 8192 + 8192].rearrange("p (dt c) -> p dt c", dt=NT),
                        in_=wqkv_in[og].rearrange("dt p c -> p dt c"),
                    )
                    ins.then_inc(sems["dG"], 16)
                    dg_wn.append(bump("dG", 16))
            blk.gpsimd(_)

            def _(t, tag=tag, ac_ev_base=ac_ev_base):
                t.wait_ge(sems["VE"], W[tag + "_apply"])
                for og in range(6):
                    t.wait_ge(sems["dG"], dg_wn[og])
                    for rs in range(4):
                        i = og * 4 + rs
                        if i >= 2:
                            t.wait_ge(sems["AC"], ac_ev_base + i - 1)
                        for dt in range(NT):
                            ins = t.matmul(pb[i % 2][:],
                                           axn[:, dt * 512 + rs * 128: dt * 512 + (rs + 1) * 128],
                                           wnb[:, (og % 2) * 8192 + dt * 512:(og % 2) * 8192 + (dt + 1) * 512],
                                           start=(dt == 0), stop=(dt == NT - 1))
                        ins.then_inc(sems["PE"], 1)
                        bump("PE")
            blk.tensor(_)

            dx_vship_base = cnt["dX"]
            ac_evs = []

            def _(sc, pe_proj_base=pe_proj_base, dx_vship_base=dx_vship_base):
                for og in range(6):
                    for rs in range(4):
                        i = og * 4 + rs
                        sc.wait_ge(sems["PE"], pe_proj_base + i + 1)
                        if og < 4:
                            ins = sc.copy(qnat[:, rs * 2048 + og * 512: rs * 2048 + (og + 1) * 512], pb[i % 2][:])
                        elif og == 4:
                            ins = sc.copy(knat[:, rs * 512:(rs + 1) * 512], pb[i % 2][:])
                        else:
                            if rs >= 2:
                                sc.wait_ge(sems["dX"], dx_vship_base + (rs - 1) * 16)
                            ins = sc.copy(shs[:, (rs % 2) * 512:(rs % 2) * 512 + 512], pb[i % 2][:])
                        ins.then_inc(sems["AC"], 1)
                        ac_evs.append(bump("AC"))
            blk.scalar(_)

            def _(sync, c=c):
                for rs in range(4):
                    sync.wait_ge(sems["AC"], ac_evs[20 + rs])
                    for b_ in range(2):
                        row = b_ * BLK_SHIP + BV + rs * 128
                        ins = sync.dma_start(
                            out=qkv_in[c][row: row + 128, 0:256],
                            in_=shs[:, (rs % 2) * 512 + b_ * 256:(rs % 2) * 512 + b_ * 256 + 256])
                        ins.then_inc(sems["dX"], 16)
                        bump("dX", 16)
                        ins = sync.dma_start(
                            out=qkv_in[c][row: row + 128, 256:512],
                            in_=shs[:, (rs % 2) * 512 + (1 - b_) * 256:(rs % 2) * 512 + (1 - b_) * 256 + 256])
                        ins.then_inc(sems["dX"], 16)
                        bump("dX", 16)
            blk.sync(_)

            def _(sc):
                sc.wait_ge(sems["AC"], ac_evs[15])
                ins = sc.activation(eqb[:], qnat[:], AF.Exp)
                ins.then_inc(sems["AC"], 1)
                W[tag + "_eq"] = bump("AC")
                sc.wait_ge(sems["AC"], ac_evs[19])
                ins = sc.activation(ekb[:], knat[:], AF.Exp)
                ins.then_inc(sems["AC"], 1)
                W[tag + "_ek"] = bump("AC")
            blk.scalar(_)

            def _(v, c=c, tag=tag):
                v.wait_ge(sems["AC"], ac_evs[19])
                for rs in range(4):
                    tile = c * 4 + rs
                    cosNq = trig[:, tile * 64: tile * 64 + 64].rearrange("p (o d) -> p o d", o=1).broadcast_to([128, 16, 64])
                    sinNq = trig[:, 2048 + tile * 64: 2048 + tile * 64 + 64].rearrange("p (o d) -> p o d", o=1).broadcast_to([128, 16, 64])
                    qv = qnat[:, rs * 2048:(rs + 1) * 2048].rearrange("p (h d) -> p h d", d=128)
                    rv = rnat[:, rs * 2048:(rs + 1) * 2048].rearrange("p (h d) -> p h d", d=128)
                    s0 = rsc[:, 0:1024].rearrange("p (h d) -> p h d", d=64)
                    s1 = rsc[:, 1024:2048].rearrange("p (h d) -> p h d", d=64)
                    s2 = rsc[:, 2048:3072].rearrange("p (h d) -> p h d", d=64)
                    s3 = rsc[:, 3072:4096].rearrange("p (h d) -> p h d", d=64)
                    v.tensor_mul(s0, qv[:, :, 0:64], cosNq)
                    v.tensor_mul(s1, qv[:, :, 64:128], sinNq)
                    v.tensor_mul(s2, qv[:, :, 64:128], cosNq)
                    ins = v.tensor_mul(s3, qv[:, :, 0:64], sinNq)
                    ins.then_inc(sems["VE"], 1)
                    v.wait_ge(sems["VE"], bump("VE"))
                    v.tensor_sub(rv[:, :, 0:64], s0, s1)
                    v.tensor_add(rv[:, :, 64:128], s2, s3)
                    cosNk = trig[:, tile * 64: tile * 64 + 64].rearrange("p (o d) -> p o d", o=1).broadcast_to([128, 4, 64])
                    sinNk = trig[:, 2048 + tile * 64: 2048 + tile * 64 + 64].rearrange("p (o d) -> p o d", o=1).broadcast_to([128, 4, 64])
                    kv_ = knat[:, rs * 512:(rs + 1) * 512].rearrange("p (h d) -> p h d", d=128)
                    kr = krn[:, rs * 512:(rs + 1) * 512].rearrange("p (h d) -> p h d", d=128)
                    k0 = rsc[:, 0:256].rearrange("p (h d) -> p h d", d=64)
                    k1 = rsc[:, 256:512].rearrange("p (h d) -> p h d", d=64)
                    k2 = rsc[:, 512:768].rearrange("p (h d) -> p h d", d=64)
                    k3 = rsc[:, 768:1024].rearrange("p (h d) -> p h d", d=64)
                    v.tensor_mul(k0, kv_[:, :, 0:64], cosNk)
                    v.tensor_mul(k1, kv_[:, :, 64:128], sinNk)
                    v.tensor_mul(k2, kv_[:, :, 64:128], cosNk)
                    ins = v.tensor_mul(k3, kv_[:, :, 0:64], sinNk)
                    ins.then_inc(sems["VE"], 1)
                    v.wait_ge(sems["VE"], bump("VE"))
                    v.tensor_sub(kr[:, :, 0:64], k0, k1)
                    ins = v.tensor_add(kr[:, :, 64:128], k2, k3)
                    ins.then_inc(sems["VE"], 1)
                    bump("VE")
                v.wait_ge(sems["AC"], W[tag + "_ek"])
                for rs in range(4):
                    v.reduce_sum(srec[:, rs * 16:(rs + 1) * 16],
                                 eqb[:, rs * 2048:(rs + 1) * 2048].rearrange("p (h d) -> p h d", d=128), AX.X)
                for rs in range(4):
                    ins = v.reduce_sum(srec[:, 64 + rs * 4:64 + (rs + 1) * 4],
                                       ekb[:, rs * 512:(rs + 1) * 512].rearrange("p (h d) -> p h d", d=128), AX.X)
                ins.then_inc(sems["VE"], 1)
                v.wait_ge(sems["VE"], bump("VE"))
                ins = v.reciprocal(qrecv[:], srec[:, 0:80])
                ins.then_inc(sems["VE"], 1)
                v.wait_ge(sems["VE"], bump("VE"))
                for rs in range(4):
                    v.tensor_mul(qsn[:, rs * 2048:(rs + 1) * 2048].rearrange("p (h d) -> p h d", d=128),
                                 eqb[:, rs * 2048:(rs + 1) * 2048].rearrange("p (h d) -> p h d", d=128),
                                 qrecv[:, rs * 16:(rs + 1) * 16].rearrange("p (h o) -> p h o", o=1).broadcast_to([128, 16, 128]))
                for rs in range(4):
                    ins = v.tensor_mul(ksn[:, rs * 512:(rs + 1) * 512].rearrange("p (h d) -> p h d", d=128),
                                       ekb[:, rs * 512:(rs + 1) * 512].rearrange("p (h d) -> p h d", d=128),
                                       qrecv[:, 64 + rs * 4:64 + (rs + 1) * 4].rearrange("p (h o) -> p h o", o=1).broadcast_to([128, 4, 128]))
                ins.then_inc(sems["VE"], 1)
                W[tag + "_derive"] = bump("VE")
            blk.vector(_)

            tlist = [(rnat, BSQR, 16, 2048), (qsn, BQS, 16, 2048),
                     (krn, BSKR, 4, 512), (ksn, BKS, 4, 512), (knat, -1, 4, 512)]
            pe_tr_base = cnt["PE"]
            ac_tr_base = cnt["AC"]
            dx_sh_base = cnt["dX"]

            def _(t, tag=tag, ac_tr_base=ac_tr_base):
                t.wait_ge(sems["VE"], W[tag + "_derive"])
                gi = 0
                for (buf, shiprow, nodt, w) in tlist:
                    for odt in range(nodt):
                        if gi >= 2:
                            t.wait_ge(sems["AC"], ac_tr_base + gi - 1)
                        for rs in range(4):
                            ins = t.matmul(pb[2 + gi % 2][:, rs * 128:(rs + 1) * 128],
                                           buf[:, rs * w + odt * 128: rs * w + (odt + 1) * 128],
                                           IDENT, start=(rs == 0), stop=(rs == 3))
                        ins.then_inc(sems["PE"], 1)
                        bump("PE")
                        gi += 1
            blk.tensor(_)

            def _(sc, tag=tag, pe_tr_base=pe_tr_base, dx_sh_base=dx_sh_base):
                gi = 0
                si = 0
                for (buf, shiprow, nodt, w) in tlist:
                    for odt in range(nodt):
                        sc.wait_ge(sems["PE"], pe_tr_base + gi + 1)
                        if shiprow >= 0:
                            if si < 4:
                                sc.wait_ge(sems["dX"], dx_sh_base)
                            else:
                                sc.wait_ge(sems["dX"], dx_sh_base + (si - 3) * 16)
                            ins = sc.copy(shs[:, (si % 4) * 512:(si % 4) * 512 + 512], pb[2 + gi % 2][:])
                            si += 1
                        else:
                            ins = sc.copy(ktrb[:, odt * 512:(odt + 1) * 512], pb[2 + gi % 2][:])
                        ins.then_inc(sems["AC"], 1)
                        bump("AC")
                        gi += 1
            blk.scalar(_)

            def _(sync, c=c, ac_tr_base=ac_tr_base):
                gi = 0
                si = 0
                for (buf, shiprow, nodt, w) in tlist:
                    for odt in range(nodt):
                        if shiprow >= 0:
                            sync.wait_ge(sems["AC"], ac_tr_base + gi + 1)
                            b_ = odt // (nodt // 2)
                            lo = odt % (nodt // 2)
                            row = b_ * BLK_SHIP + shiprow + lo * 128
                            ins = sync.dma_start(
                                out=qkv_in[c][row: row + 128, :],
                                in_=shs[:, (si % 4) * 512:(si % 4) * 512 + 512])
                            ins.then_inc(sems["dX"], 16)
                            bump("dX", 16)
                            si += 1
                        gi += 1
            blk.sync(_)
            ac_ktr_done = cnt["AC"]

            ve_gl0 = cnt["VE"] + 1

            def _(sc, ac_ktr_done=ac_ktr_done, ve_gl0=ve_gl0):
                sc.wait_ge(sems["AC"], ac_ktr_done)
                for odt in range(4):
                    if odt >= 1:
                        sc.wait_ge(sems["VE"], ve_gl0 + odt - 1)
                    sc.activation(dsc[:, 0:512], ktrb[:, odt * 512:(odt + 1) * 512], AF.Abs)
                    sc.activation(dsc[:, 512:1024], dsc[:, 0:512], AF.Exp, scale=-1.0)
                    ins = sc.activation(dsc[:, 0:512], dsc[:, 512:1024], AF.Ln, bias=1.0)
                    ins.then_inc(sems["AC"], 1)
                    bump("AC")
            blk.scalar(_)
            ac_gl_base = ac_ktr_done

            def _(v, tag=tag, ac_gl_base=ac_gl_base):
                for odt in range(4):
                    v.wait_ge(sems["AC"], ac_gl_base + odt + 1)
                    v.tensor_scalar_min(dsc[:, 1024:1536], ktrb[:, odt * 512:(odt + 1) * 512], 0.0)
                    ins = v.tensor_sub(glb[:, odt * 512:(odt + 1) * 512], dsc[:, 1024:1536], dsc[:, 0:512])
                    ins.then_inc(sems["VE"], 1)
                    bump("VE")
            blk.vector(_)
            ve_gl_base = cnt["VE"] - 4

            def _(sync, c=c, ve_gl_base=ve_gl_base):
                for odt in range(4):
                    sync.wait_ge(sems["VE"], ve_gl_base + odt + 1)
                    b_ = odt // 2
                    row = b_ * BLK_SHIP + BGL + (odt % 2) * 128
                    ins = sync.dma_start(out=qkv_in[c][row: row + 128, :],
                                         in_=glb[:, odt * 512:(odt + 1) * 512])
                    ins.then_inc(sems["dX"], 16)
                    bump("dX", 16)
            blk.sync(_)
            W[tag + "_shipdone"] = cnt["dX"]

            def _(g, c=c, tag=tag):
                g.wait_ge(sems["dX"], W[tag + "_shipdone"])
                g.wait_ge(sems["CC"], cnt["CC"])
                ins = g.collective_compute("AllGather", ALU.bypass,
                                           replica_groups=[[0, 1], [2, 3], [4, 5], [6, 7]],
                                           ins=[qkv_in[c]], outs=[qkv_out[c]])
                ins.then_inc(sems["CC"], 1)
                bump("CC")
            blk.gpsimd(_)

        snd_pe_end = cnt["PE"]
        snd_ve_end = cnt["VE"]
        snd_ac_end = cnt["AC"]
        snd.close()

        rcv = ExitStack()
        RA = lambda n, s, d: rcv.enter_context(nc.sbuf_tensor(n, s, d))
        stg = RA("stg", [128, 26 * 512], BF16)
        skrT = RA("skrT", [128, KVH * SEQ], BF16)
        vful = RA("vful", [128, KVH * 32 * 128], BF16)
        Acum = RA("Acum", [128, KVH * 512], F32)
        eAb = RA("eAb", [128, KVH * 512], F32)
        eAn = RA("eAn", [128, KVH * 512], F32)
        kt_ = RA("kt_", [128, KVH * 512], BF16)
        ktn = RA("ktn", [128, KVH * 128], BF16)
        zf = RA("zf", [128, 128], BF16)
        Sst = RA("Sst", [128, KVH * 128], F32)
        Ssc = RA("Ssc", [128, KVH * 128], F32)
        Sb = RA("Sb", [128, KVH * 128], BF16)
        pgb = RA("pgb", [128, 128], BF16)
        o05 = RA("o05", [128, H * 512], BF16)
        oTb = RA("oTb", [128, H * 512], BF16)
        Pring = RA("Pring", [128, 2 * 2048], BF16)
        hdall = RA("hdall", [1, H * 512], BF16)
        hdb = RA("hdb", [1, H * 512], BF16)
        yraw = RA("yraw", [128, H * 512], BF16)
        qt_ = yraw
        yscr = RA("yscr", [128, 512], F32)
        ost = RA("ost", [128, 16 * 512], BF16)
        wob = RA("wob", [128, 2 * 2048], BF16)
        x1r = RA("x1r", [128, 2 * 512], F32)
        x2o = RA("x2o", [128, 2 * 512], F32)

        def _(v):
            v.wait_ge(sems["VE"], snd_ve_end)
            v.wait_ge(sems["PE"], snd_pe_end)
            v.wait_ge(sems["AC"], snd_ac_end)
            v.memset(zf[:], 0.0)
            ins = v.memset(Sst[:], 0.0)
            ins.then_inc(sems["VE"], 1)
            W["rcv_init"] = bump("VE")
        blk.vector(_)

        dg0 = cnt["dG"]
        for g_ in range(NG):
            W[f"g{g_}_staged"] = dg0 + (g_ + 1) * 16
        dg_p3 = dg0 + NG * 16
        for c_ in range(NB):
            W[f"wo_c{c_}_ost"] = dg_p3 + (c_ * 17 + 1) * 16
            for dmt_ in range(16):
                W[f"wo_c{c_}_wo{dmt_}"] = dg_p3 + (c_ * 17 + 1 + dmt_ + 1) * 16
        out20 = cnt["OUT"]
        for c_ in range(NB):
            for dmt_ in range(16):
                W[f"wo_c{c_}_st{dmt_}"] = out20 + (c_ * 16 + dmt_ + 1) * 16

        g_pe_end = {-1: snd_pe_end}
        g_ve_end = {-1: W["rcv_init"]}
        g_ac_end = {-1: snd_ac_end}
        g_ship = {-1: 0}

        for g in range(NG):
            slot, step = g // 4, g % 4
            src2d = qkv_out[step]


            def _(v, g=g):
                v.wait_ge(sems["dG"], W[f"g{g}_staged"])
                v.tensor_copy(skrT[:].rearrange("p (kv r) -> p kv r", kv=KVH)[:, :, g * 512:(g + 1) * 512],
                              stg[:, CSKR: CSKR + 1024].rearrange("p (kv r) -> p kv r", kv=KVH))
                for kv in range(KVH):
                    v.tensor_copy(
                        vful[:, kv * 4096 + g * 512: kv * 4096 + (g + 1) * 512].rearrange("p (rs d) -> p rs d", d=128),
                        stg[:, CV:].rearrange("p (rs c) -> p rs c", c=512)[:, :, kv * 128:(kv + 1) * 128])
                for kv in range(KVH):
                    for j in range(4):
                        ins = v.tensor_tensor_scan(Acum[:, kv * 512 + j * 128: kv * 512 + (j + 1) * 128],
                                                   stg[:, CGL + kv * 512 + j * 128: CGL + kv * 512 + (j + 1) * 128],
                                                   zf[:], 0.0, ALU.add, ALU.add)
                ins.then_inc(sems["VE"], 1)
                W[f"g{g}_scan"] = bump("VE")
            blk.vector(_)

            def _(sc, g=g):
                sc.wait_ge(sems["VE"], W[f"g{g}_scan"])
                sc.activation(eAb[:], Acum[:], AF.Exp, scale=1.0 / 16.0)
                ins = sc.activation(eAn[:], Acum[:], AF.Exp, scale=-1.0 / 16.0)
                ins.then_inc(sems["AC"], 1)
                W[f"g{g}_eA"] = bump("AC")
            blk.scalar(_)

            def _(v, g=g):
                v.wait_ge(sems["AC"], W[f"g{g}_eA"])
                for h in range(H):
                    v.tensor_mul(qt_[:, h * 512:(h + 1) * 512], stg[:, CQS + h * 512: CQS + (h + 1) * 512],
                                 eAb[:, (h // 4) * 512:(h // 4 + 1) * 512])
                for kv in range(KVH):
                    ins = v.tensor_mul(kt_[:, kv * 512:(kv + 1) * 512], stg[:, CKS + kv * 512: CKS + (kv + 1) * 512],
                                       eAn[:, kv * 512:(kv + 1) * 512])
                ins.then_inc(sems["VE"], 1)
                W[f"g{g}_qtkt"] = bump("VE")
            blk.vector(_)

            for j in range(4):
                jt = f"g{g}j{j}"
                def _(sc, g=g, j=j, jt=jt):
                    if j > 0:
                        sc.wait_ge(sems["VE"], W[f"g{g}j{j-1}_supd"])
                    elif g > 0:
                        sc.wait_ge(sems["VE"], W[f"g{g-1}j3_supd"])
                    else:
                        sc.wait_ge(sems["VE"], W["rcv_init"])
                    if g == 0 and j == 0:
                        pass
                    sc.wait_ge(sems["VE"], W[f"g{g}_qtkt"])
                    if j == 0 and g > 0:
                        sc.wait_ge(sems["VE"], W[f"g{g-1}_oT"])
                    ins = sc.copy(Sb[:], Sst[:])
                    ins.then_inc(sems["AC"], 1)
                    W[jt + "_sb"] = bump("AC")
                blk.scalar(_)

                ac_ktn0 = cnt["AC"] + 1

                def _(t, g=g, j=j, jt=jt, ac_ktn0=ac_ktn0):
                    t.wait_ge(sems["VE"], W[f"g{g}_qtkt"])
                    if j > 0:
                        t.wait_ge(sems["VE"], W[f"g{g}j{j-1}_supd"])
                    elif g > 0:
                        t.wait_ge(sems["VE"], W[f"g{g-1}j3_supd"])
                    for kv in range(KVH):
                        if kv == 1:
                            t.wait_ge(sems["AC"], ac_ktn0)
                        ins = t.matmul(pb[6][:, 0:128],
                                       kt_[:, kv * 512 + j * 128: kv * 512 + (j + 1) * 128],
                                       IDENT, start=True, stop=True)
                        ins.then_inc(sems["PE"], 1)
                        W[jt + f"_ktr{kv}"] = bump("PE")
                blk.tensor(_)

                def _(sc, jt=jt):
                    for kv in range(KVH):
                        sc.wait_ge(sems["PE"], W[jt + f"_ktr{kv}"])
                        ins = sc.copy(ktn[:, kv * 128:(kv + 1) * 128], pb[6][:, 0:128])
                        ins.then_inc(sems["AC"], 1)
                        W[jt + f"_ktn{kv}"] = bump("AC")
                blk.scalar(_)

                def _(t, g=g, j=j, jt=jt):
                    t.wait_ge(sems["AC"], W[jt + "_ktn1"])
                    for kv in range(KVH):
                        ins = t.matmul(pb[6][:, 128 + kv * 128: 128 + (kv + 1) * 128],
                                       ktn[:, kv * 128:(kv + 1) * 128],
                                       vful[:, kv * 4096 + (g * 4 + j) * 128: kv * 4096 + (g * 4 + j) * 128 + 128],
                                       start=True, stop=True)
                        ins.then_inc(sems["PE"], 1)
                        W[jt + f"_u0{kv}"] = bump("PE")
                blk.tensor(_)

                for h in range(H):
                    kv = h // 4
                    def _(t, g=g, j=j, h=h, kv=kv, jt=jt):
                        if h >= 2:
                            t.wait_ge(sems["VE"], W[jt + f"_pg{h-2}"])
                        ins = t.matmul(pb[4 + h % 2][:, 0:128],
                                       kt_[:, kv * 512 + j * 128: kv * 512 + (j + 1) * 128],
                                       qt_[:, h * 512 + j * 128: h * 512 + (j + 1) * 128],
                                       start=True, stop=True)
                        ins.then_inc(sems["PE"], 1)
                        W[jt + f"_pgm{h}"] = bump("PE")
                    blk.tensor(_)

                    def _(v, h=h, jt=jt):
                        v.wait_ge(sems["PE"], W[jt + f"_pgm{h}"])
                        if h >= 1:
                            v.wait_ge(sems["PE"], W[jt + f"_om{h-1}"])
                        ins = v.tensor_mul(pgb[:], pb[4 + h % 2][:, 0:128], MASKGLA)
                        ins.then_inc(sems["VE"], 1)
                        W[jt + f"_pg{h}"] = bump("VE")
                    blk.vector(_)

                    def _(t, g=g, j=j, h=h, kv=kv, jt=jt):
                        t.wait_ge(sems["VE"], W[jt + f"_pg{h}"])
                        t.wait_ge(sems["AC"], W[jt + "_sb"])
                        if h >= 2:
                            t.wait_ge(sems["AC"], W[jt + f"_oev{h-2}"])
                        t.matmul(pb[2 + h % 2][:, 0:128],
                                 vful[:, kv * 4096 + (g * 4 + j) * 128: kv * 4096 + (g * 4 + j) * 128 + 128],
                                 pgb[:], start=True, stop=False)
                        ins = t.matmul(pb[2 + h % 2][:, 0:128],
                                       Sb[:, kv * 128:(kv + 1) * 128],
                                       qt_[:, h * 512 + j * 128: h * 512 + (j + 1) * 128],
                                       start=False, stop=True)
                        ins.then_inc(sems["PE"], 1)
                        W[jt + f"_om{h}"] = bump("PE")
                    blk.tensor(_)

                    def _(sc, g=g, j=j, h=h, jt=jt):
                        sc.wait_ge(sems["PE"], W[jt + f"_om{h}"])
                        ins = sc.mul(o05[:, h * 512 + j * 128: h * 512 + (j + 1) * 128],
                                     pb[2 + h % 2][:, 0:128], 0.5)
                        ins.then_inc(sems["AC"], 1)
                        W[jt + f"_oev{h}"] = bump("AC")
                    blk.scalar(_)

                def _(v, g=g, j=j, jt=jt):
                    v.wait_ge(sems["PE"], W[jt + "_u01"])
                    v.wait_ge(sems["AC"], W[jt + "_sb"])
                    for kv in range(KVH):
                        eac = eAb[:, kv * 512 + j * 128 + 127: kv * 512 + j * 128 + 128]
                        v.tensor_add(Ssc[:, kv * 128:(kv + 1) * 128], Sst[:, kv * 128:(kv + 1) * 128],
                                     pb[6][:, 128 + kv * 128: 128 + (kv + 1) * 128])
                        ins = v.tensor_scalar(Sst[:, kv * 128:(kv + 1) * 128], Ssc[:, kv * 128:(kv + 1) * 128],
                                              eac, None, ALU.mult)
                        ins.then_inc(sems["VE"], 1)
                        v.wait_ge(sems["VE"], bump("VE"))
                    W[f"g{g}j{j}_supd"] = cnt["VE"]
                blk.vector(_)

            K = 4 * g + 4
            KP = K // 2
            for hp in range(H // 2):
                h0, h1 = 2 * hp, 2 * hp + 1
                pe0 = cnt["PE"]
                ac0 = cnt["AC"]
                ve0 = cnt["VE"]
                order = []
                for kt in range(min(4, K)):
                    order.append(("sc", h0, kt))
                    order.append(("sc", h1, kt))
                for p in range(2, KP):
                    for kt in (2 * p - 4, 2 * p - 3):
                        order.append(("av", h0, kt))
                        order.append(("av", h1, kt))
                    for kt in (2 * p, 2 * p + 1):
                        order.append(("sc", h0, kt))
                        order.append(("sc", h1, kt))
                for kt in (K - 4, K - 3, K - 2, K - 1):
                    order.append(("av", h0, kt))
                    order.append(("av", h1, kt))
                POS = {it[0] + str(it[1]) + "_" + str(it[2]): i + 1 for i, it in enumerate(order)}
                pe_sc_f = lambda h, kt, pe0=pe0, POS=POS: pe0 + POS[f"sc{h}_{kt}"]
                pe_av_f = lambda h, kt, pe0=pe0, POS=POS: pe0 + POS[f"av{h}_{kt}"]
                ac_ex_f = lambda h, p, ac0=ac0: ac0 + 2 * p + 1 + (h % 2)
                ve_mk_f = lambda h, kt, g=g, ve0=ve0: ve0 + 2 * (kt - 4 * g) + 1 + (h % 2)
                PB_Y = {h0: pb[2], h1: pb[6]}
                PB_D = {h0: pb[3], h1: pb[7]}
                PRS = {h0: 0, h1: 2048}

                def _(t, g=g, hp=hp, h0=h0, h1=h1, K=K, KP=KP, order=order):
                    if hp == 0:
                        t.wait_ge(sems["AC"], W[f"g{g}j3_oev{H-1}"])
                        t.wait_ge(sems["VE"], W[f"g{g}j3_pg{H-1}"])
                        t.wait_ge(sems["VE"], W[f"g{g}j3_supd"])
                    else:
                        t.wait_ge(sems["VE"], W[f"g{g}h{h0-2}_dev"])
                        t.wait_ge(sems["AC"], W[f"g{g}h{h0-2}_yev"])
                        t.wait_ge(sems["VE"], W[f"g{g}h{h1-2}_dev"])
                        t.wait_ge(sems["AC"], W[f"g{g}h{h1-2}_yev"])
                    for (kind, h, kt) in order:
                        kv = h // 4
                        p = kt // 2
                        if kind == "sc":
                            if p >= 1:
                                t.wait_ge(sems["AC"], ac_ex_f(h, p - 1))
                            bankpair = pbA if h % 2 == 0 else pbB
                            ins = t.matmul(bankpair[:, (kt % 2) * 512:(kt % 2) * 512 + 512],
                                           skrT[:, kv * SEQ + kt * 128: kv * SEQ + (kt + 1) * 128],
                                           stg[:, CSQR + h * 512: CSQR + (h + 1) * 512],
                                           start=True, stop=True)
                            ins.then_inc(sems["PE"], 1)
                            assert bump("PE") == pe_sc_f(h, kt)
                        else:
                            if kt >= 4 * g:
                                t.wait_ge(sems["VE"], ve_mk_f(h, kt))
                            else:
                                t.wait_ge(sems["AC"], ac_ex_f(h, p))
                            slot = PRS[h] + (p % 2) * 1024 + (kt % 2) * 512
                            t.matmul(PB_Y[h][:],
                                     vful[:, kv * 4096 + kt * 128: kv * 4096 + (kt + 1) * 128],
                                     Pring[:, slot: slot + 512],
                                     start=(kt == 0), stop=(kt == K - 1))
                            ins = t.matmul(PB_D[h][0:1, :], ONES_B,
                                           Pring[:, slot: slot + 512],
                                           start=(kt == 0), stop=(kt == K - 1))
                            ins.then_inc(sems["PE"], 1)
                            assert bump("PE") == pe_av_f(h, kt)
                blk.tensor(_)

                def _(sc, g=g, h0=h0, h1=h1, K=K, KP=KP):
                    for p in range(KP):
                        for h in (h0, h1):
                            sc.wait_ge(sems["PE"], pe_sc_f(h, 2 * p + 1))
                            if p >= 2:
                                sc.wait_ge(sems["PE"], pe_av_f(h, 2 * (p - 2) + 1))
                            bankpair = pbA if h % 2 == 0 else pbB
                            ins = sc.activation(Pring[:, PRS[h] + (p % 2) * 1024: PRS[h] + (p % 2) * 1024 + 1024],
                                                bankpair[:], AF.Exp, scale=float(HD) ** -0.5)
                            ins.then_inc(sems["AC"], 1)
                            assert bump("AC") == ac_ex_f(h, p)
                    for h in (h0, h1):
                        sc.wait_ge(sems["PE"], pe_av_f(h, K - 1))
                        ins = sc.copy(yraw[:, h * 512:(h + 1) * 512], PB_Y[h][:])
                        ins.then_inc(sems["AC"], 1)
                        W[f"g{g}h{h}_yev"] = bump("AC")
                blk.scalar(_)

                def _(v, g=g, h0=h0, h1=h1, K=K):
                    for kt in range(4 * g, K):
                        p = kt // 2
                        for h in (h0, h1):
                            slot = PRS[h] + (p % 2) * 1024 + (kt % 2) * 512
                            v.wait_ge(sems["AC"], ac_ex_f(h, p))
                            ins = v.tensor_mul(Pring[:, slot: slot + 512],
                                               Pring[:, slot: slot + 512],
                                               MASKDIAG(kt - 4 * g))
                            ins.then_inc(sems["VE"], 1)
                            assert bump("VE") == ve_mk_f(h, kt)
                    for h in (h0, h1):
                        v.wait_ge(sems["PE"], pe_av_f(h, K - 1))
                        ins = v.tensor_copy(hdall[0:1, h * 512:(h + 1) * 512], PB_D[h][0:1, :])
                        ins.then_inc(sems["VE"], 1)
                        W[f"g{g}h{h}_dev"] = bump("VE")
                blk.vector(_)

            def _(v, g=g):
                v.wait_ge(sems["AC"], W[f"g{g}h{H-1}_yev"])
                v.wait_ge(sems["VE"], W[f"g{g}h{H-1}_dev"])
                with nc.allow_low_precision(reason="bf16 softmax denominator reciprocal"):
                    ins = v.reciprocal(hdb[0:1, :], hdall[0:1, :])
                ins.then_inc(sems["VE"], 1)
                W[f"g{g}_hdb"] = bump("VE")
            blk.vector(_)

            ve_nm0 = cnt["VE"]

            def _(t, g=g, ve_nm0=ve_nm0):
                t.wait_ge(sems["VE"], W[f"g{g}_hdb"])
                for h in range(H):
                    if h >= 2:
                        t.wait_ge(sems["VE"], ve_nm0 + 2 * (h - 1))
                    ins = t.matmul(pb[6 + h % 2][:], HALFROW_B, hdb[0:1, h * 512:(h + 1) * 512],
                                   start=True, stop=True)
                    ins.then_inc(sems["PE"], 1)
                    W[f"g{g}nb{h}"] = bump("PE")
            blk.tensor(_)

            def _(v, g=g, ve_nm0=ve_nm0):
                for h in range(H):
                    v.wait_ge(sems["PE"], W[f"g{g}nb{h}"])
                    if g > 0 and h == 0:
                        v.wait_ge(sems["dX"], g_ship[g - 1])
                    ins = v.tensor_mul(yscr[:], yraw[:, h * 512:(h + 1) * 512], pb[6 + h % 2][:])
                    ins.then_inc(sems["VE"], 1)
                    bump("VE")
                    ins = v.tensor_add(oTb[:, h * 512:(h + 1) * 512], yscr[:],
                                       o05[:, h * 512:(h + 1) * 512])
                    ins.then_inc(sems["VE"], 1)
                    bump("VE")
            blk.vector(_)
            W[f"g{g}_oT"] = cnt["VE"]

            def _(sync, g=g):
                sync.wait_ge(sems["VE"], W[f"g{g}_oT"])
                ins = sync.dma_start(out=o_in[g].rearrange("(h p) r -> p h r", p=128),
                                     in_=oTb[:].rearrange("p (h r) -> p h r", h=H))
                ins.then_inc(sems["dX"], 16)
                g_ship[g] = bump("dX", 16)
            blk.sync(_)


            g_pe_end[g] = cnt["PE"]
            g_ve_end[g] = cnt["VE"]
            g_ac_end[g] = cnt["AC"]

        o_cc_base = cnt["CC"]

        o_out_flat = o_out.rearrange("g row c -> (g row) c")
        ve_p3 = cnt["VE"]
        pe_p3 = cnt["PE"]
        for c_ in range(NB):
            for dmt_ in range(16):
                W[f"wo_c{c_}_ev{dmt_}"] = ve_p3 + c_ * 16 + dmt_ + 1
                W[f"wo_c{c_}_m{dmt_}"] = pe_p3 + c_ * 16 + dmt_ + 1
            W[f"wo_c{c_}_pe_end"] = pe_p3 + (c_ + 1) * 16
        for c in range(NB):
            wt = f"wo_c{c}"

            def _(sync, c=c, wt=wt):
                for dmt in range(16):
                    if dmt >= 2:
                        sync.wait_ge(sems["VE"], W[wt + f"_ev{dmt-2}"])
                    elif c > 0:
                        sync.wait_ge(sems["VE"], W[f"wo_c{c-1}_ev{dmt+14}"])
                    ins = sync.dma_start(
                        out=x1r[:, (dmt % 2) * 512:(dmt % 2) * 512 + 512].rearrange("p (o r) -> p o r", o=1),
                        in_=x1T.rearrange("(dt p) r -> p dt r", p=128)[:, dmt:dmt + 1, c * 512:(c + 1) * 512])
                    ins.then_inc(sems["dX"], 16)
                    W[wt + f"_x1{dmt}"] = bump("dX", 16)
            blk.sync(_)

            def _(t, c=c, wt=wt):
                t.wait_ge(sems["dG"], W[wt + "_ost"])
                if c == 0:
                    t.wait_ge(sems["AC"], g_ac_end[NG - 1])
                    t.wait_ge(sems["VE"], g_ve_end[NG - 1])
                for dmt in range(16):
                    t.wait_ge(sems["dG"], W[wt + f"_wo{dmt}"])
                    if dmt >= 2:
                        t.wait_ge(sems["VE"], W[wt + f"_ev{dmt-2}"])
                    for ot in range(16):
                        ins = t.matmul(pb[dmt % 2][:],
                                       wob[:, (dmt % 2) * 2048 + ot * 128:(dmt % 2) * 2048 + (ot + 1) * 128],
                                       ost[:, ot * 512:(ot + 1) * 512],
                                       start=(ot == 0), stop=(ot == 15))
                    ins.then_inc(sems["PE"], 1)
                    assert bump("PE") == W[wt + f"_m{dmt}"]
                assert cnt["PE"] == W[wt + "_pe_end"]
            blk.tensor(_)

            def _(v, c=c, wt=wt):
                for dmt in range(16):
                    v.wait_ge(sems["PE"], W[wt + f"_m{dmt}"])
                    v.wait_ge(sems["dX"], W[wt + f"_x1{dmt}"])
                    if dmt >= 2:
                        v.wait_ge(sems["OUT"], W[wt + f"_st{dmt-2}"])
                    elif c > 0:
                        v.wait_ge(sems["OUT"], W[f"wo_c{c-1}_st{dmt+14}"])
                    ins = v.tensor_add(x2o[:, (dmt % 2) * 512:(dmt % 2) * 512 + 512],
                                       pb[dmt % 2][:], x1r[:, (dmt % 2) * 512:(dmt % 2) * 512 + 512])
                    ins.then_inc(sems["VE"], 1)
                    assert bump("VE") == W[wt + f"_ev{dmt}"]
            blk.vector(_)


        qkv_flat = qkv_out.rearrange("s r c -> (s r c)")
        o_flat1 = o_out.rearrange("g row c -> (g row c)")

        def _(gp):
            gp.wait_ge(sems["dX"], W["p_loads"])
            REG = [nc.values_load(offs_s[0:1, i:i + 1], engines=[PoolE],
                                  skip_runtime_bounds_check=True) for i in range(12)]
            for g in range(NG):
                slot, step = g // 4, g % 4
                gp.wait_ge(sems["CC"], cc_qkv_base + step + 1)
                gp.wait_ge(sems["PE"], g_pe_end[g - 1])
                gp.wait_ge(sems["VE"], g_ve_end[g - 1])
                gp.wait_ge(sems["AC"], g_ac_end[g - 1])
                gp.dma_start(
                    out=stg[:].rearrange("p (grp r) -> p grp r", r=512),
                    in_=qkv_flat[ds(REG[g], 26 * 128 * 512)]
                    .rearrange("(grp p r) -> p grp r", p=128, r=512)).then_inc(sems["dG"], 16)
                bump("dG", 16)
                assert cnt["dG"] == W[f"g{g}_staged"]
                if g > 0:
                    gp.wait_ge(sems["dX"], g_ship[g - 1])
                    gp.wait_ge(sems["CC"], cnt["CC"])
                    gp.collective_compute("AllGather", ALU.bypass,
                                          replica_groups=[[0, 1], [2, 3], [4, 5], [6, 7]],
                                          ins=[o_in[g - 1]], outs=[o_out[g - 1]]).then_inc(sems["CC"], 1)
                    assert bump("CC") == o_cc_base + g
            gp.wait_ge(sems["dX"], g_ship[NG - 1])
            gp.wait_ge(sems["CC"], cnt["CC"])
            gp.collective_compute("AllGather", ALU.bypass,
                                  replica_groups=[[0, 1], [2, 3], [4, 5], [6, 7]],
                                  ins=[o_in[NG - 1]], outs=[o_out[NG - 1]]).then_inc(sems["CC"], 1)
            assert bump("CC") == o_cc_base + NG

            for c in range(NB):
                wt = f"wo_c{c}"
                gp.wait_ge(sems["CC"], o_cc_base + min(5 + c, NG))
                if c > 0:
                    gp.wait_ge(sems["PE"], W[f"wo_c{c-1}_pe_end"])
                gp.dma_start(out=ost[:].rearrange("p (ot r) -> p ot r", ot=16),
                             in_=o_flat1[ds(REG[8 + c], 2048 * 512)].rearrange("(ot p r) -> p ot r", p=128, r=512)
                             ).then_inc(sems["dG"], 16)
                assert bump("dG", 16) == W[wt + "_ost"]
                for dmt in range(16):
                    if dmt >= 2:
                        gp.wait_ge(sems["PE"], W[wt + f"_m{dmt-2}"])
                        gp.wait_ge(sems["VE"], W[wt + f"_ev{dmt-2}"])
                        gp.dma_start(
                            out=x2T.rearrange("(dt p) r -> p dt r", p=128)[:, dmt - 2:dmt - 1, c * 512:(c + 1) * 512],
                            in_=x2o[:, (dmt % 2) * 512:(dmt % 2) * 512 + 512].rearrange("p (o r) -> p o r", o=1)
                        ).then_inc(sems["OUT"], 16)
                        assert bump("OUT", 16) == W[wt + f"_st{dmt-2}"]
                    gp.dma_start(out=wob[:, (dmt % 2) * 2048:(dmt % 2) * 2048 + 2048].rearrange("p (ot cc) -> p ot cc", ot=16),
                                 in_=wo_in[dmt].rearrange("ot p cc -> p ot cc")).then_inc(sems["dG"], 16)
                    assert bump("dG", 16) == W[wt + f"_wo{dmt}"]
                for dmt in range(14, 16):
                    gp.wait_ge(sems["VE"], W[wt + f"_ev{dmt}"])
                    gp.dma_start(
                        out=x2T.rearrange("(dt p) r -> p dt r", p=128)[:, dmt:dmt + 1, c * 512:(c + 1) * 512],
                        in_=x2o[:, (dmt % 2) * 512:(dmt % 2) * 512 + 512].rearrange("p (o r) -> p o r", o=1)
                    ).then_inc(sems["OUT"], 16)
                    assert bump("OUT", 16) == W[wt + f"_st{dmt}"]
        blk.gpsimd(_)

        rcv.close()
        return cnt["VE"], max(cnt["dX"], 0), cnt["OUT"]

    with ExitStack() as pes:
        phase_prologue(pes)

    def _(t):
        t.wait_ge(sems["CC"], W["p_wAG0"])
    blk.tensor(_)
    def _(sync):
        sync.wait_ge(sems["CC"], W["p_wAG0"])
    blk.sync(_)

    if N_SUBLAYERS == 1:
        with ExitStack() as fes:
            F = alloc_ffn_bufs(fes, "f1")
            phase_ffn("f1", F, xT_in, out_T, "OUT", free0=0)
        final_out = cnt["OUT"]
    else:
        with ExitStack() as fes:
            F = alloc_ffn_bufs(fes, "f1")
            phase_ffn("f1", F, xT_in, x1T, "dS", free0=0)
        f1_pe_end = cnt["PE"]
        f1_stored = [W[f"f1_b{b}_stored"] for b in range(NB)]
        at_ve_end, at_dx_end, at_out_end = phase_attn(W[f"f1_b{NB-1}_evacall"], f1_pe_end, f1_stored)

        def _(sync, at_out_end=at_out_end):
            sync.wait_ge(sems["OUT"], at_out_end)
        blk.sync(_)

        with ExitStack() as fes:
            F3 = alloc_ffn_bufs(fes, "f3")
            phase_ffn("f3", F3, x2T, out_T, "OUT", free0=at_ve_end, dx_gate=at_dx_end)
        final_out = cnt["OUT"]

    def _(sync, final_out=final_out):
        sync.wait_ge(sems["OUT"], final_out)
    blk.sync(_)

    blk.__exit__(None, None, None)
    es.close()
    return nc



def _host_prep(inputs):
    x = np.ascontiguousarray(np.asarray(inputs["L0_attn_hidden_states"], dtype=np.float32))
    Wq = np.asarray(inputs["L0_attn_Wq"], dtype=np.float32)
    Wk = np.asarray(inputs["L0_attn_Wk"], dtype=np.float32)
    Wv = np.asarray(inputs["L0_attn_Wv"], dtype=np.float32)
    Wo = np.asarray(inputs["L0_attn_Wo"], dtype=np.float32)
    w1 = np.asarray(inputs["L0_ffn_w1"], dtype=np.float32)
    b1 = np.asarray(inputs["L0_ffn_b1"], dtype=np.float32)
    w2 = np.asarray(inputs["L0_ffn_w2"], dtype=np.float32)
    b2 = np.asarray(inputs["L0_ffn_b2"], dtype=np.float32)
    import ml_dtypes
    bf = ml_dtypes.bfloat16

    w1t = np.ascontiguousarray(w1.reshape(FT, 128, NT, 128).transpose(0, 2, 3, 1))
    w2t = np.ascontiguousarray(w2.reshape(16, 128, FT, 128).transpose(0, 2, 3, 1))
    wqkv_cat = np.concatenate([Wq, Wk, Wv], axis=1)
    wqkvN = np.ascontiguousarray(wqkv_cat.reshape(NT, 128, 6, 512).transpose(2, 0, 1, 3))
    woT = np.ascontiguousarray(Wo.reshape(16, 128, 16, 128).transpose(2, 0, 1, 3))

    constf = np.zeros((128, 96), np.float32)
    constf[:, 0] = 1.0
    constf[:, 1:17] = b2.reshape(16, 128).T
    constf[:, 17:81] = b1.reshape(FT, 128).T
    constf[0, 82] = EPS

    constb = np.zeros((128, 5248), np.float32)
    constb[:, 0] = 1.0
    constb[:, 2:130] = np.eye(128)
    s_idx = np.arange(128)[:, None]
    q_idx512 = np.arange(512)[None, :]
    t_idx = np.arange(128)[None, :]
    constb[:, 130:258] = (t_idx >= s_idx).astype(np.float32)
    for j in range(4):
        constb[:, 258 + j * 512: 258 + (j + 1) * 512] = (q_idx512 >= (s_idx + 128 * j)).astype(np.float32)
    constb[0, 2306:2434] = 1.0
    constb[0, 2434:2946] = 1.0
    constb[0, 2946:4994] = 0.5 * b2
    constb[0, 4994:5122] = 0.5
    constb = constb.astype(bf)

    inv_freq = 1.0 / (10000.0 ** (np.arange(0, HD, 2, dtype=np.float64) / HD))
    pos = np.arange(SEQ, dtype=np.float64)
    fr = pos[:, None] * inv_freq[None, :]
    cosn = np.cos(fr).astype(np.float32).reshape(32, 128, 64).transpose(1, 0, 2).reshape(128, 2048)
    sinn = np.sin(fr).astype(np.float32).reshape(32, 128, 64).transpose(1, 0, 2).reshape(128, 2048)
    trig = np.concatenate([cosn, sinn], axis=1).astype(bf)

    per_core = []
    for c in range(8):
        b_, r = c // 2, c % 2
        xT = np.ascontiguousarray(x[b_, r * ROWS:(r + 1) * ROWS, :].T)
        offs = np.zeros((1, 16), np.int32)
        for g_ in range(8):
            slot_, step_ = g_ // 4, g_ % 4
            offs[0, g_] = (step_ * 13312 + slot_ * 6656 + r * 3328) * 512
        for lc in range(4):
            offs[0, 8 + lc] = ((4 * r + lc) * 2048) * 512
        per_core.append({
            "xT_in": xT,
            "w1s": np.ascontiguousarray(w1t[c * 8:(c + 1) * 8]),
            "w2s": np.ascontiguousarray(w2t[c * 2:(c + 1) * 2]),
            "wqkvN": wqkvN,
            "woT": woT,
            "constf": constf,
            "constb": constb,
            "trigN": trig,
            "offs": offs,
        })
    return per_core


def kernel(**inputs):
    per_core = _host_prep(inputs)
    nc = build_nc()
    res = run_bass_kernel_spmd(nc, per_core, core_ids=list(range(8)))
    out = np.empty((4, SEQ, DM), np.float32)
    for c in range(8):
        b_, r = c // 2, c % 2
        out[b_, r * ROWS:(r + 1) * ROWS, :] = res.results[c]["out_T"].T
    return out


# version 06
# speedup vs baseline: 1.0943x; 1.0203x over previous
import numpy as np
from contextlib import ExitStack

import concourse.bass as bass
import concourse.mybir as mybir
from concourse.bass_utils import run_bass_kernel_spmd

F32 = mybir.dt.float32
BF16 = mybir.dt.bfloat16
I32 = mybir.dt.int32
AF = mybir.ActivationFunctionType
ALU = mybir.AluOpType
AX = mybir.AxisListType
ds = bass.ds

DM = 2048
DFF = 8192
NT = DM // 128
FT = DFF // 128
ROWS = 2048
NB = 4
BLKR = 512
SEQ = 4096
NG = 8
H = 8
KVH = 2
HD = 128
EPS = 1e-5

N_SUBLAYERS = 3


def build_nc():
    nc = bass.Bass("TRN2", target_bir_lowering=False, detect_race_conditions=False)

    xT_in = nc.dram_tensor("xT_in", [DM, ROWS], F32, kind="ExternalInput").ap()
    w1s_in = nc.dram_tensor("w1s", [8, NT, 128, 128], F32, kind="ExternalInput").ap()
    w2s_in = nc.dram_tensor("w2s", [2, FT, 128, 128], F32, kind="ExternalInput").ap()
    wqkv_in = nc.dram_tensor("wqkvN", [6, NT, 128, 512], F32, kind="ExternalInput").ap()
    wo_in = nc.dram_tensor("woT", [16, 16, 128, 128], F32, kind="ExternalInput").ap()
    constf_in = nc.dram_tensor("constf", [128, 96], F32, kind="ExternalInput").ap()
    constb_in = nc.dram_tensor("constb", [128, 5248], BF16, kind="ExternalInput").ap()
    trig_in = nc.dram_tensor("trigN", [128, 4096], BF16, kind="ExternalInput").ap()
    offs_in = nc.dram_tensor("offs", [1, 16], I32, kind="ExternalInput").ap()

    out_T = nc.dram_tensor("out_T", [DM, ROWS], F32, kind="ExternalOutput").ap()

    x1T = nc.dram_tensor("x1T", [DM, ROWS], F32, kind="Internal").ap()
    x2T = nc.dram_tensor("x2T", [DM, ROWS], F32, kind="Internal").ap()
    sc_loc = nc.dram_tensor("sc_loc", [128, 128], F32, kind="Internal").ap()
    sc_all = nc.dram_tensor("sc_all", [1024, 128], F32, kind="Internal").ap()
    w1q_loc = nc.dram_tensor("w1q_loc", [8 * NT * 128, 128], BF16, kind="Internal").ap()
    w1q_all = nc.dram_tensor("w1q_all", [FT * NT * 128, 128], BF16, kind="Internal").ap()
    w2q_loc = nc.dram_tensor("w2q_loc", [2 * FT * 128, 128], BF16, kind="Internal").ap()
    w2q_all = nc.dram_tensor("w2q_all", [16 * FT * 128, 128], BF16, kind="Internal").ap()
    qkv_in = nc.dram_tensor("qkv_in", [4, 6656, 512], BF16, kind="Internal").ap()
    qkv_out = nc.dram_tensor("qkv_out", [4, 13312, 512], BF16, kind="Internal").ap()
    o_in = nc.dram_tensor("o_in", [NG, 1024, 512], BF16, kind="Internal").ap()
    o_out = nc.dram_tensor("o_out", [NG, 2048, 512], BF16, kind="Internal").ap()

    es = ExitStack()

    def SBp(name, shape, dt):
        return es.enter_context(nc.sbuf_tensor(name, shape, dt))

    constf = SBp("constf_s", [128, 96], F32)
    constb = SBp("constb_s", [128, 5248], BF16)
    trig = SBp("trig_s", [128, 4096], BF16)
    offs_s = SBp("offs_s", [1, 16], I32)
    onesrow_f = SBp("onesrow_f", [1, 128], F32)
    thr_s = SBp("thr_s", [1, 4], F32)
    thrv = SBp("thrv", [128, 2], F32)
    nthrv = SBp("nthrv", [128, 2], F32)
    stat = SBp("stat", [1, 2048], F32)
    statb = SBp("statb", [1, 1024], BF16)

    pbA = es.enter_context(nc.psum_tensor("pbA", [128, 1024], F32))
    pb2_t = es.enter_context(nc.psum_tensor("pb2", [128, 512], F32))
    pb3_t = es.enter_context(nc.psum_tensor("pb3", [128, 512], F32))
    pbB = es.enter_context(nc.psum_tensor("pbB", [128, 1024], F32))
    pb6_t = es.enter_context(nc.psum_tensor("pb6", [128, 512], F32))
    pb7_t = es.enter_context(nc.psum_tensor("pb7", [128, 512], F32))
    pb = [pbA[:, 0:512], pbA[:, 512:1024], pb2_t.ap(), pb3_t.ap(),
          pbB[:, 0:512], pbB[:, 512:1024], pb6_t.ap(), pb7_t.ap()]

    sems = {}
    cnt = {}
    for s in ["dW", "dX", "dG", "dS", "CC", "PE", "AC", "VE", "OUT"]:
        sems[s] = es.enter_context(nc.semaphore(s))
        cnt[s] = 0

    def bump(s, by=1):
        cnt[s] += by
        return cnt[s]

    blk = nc.Block()
    blk.__enter__()

    W = {}

    ONES_F = constf[:, 0:1]
    B2COL = lambda dmt: constf[:, 1 + dmt: 2 + dmt]
    B1COL = lambda ft: constf[:, 17 + ft: 18 + ft]
    ONES_B = constb[:, 0:1]
    IDENT = constb[:, 2:130]
    MASKGLA = constb[:, 130:258]
    MASKDIAG = lambda j: constb[:, 258 + j * 512: 258 + (j + 1) * 512]
    ONESROW_B = constb[0:1, 2306:2434]
    HALFROW_B = constb[0:1, 4994:5122]
    ONESROW512_B = constb[0:1, 2434:2946]
    B2ROW = lambda dmt: constb[0:1, 2946 + dmt * 128: 2946 + (dmt + 1) * 128]

    def phase_prologue(pes):
        wslab = pes.enter_context(nc.sbuf_tensor("wslab", [128, 16384], F32))
        wslab2 = pes.enter_context(nc.sbuf_tensor("wslab2", [128, 16384], F32))
        wqslab = pes.enter_context(nc.sbuf_tensor("wqslab", [128, 16384], BF16))
        tmpq = pes.enter_context(nc.sbuf_tensor("tmpq", [128, 8192], BF16))
        sc_s = pes.enter_context(nc.sbuf_tensor("sc_s", [128, 128], F32))
        sc8 = pes.enter_context(nc.sbuf_tensor("sc8", [128, 16], F32))
        scacc = pes.enter_context(nc.sbuf_tensor("scacc", [128, 2], F32))

        def _(sync):
            sync.dma_start(out=constf[:], in_=constf_in).then_inc(sems["dX"], 16)
            sync.dma_start(out=constb[:], in_=constb_in).then_inc(sems["dX"], 16)
            sync.dma_start(out=trig[:], in_=trig_in).then_inc(sems["dX"], 16)
            sync.dma_start(out=offs_s[:], in_=offs_in).then_inc(sems["dX"], 16)
            sync.dma_start(out=wslab[:].rearrange("p (f d c) -> p f d c", f=8, d=NT),
                           in_=w1s_in.rearrange("f d p c -> p f d c")).then_inc(sems["dX"], 16)
        blk.sync(_)
        bump("dX", 16 * 5)
        W["p_loads"] = cnt["dX"]

        def _(sync):
            sync.dma_start(out=wslab2[:].rearrange("p (f d c) -> p f d c", f=2, d=FT),
                           in_=w2s_in.rearrange("f d p c -> p f d c")).then_inc(sems["dX"], 16)
        blk.sync(_)
        W["p_w2load"] = bump("dX", 16)

        def _(v):
            v.wait_ge(sems["dX"], W["p_loads"])
            v.memset(onesrow_f[:], 1.0)
            v.memset(sc_s[:], 0.0)
            v.reduce_sum(sc_s[:, 0:1], wslab[:], AX.X, apply_absolute_value=True)
            v.wait_ge(sems["dX"], W["p_w2load"])
            ins = v.reduce_sum(sc_s[:, 1:2], wslab2[:], AX.X, apply_absolute_value=True)
            ins.then_inc(sems["VE"], 1)
        blk.vector(_)
        W["p_red"] = bump("VE")

        def _(sync):
            sync.wait_ge(sems["VE"], W["p_red"])
            sync.dma_start(out=sc_loc, in_=sc_s[:]).then_inc(sems["dX"], 16)
        blk.sync(_)
        W["p_scst"] = bump("dX", 16)

        def _(g):
            g.wait_ge(sems["dX"], W["p_scst"])
            g.collective_compute("AllGather", ALU.bypass, replica_groups=[list(range(8))],
                                 ins=[sc_loc], outs=[sc_all]).then_inc(sems["CC"], 1)
        blk.gpsimd(_)
        W["p_scAG"] = bump("CC")

        def _(sync):
            sync.wait_ge(sems["CC"], W["p_scAG"])
            sync.dma_start(out=sc8[:].rearrange("p (k c) -> p k c", k=8),
                           in_=sc_all[:, 0:2].rearrange("(k p) c -> p k c", p=128)).then_inc(sems["dX"], 16)
        blk.sync(_)
        W["p_sc8"] = bump("dX", 16)

        def _(v):
            v.wait_ge(sems["dX"], W["p_sc8"])
            v.reduce_sum(scacc[:], sc8[:].rearrange("p (k c) -> p c k", k=8), AX.X).then_inc(sems["VE"], 1)
        blk.vector(_)
        W["p_scsum"] = bump("VE")

        def _(t):
            t.wait_ge(sems["VE"], W["p_scsum"])
            t.matmul(pb[6][0:1, 0:2], ONES_F, scacc[:], start=True, stop=True).then_inc(sems["PE"], 1)
        blk.tensor(_)
        W["p_tot"] = bump("PE")

        def _(sc):
            sc.wait_ge(sems["PE"], W["p_tot"])
            sc.mul(thr_s[0:1, 0:2], pb[6][0:1, 0:2], 0.7 / (DFF * DM)).then_inc(sems["AC"], 1)
        blk.scalar(_)
        W["p_thr"] = bump("AC")

        def _(t):
            t.wait_ge(sems["AC"], W["p_thr"])
            t.matmul(pb[6][:, 0:1], onesrow_f[:], thr_s[0:1, 0:1], start=True, stop=True)
            t.matmul(pb[6][:, 1:2], onesrow_f[:], thr_s[0:1, 1:2], start=True, stop=True).then_inc(sems["PE"], 1)
        blk.tensor(_)
        W["p_thrb"] = bump("PE")

        def _(sc):
            sc.wait_ge(sems["PE"], W["p_thrb"])
            sc.copy(thrv[:], pb[6][:, 0:2])
            sc.mul(nthrv[:], pb[6][:, 0:2], -1.0).then_inc(sems["AC"], 1)
        blk.scalar(_)
        W["p_thrv"] = bump("AC")

        for wi in range(2):
            slab = wslab if wi == 0 else wslab2
            wq_loc_t = w1q_loc if wi == 0 else w2q_loc
            wq_all_t = w1q_all if wi == 0 else w2q_all
            dpar = NT if wi == 0 else FT

            def _(v, wi=wi, slab=slab):
                v.wait_ge(sems["AC"], W["p_thrv"])
                if wi == 1:
                    v.wait_ge(sems["dX"], W["p_qst0"])
                for hh in range(2):
                    sl = slice(hh * 8192, (hh + 1) * 8192)
                    v.tensor_scalar(tmpq[:], slab[:, sl], nthrv[:, wi:wi + 1], None, ALU.is_le)
                    ins = v.scalar_tensor_tensor(wqslab[:, sl], slab[:, sl], thrv[:, wi:wi + 1],
                                                 tmpq[:], ALU.is_ge, ALU.subtract)
                ins.then_inc(sems["VE"], 1)
            blk.vector(_)
            W[f"p_q{wi}"] = bump("VE")

            def _(sync, wi=wi, wq_loc_t=wq_loc_t, dpar=dpar):
                sync.wait_ge(sems["VE"], W[f"p_q{wi}"])
                sync.dma_start(out=wq_loc_t.rearrange("(f d p) c -> p f d c", p=128, d=dpar),
                               in_=wqslab[:].rearrange("p (f d c) -> p f d c", d=dpar, c=128)).then_inc(sems["dX"], 16)
            blk.sync(_)
            W[f"p_qst{wi}"] = bump("dX", 16)

            def _(g, wi=wi, wq_loc_t=wq_loc_t, wq_all_t=wq_all_t):
                g.wait_ge(sems["dX"], W[f"p_qst{wi}"])
                g.wait_ge(sems["CC"], cnt["CC"])
                g.collective_compute("AllGather", ALU.bypass, replica_groups=[list(range(8))],
                                     ins=[wq_loc_t], outs=[wq_all_t]).then_inc(sems["CC"], 1)
            blk.gpsimd(_)
            W[f"p_wAG{wi}"] = bump("CC")

    def emit_ln(tag, F, free_after, pre_bf16=False, xc_sem="dX", direct=False):
        xc, xcb, xsq, xn, scr = F["xc"], F["xcb"], F["xsq"], F["xn"], F["scr"]

        if not direct:
            def _(v):
                v.wait_ge(sems[xc_sem], W[tag + "_xc"])
                v.wait_ge(sems["VE"], free_after)
                if pre_bf16:
                    v.engine_nop().then_inc(sems["VE"], 1)
                else:
                    v.tensor_copy(xcb[:, 0:NT * BLKR], xc[:]).then_inc(sems["VE"], 1)
            blk.vector(_)
            ve_cast = bump("VE")

        def _(sc):
            if direct:
                sc.wait_ge(sems[xc_sem], W[tag + "_xc"])
            else:
                sc.wait_ge(sems["VE"], ve_cast)
            sc.activation(xsq[:], xc[:], AF.Square).then_inc(sems["AC"], 1)
        blk.scalar(_)
        ac_sq = bump("AC")

        def _(t):
            t.wait_ge(sems["VE"], free_after if direct else ve_cast)
            t.wait_ge(sems["AC"], ac_sq)
            for dt in range(NT):
                t.matmul(pb[4][0:1, :], ONES_B, xcb[:, dt * 512:(dt + 1) * 512],
                         start=(dt == 0), stop=(dt == NT - 1))
            for dt in range(NT):
                ins = t.matmul(pb[5][0:1, :], ONES_B, xsq[:, dt * 512:(dt + 1) * 512],
                               start=(dt == 0), stop=(dt == NT - 1))
            ins.then_inc(sems["PE"], 1)
        blk.tensor(_)
        pe_sums = bump("PE")

        def _(sc):
            sc.wait_ge(sems["PE"], pe_sums)
            sc.mul(stat[0:1, 0:512], pb[4][0:1, :], -1.0 / DM)
            sc.mul(stat[0:1, 512:1024], pb[5][0:1, :], 1.0 / DM).then_inc(sems["AC"], 1)
        blk.scalar(_)
        ac_stat = bump("AC")

        def _(v):
            v.wait_ge(sems["AC"], ac_stat)
            v.tensor_mul(stat[0:1, 1024:1536], stat[0:1, 0:512], stat[0:1, 0:512])
            v.tensor_sub(stat[0:1, 1536:2048], stat[0:1, 512:1024], stat[0:1, 1024:1536]).then_inc(sems["VE"], 1)
        blk.vector(_)
        ve_var = bump("VE")

        def _(sc):
            sc.wait_ge(sems["VE"], ve_var)
            sc.activation(stat[0:1, 1024:1536], stat[0:1, 1536:2048], AF.Ln, bias=constf[0:1, 82:83])
            sc.activation(stat[0:1, 1536:2048], stat[0:1, 1024:1536], AF.Exp, scale=-0.5)
            sc.copy(statb[0:1, 0:512], stat[0:1, 0:512])
            sc.copy(statb[0:1, 512:1024], stat[0:1, 1536:2048]).then_inc(sems["AC"], 1)
        blk.scalar(_)
        ac_rstd = bump("AC")

        def _(t):
            t.wait_ge(sems["AC"], ac_rstd)
            t.matmul(pb[4][:], ONESROW_B, statb[0:1, 0:512], start=True, stop=True)
            t.matmul(pb[5][:], ONESROW_B, statb[0:1, 512:1024], start=True, stop=True).then_inc(sems["PE"], 1)
        blk.tensor(_)
        pe_bc = bump("PE")

        def _(v):
            v.wait_ge(sems["PE"], pe_bc)
            for dt in range(NT):
                v.tensor_add(scr[:], xc[:, dt * 512:(dt + 1) * 512], pb[4][:])
                ins = v.tensor_mul(xn[:, dt * 512:(dt + 1) * 512], scr[:], pb[5][:])
            ins.then_inc(sems["VE"], 1)
        blk.vector(_)
        W[tag + "_apply"] = bump("VE")

    def phase_ffn(name, F, xin, xout, out_sem, free0, dx_gate=0):
        def emit_xload(b):
            def _(g, b=b, xin=xin, F=F, name=name, dx_gate=dx_gate if b == 0 else None):
                g.wait_ge(sems["VE"], W.get(f"{name}_b{b-1}_apply", free0))
                if dx_gate:
                    g.wait_ge(sems[dx_gate[0]], dx_gate[1])
                g.dma_start(
                    out=F["xcb"][:].rearrange("p (dt r) -> p dt r", dt=NT),
                    in_=xin.rearrange("(dt p) r -> p dt r", p=128)[:, :, ds(b * BLKR, BLKR)],
                ).then_inc(sems[out_sem], 16)
            blk.gpsimd(_)
            W[f"{name}_b{b}_xc"] = bump(out_sem, 16)

        emit_xload(0)
        for b in range(NB):
            tag = f"{name}_b{b}"
            prev_free = W.get(f"{name}_b{b-1}_evacall", free0)

            emit_ln(tag, F, W.get(f"{name}_b{b-1}_apply", free0),
                    pre_bf16=True, xc_sem=out_sem, direct=True)

            gelu_base = cnt["AC"]
            dw_w1_base = cnt["dW"]

            def _(sync, gelu_base=gelu_base, F=F):
                for ft in range(FT):
                    if ft >= 2:
                        sync.wait_ge(sems["AC"], gelu_base + ft - 1)
                    sync.dma_start(
                        out=F["w1b"][:, (ft % 2) * DM:(ft % 2) * DM + DM].rearrange("p (d c) -> p d c", c=128),
                        in_=w1q_all[ft * DM:(ft + 1) * DM, :].rearrange("(d p) c -> p d c", p=128),
                    ).then_inc(sems["dW"], 16)
            blk.sync(_)
            bump("dW", 16 * FT)

            pe_h_base = cnt["PE"]

            def _(t, tag=tag, gelu_base=gelu_base, dw_w1_base=dw_w1_base, F=F):
                t.wait_ge(sems["VE"], W[tag + "_apply"])
                for ft in range(FT):
                    t.wait_ge(sems["dW"], dw_w1_base + (ft + 1) * 16)
                    if ft >= 2:
                        t.wait_ge(sems["AC"], gelu_base + ft - 1)
                    for dt in range(NT):
                        ins = t.matmul(pb[ft % 2][:],
                                       F["w1b"][:, (ft % 2) * DM + dt * 128:(ft % 2) * DM + (dt + 1) * 128],
                                       F["xn"][:, dt * 512:(dt + 1) * 512],
                                       start=(dt == 0), stop=(dt == NT - 1))
                    ins.then_inc(sems["PE"], 1)
            blk.tensor(_)
            bump("PE", FT)

            def _(sc, pe_h_base=pe_h_base, F=F):
                for ft in range(FT):
                    sc.wait_ge(sems["PE"], pe_h_base + ft + 1)
                    sc.activation(F["hT"][:, ft * 512:(ft + 1) * 512], pb[ft % 2][:],
                                  AF.Gelu, bias=B1COL(ft)).then_inc(sems["AC"], 1)
            blk.scalar(_)
            bump("AC", FT)
            ac_gelu_all = cnt["AC"]
            assert ac_gelu_all == gelu_base + FT

            evac_base = cnt["VE"]
            dw_w2_base = cnt["dW"]
            prev_evacall = W.get(f"{name}_b{b-1}_evacall", 0)

            dxr_base = cnt["dX"]

            def _(sync, evac_base=evac_base, F=F, b=b, name=name, xin=xin):
                if b == 0 and name == "f1":
                    sync.wait_ge(sems["CC"], W["p_wAG1"])
                for dmt in range(16):
                    if dmt >= 2:
                        sync.wait_ge(sems["VE"], evac_base + dmt - 1)
                    sync.dma_start(
                        out=F["xr"][:, (dmt % 2) * 512:(dmt % 2) * 512 + 512].rearrange("p (o r) -> p o r", o=1),
                        in_=xin.rearrange("(dt p) r -> p dt r", p=128)[:, dmt:dmt + 1, b * BLKR:(b + 1) * BLKR],
                    ).then_inc(sems["dX"], 16)
                    sync.dma_start(
                        out=F["w2b"][:, (dmt % 2) * DFF:(dmt % 2) * DFF + DFF].rearrange("p (f c) -> p f c", c=128),
                        in_=w2q_all[dmt * DFF:(dmt + 1) * DFF, :].rearrange("(f p) c -> p f c", p=128),
                    ).then_inc(sems["dW"], 16)
            blk.sync(_)
            bump("dW", 16 * 16)
            bump("dX", 16 * 16)

            pe_o_base = cnt["PE"]

            def _(t, ac_gelu_all=ac_gelu_all, evac_base=evac_base,
                  dw_w2_base=dw_w2_base, prev_evacall=prev_evacall, F=F):
                t.wait_ge(sems["AC"], ac_gelu_all)
                if prev_evacall:
                    t.wait_ge(sems["VE"], prev_evacall)
                for dmt in range(16):
                    t.wait_ge(sems["dW"], dw_w2_base + (dmt + 1) * 16)
                    if dmt >= 2:
                        t.wait_ge(sems["VE"], evac_base + dmt - 1)
                    for ft in range(FT):
                        t.matmul(pb[2 + dmt % 2][:],
                                 F["w2b"][:, (dmt % 2) * DFF + ft * 128:(dmt % 2) * DFF + (ft + 1) * 128],
                                 F["hT"][:, ft * 512:(ft + 1) * 512],
                                 start=(ft == 0), stop=False)
                    ins = t.matmul(pb[2 + dmt % 2][:], B2ROW(dmt), ONESROW512_B,
                                   start=False, stop=True)
                    ins.then_inc(sems["PE"], 1)
            blk.tensor(_)
            bump("PE", 16)

            xst_base = cnt[out_sem]

            def _(v, pe_o_base=pe_o_base, xst_base=xst_base, out_sem=out_sem, F=F, dxr_base=dxr_base):
                for dmt in range(16):
                    v.wait_ge(sems["PE"], pe_o_base + dmt + 1)
                    v.wait_ge(sems["dX"], dxr_base + (dmt + 1) * 16)
                    if dmt >= 2:
                        v.wait_ge(sems[out_sem], xst_base + (dmt - 1) * 16)
                    v.scalar_tensor_tensor(F["xo"][:, (dmt % 2) * 512:(dmt % 2) * 512 + 512],
                                           pb[2 + dmt % 2][:], 0.5,
                                           F["xr"][:, (dmt % 2) * 512:(dmt % 2) * 512 + 512],
                                           ALU.mult, ALU.add).then_inc(sems["VE"], 1)
            blk.vector(_)
            bump("VE", 16)
            W[tag + "_evacall"] = cnt["VE"]

            if b + 1 < NB:
                emit_xload(b + 1)

            def _(g, b=b, evac_base=evac_base, xout=xout, out_sem=out_sem, F=F):
                for dmt in range(16):
                    g.wait_ge(sems["VE"], evac_base + dmt + 1)
                    g.dma_start(
                        out=xout.rearrange("(dt p) r -> p dt r", p=128)[:, dmt:dmt + 1, ds(b * BLKR, BLKR)],
                        in_=F["xo"][:, (dmt % 2) * 512:(dmt % 2) * 512 + 512].rearrange("p (o r) -> p o r", o=1),
                    ).then_inc(sems[out_sem], 16)
            blk.gpsimd(_)
            bump(out_sem, 16 * 16)
            W[tag + "_stored"] = cnt[out_sem]
        return W[f"{name}_b{NB-1}_evacall"]

    def alloc_ffn_bufs(stack, pfx):
        F = {}
        F["xcb"] = stack.enter_context(nc.sbuf_tensor(pfx + "xcb", [128, NT * BLKR], BF16))
        F["xc"] = F["xcb"]
        F["xn"] = stack.enter_context(nc.sbuf_tensor(pfx + "xn", [128, NT * BLKR], BF16))
        F["hT"] = stack.enter_context(nc.sbuf_tensor(pfx + "hT", [128, FT * BLKR], BF16))
        F["w1b"] = stack.enter_context(nc.sbuf_tensor(pfx + "w1b", [128, 2 * DM], BF16))
        F["w2b"] = stack.enter_context(nc.sbuf_tensor(pfx + "w2b", [128, 2 * DFF], BF16))
        F["scr"] = stack.enter_context(nc.sbuf_tensor(pfx + "scr", [128, 512], F32))
        F["scr2"] = stack.enter_context(nc.sbuf_tensor(pfx + "scr2", [128, 2 * 512], F32))
        F["xo"] = stack.enter_context(nc.sbuf_tensor(pfx + "xo", [128, 2 * 512], F32))
        F["xr"] = stack.enter_context(nc.sbuf_tensor(pfx + "xr", [128, 2 * 512], F32))
        F["xsq"] = F["xn"]
        return F



    BLK_SHIP = 3328
    BSQR, BQS, BSKR, BKS, BGL, BV = 0, 1024, 2048, 2304, 2560, 2816
    CSQR, CQS, CSKR, CKS, CGL, CV = 0, 4096, 8192, 9216, 10240, 11264
    PoolE = mybir.EngineType.Pool

    def phase_attn(free0_ve, pe_free, dx_x1_stored):
        cc_qkv_base = cnt["CC"]

        snd = ExitStack()
        SA = lambda n, s, d: snd.enter_context(nc.sbuf_tensor(n, s, d))
        axc = SA("axc", [128, NT * BLKR], BF16)
        axn = SA("axn", [128, NT * BLKR], BF16)
        ascr = SA("ascr", [128, 512], F32)
        wnb = SA("wnb", [128, 2 * NT * 512], BF16)
        qnat = SA("qnat", [128, 4 * 2048], BF16)
        rnat = SA("rnat", [128, 4 * 2048], BF16)
        qsn = SA("qsn", [128, 4 * 2048], BF16)
        eqb = SA("eqb", [128, 4 * 2048], BF16)
        rsc = SA("rsc", [128, 4096], BF16)
        srec = SA("srec", [128, 128], F32)
        knat = SA("knat", [128, 4 * 512], BF16)
        krn = SA("krn", [128, 4 * 512], BF16)
        ksn = SA("ksn", [128, 4 * 512], BF16)
        ekb = SA("ekb", [128, 4 * 512], BF16)
        ktrb = SA("ktrb", [128, 4 * 512], BF16)
        glb = SA("glb", [128, 4 * 512], BF16)
        dsc = SA("dsc", [128, 2048], F32)
        shs = SA("shs", [128, 4 * 512], BF16)
        AF_ = {"xc": axc, "xcb": axc, "xsq": axn, "xn": axn, "scr": ascr}
        qrecv = SA("qrecv", [128, 80], F32)

        for c in range(NB):
            tag = f"at_c{c}"
            prev_apply = W.get(f"at_c{c-1}_apply", free0_ve)
            prev_done = W.get(f"at_c{c-1}_shipdone", 0)

            def _(g, c=c, prev_apply=prev_apply, prev_done=prev_done):
                g.wait_ge(sems["dS"], dx_x1_stored[c])
                g.wait_ge(sems["VE"], prev_apply)
                if c == 0:
                    g.wait_ge(sems["PE"], pe_free)
                if prev_done:
                    g.wait_ge(sems["dX"], prev_done)
                ins = g.dma_start(
                    out=axc[:].rearrange("p (dt r) -> p dt r", dt=NT),
                    in_=x1T.rearrange("(dt p) r -> p dt r", p=128)[:, :, ds(c * BLKR, BLKR)],
                )
                ins.then_inc(sems["dG"], 16)
                W[f"at_c{c}_xc"] = bump("dG", 16)
            blk.gpsimd(_)

            emit_ln(tag, AF_, prev_apply, pre_bf16=True, xc_sem="dG")

            pe_proj_base = cnt["PE"]
            ac_ev_base = cnt["AC"]
            dg_wn = []

            def _(g, pe_proj_base=pe_proj_base):
                for og in range(6):
                    if og >= 2:
                        g.wait_ge(sems["PE"], pe_proj_base + (og - 1) * 4)
                    ins = g.dma_start(
                        out=wnb[:, (og % 2) * 8192:(og % 2) * 8192 + 8192].rearrange("p (dt c) -> p dt c", dt=NT),
                        in_=wqkv_in[og].rearrange("dt p c -> p dt c"),
                    )
                    ins.then_inc(sems["dG"], 16)
                    dg_wn.append(bump("dG", 16))
            blk.gpsimd(_)

            def _(t, tag=tag, ac_ev_base=ac_ev_base):
                t.wait_ge(sems["VE"], W[tag + "_apply"])
                for og in range(6):
                    t.wait_ge(sems["dG"], dg_wn[og])
                    for rs in range(4):
                        i = og * 4 + rs
                        if i >= 2:
                            t.wait_ge(sems["AC"], ac_ev_base + i - 1)
                        for dt in range(NT):
                            ins = t.matmul(pb[i % 2][:],
                                           axn[:, dt * 512 + rs * 128: dt * 512 + (rs + 1) * 128],
                                           wnb[:, (og % 2) * 8192 + dt * 512:(og % 2) * 8192 + (dt + 1) * 512],
                                           start=(dt == 0), stop=(dt == NT - 1))
                        ins.then_inc(sems["PE"], 1)
                        bump("PE")
            blk.tensor(_)

            dx_vship_base = cnt["dX"]
            ac_evs = []

            def _(sc, pe_proj_base=pe_proj_base, dx_vship_base=dx_vship_base):
                for og in range(6):
                    for rs in range(4):
                        i = og * 4 + rs
                        sc.wait_ge(sems["PE"], pe_proj_base + i + 1)
                        if og < 4:
                            ins = sc.copy(qnat[:, rs * 2048 + og * 512: rs * 2048 + (og + 1) * 512], pb[i % 2][:])
                        elif og == 4:
                            ins = sc.copy(knat[:, rs * 512:(rs + 1) * 512], pb[i % 2][:])
                        else:
                            if rs >= 2:
                                sc.wait_ge(sems["dX"], dx_vship_base + (rs - 1) * 16)
                            ins = sc.copy(shs[:, (rs % 2) * 512:(rs % 2) * 512 + 512], pb[i % 2][:])
                        ins.then_inc(sems["AC"], 1)
                        ac_evs.append(bump("AC"))
            blk.scalar(_)

            def _(sync, c=c):
                for rs in range(4):
                    sync.wait_ge(sems["AC"], ac_evs[20 + rs])
                    for b_ in range(2):
                        row = b_ * BLK_SHIP + BV + rs * 128
                        ins = sync.dma_start(
                            out=qkv_in[c][row: row + 128, 0:256],
                            in_=shs[:, (rs % 2) * 512 + b_ * 256:(rs % 2) * 512 + b_ * 256 + 256])
                        ins.then_inc(sems["dX"], 16)
                        bump("dX", 16)
                        ins = sync.dma_start(
                            out=qkv_in[c][row: row + 128, 256:512],
                            in_=shs[:, (rs % 2) * 512 + (1 - b_) * 256:(rs % 2) * 512 + (1 - b_) * 256 + 256])
                        ins.then_inc(sems["dX"], 16)
                        bump("dX", 16)
            blk.sync(_)

            def _(sc):
                sc.wait_ge(sems["AC"], ac_evs[15])
                ins = sc.activation(eqb[:], qnat[:], AF.Exp)
                ins.then_inc(sems["AC"], 1)
                W[tag + "_eq"] = bump("AC")
                sc.wait_ge(sems["AC"], ac_evs[19])
                ins = sc.activation(ekb[:], knat[:], AF.Exp)
                ins.then_inc(sems["AC"], 1)
                W[tag + "_ek"] = bump("AC")
            blk.scalar(_)

            def _(v, c=c, tag=tag):
                v.wait_ge(sems["AC"], ac_evs[19])
                for rs in range(4):
                    tile = c * 4 + rs
                    cosNq = trig[:, tile * 64: tile * 64 + 64].rearrange("p (o d) -> p o d", o=1).broadcast_to([128, 16, 64])
                    sinNq = trig[:, 2048 + tile * 64: 2048 + tile * 64 + 64].rearrange("p (o d) -> p o d", o=1).broadcast_to([128, 16, 64])
                    qv = qnat[:, rs * 2048:(rs + 1) * 2048].rearrange("p (h d) -> p h d", d=128)
                    rv = rnat[:, rs * 2048:(rs + 1) * 2048].rearrange("p (h d) -> p h d", d=128)
                    s0 = rsc[:, 0:1024].rearrange("p (h d) -> p h d", d=64)
                    s1 = rsc[:, 1024:2048].rearrange("p (h d) -> p h d", d=64)
                    s2 = rsc[:, 2048:3072].rearrange("p (h d) -> p h d", d=64)
                    s3 = rsc[:, 3072:4096].rearrange("p (h d) -> p h d", d=64)
                    v.tensor_mul(s0, qv[:, :, 0:64], cosNq)
                    v.tensor_mul(s1, qv[:, :, 64:128], sinNq)
                    v.tensor_mul(s2, qv[:, :, 64:128], cosNq)
                    ins = v.tensor_mul(s3, qv[:, :, 0:64], sinNq)
                    ins.then_inc(sems["VE"], 1)
                    v.wait_ge(sems["VE"], bump("VE"))
                    v.tensor_sub(rv[:, :, 0:64], s0, s1)
                    v.tensor_add(rv[:, :, 64:128], s2, s3)
                    cosNk = trig[:, tile * 64: tile * 64 + 64].rearrange("p (o d) -> p o d", o=1).broadcast_to([128, 4, 64])
                    sinNk = trig[:, 2048 + tile * 64: 2048 + tile * 64 + 64].rearrange("p (o d) -> p o d", o=1).broadcast_to([128, 4, 64])
                    kv_ = knat[:, rs * 512:(rs + 1) * 512].rearrange("p (h d) -> p h d", d=128)
                    kr = krn[:, rs * 512:(rs + 1) * 512].rearrange("p (h d) -> p h d", d=128)
                    k0 = rsc[:, 0:256].rearrange("p (h d) -> p h d", d=64)
                    k1 = rsc[:, 256:512].rearrange("p (h d) -> p h d", d=64)
                    k2 = rsc[:, 512:768].rearrange("p (h d) -> p h d", d=64)
                    k3 = rsc[:, 768:1024].rearrange("p (h d) -> p h d", d=64)
                    v.tensor_mul(k0, kv_[:, :, 0:64], cosNk)
                    v.tensor_mul(k1, kv_[:, :, 64:128], sinNk)
                    v.tensor_mul(k2, kv_[:, :, 64:128], cosNk)
                    ins = v.tensor_mul(k3, kv_[:, :, 0:64], sinNk)
                    ins.then_inc(sems["VE"], 1)
                    v.wait_ge(sems["VE"], bump("VE"))
                    v.tensor_sub(kr[:, :, 0:64], k0, k1)
                    ins = v.tensor_add(kr[:, :, 64:128], k2, k3)
                    ins.then_inc(sems["VE"], 1)
                    bump("VE")
                v.wait_ge(sems["AC"], W[tag + "_ek"])
                for rs in range(4):
                    v.reduce_sum(srec[:, rs * 16:(rs + 1) * 16],
                                 eqb[:, rs * 2048:(rs + 1) * 2048].rearrange("p (h d) -> p h d", d=128), AX.X)
                for rs in range(4):
                    ins = v.reduce_sum(srec[:, 64 + rs * 4:64 + (rs + 1) * 4],
                                       ekb[:, rs * 512:(rs + 1) * 512].rearrange("p (h d) -> p h d", d=128), AX.X)
                ins.then_inc(sems["VE"], 1)
                v.wait_ge(sems["VE"], bump("VE"))
                ins = v.reciprocal(qrecv[:], srec[:, 0:80])
                ins.then_inc(sems["VE"], 1)
                v.wait_ge(sems["VE"], bump("VE"))
                for rs in range(4):
                    v.tensor_mul(qsn[:, rs * 2048:(rs + 1) * 2048].rearrange("p (h d) -> p h d", d=128),
                                 eqb[:, rs * 2048:(rs + 1) * 2048].rearrange("p (h d) -> p h d", d=128),
                                 qrecv[:, rs * 16:(rs + 1) * 16].rearrange("p (h o) -> p h o", o=1).broadcast_to([128, 16, 128]))
                for rs in range(4):
                    ins = v.tensor_mul(ksn[:, rs * 512:(rs + 1) * 512].rearrange("p (h d) -> p h d", d=128),
                                       ekb[:, rs * 512:(rs + 1) * 512].rearrange("p (h d) -> p h d", d=128),
                                       qrecv[:, 64 + rs * 4:64 + (rs + 1) * 4].rearrange("p (h o) -> p h o", o=1).broadcast_to([128, 4, 128]))
                ins.then_inc(sems["VE"], 1)
                W[tag + "_derive"] = bump("VE")
            blk.vector(_)

            tlist = [(rnat, BSQR, 16, 2048), (qsn, BQS, 16, 2048),
                     (krn, BSKR, 4, 512), (ksn, BKS, 4, 512), (knat, -1, 4, 512)]
            pe_tr_base = cnt["PE"]
            ac_tr_base = cnt["AC"]
            dx_sh_base = cnt["dX"]

            def _(t, tag=tag, ac_tr_base=ac_tr_base):
                t.wait_ge(sems["VE"], W[tag + "_derive"])
                gi = 0
                for (buf, shiprow, nodt, w) in tlist:
                    for odt in range(nodt):
                        if gi >= 2:
                            t.wait_ge(sems["AC"], ac_tr_base + gi - 1)
                        for rs in range(4):
                            ins = t.matmul(pb[2 + gi % 2][:, rs * 128:(rs + 1) * 128],
                                           buf[:, rs * w + odt * 128: rs * w + (odt + 1) * 128],
                                           IDENT, start=(rs == 0), stop=(rs == 3))
                        ins.then_inc(sems["PE"], 1)
                        bump("PE")
                        gi += 1
            blk.tensor(_)

            def _(sc, tag=tag, pe_tr_base=pe_tr_base, dx_sh_base=dx_sh_base):
                gi = 0
                si = 0
                for (buf, shiprow, nodt, w) in tlist:
                    for odt in range(nodt):
                        sc.wait_ge(sems["PE"], pe_tr_base + gi + 1)
                        if shiprow >= 0:
                            if si < 4:
                                sc.wait_ge(sems["dX"], dx_sh_base)
                            else:
                                sc.wait_ge(sems["dX"], dx_sh_base + (si - 3) * 16)
                            ins = sc.copy(shs[:, (si % 4) * 512:(si % 4) * 512 + 512], pb[2 + gi % 2][:])
                            si += 1
                        else:
                            ins = sc.copy(ktrb[:, odt * 512:(odt + 1) * 512], pb[2 + gi % 2][:])
                        ins.then_inc(sems["AC"], 1)
                        bump("AC")
                        gi += 1
            blk.scalar(_)

            def _(sync, c=c, ac_tr_base=ac_tr_base):
                gi = 0
                si = 0
                for (buf, shiprow, nodt, w) in tlist:
                    for odt in range(nodt):
                        if shiprow >= 0:
                            sync.wait_ge(sems["AC"], ac_tr_base + gi + 1)
                            b_ = odt // (nodt // 2)
                            lo = odt % (nodt // 2)
                            row = b_ * BLK_SHIP + shiprow + lo * 128
                            ins = sync.dma_start(
                                out=qkv_in[c][row: row + 128, :],
                                in_=shs[:, (si % 4) * 512:(si % 4) * 512 + 512])
                            ins.then_inc(sems["dX"], 16)
                            bump("dX", 16)
                            si += 1
                        gi += 1
            blk.sync(_)
            ac_ktr_done = cnt["AC"]

            ve_gl0 = cnt["VE"] + 1

            def _(sc, ac_ktr_done=ac_ktr_done, ve_gl0=ve_gl0):
                sc.wait_ge(sems["AC"], ac_ktr_done)
                for odt in range(4):
                    if odt >= 1:
                        sc.wait_ge(sems["VE"], ve_gl0 + odt - 1)
                    sc.activation(dsc[:, 0:512], ktrb[:, odt * 512:(odt + 1) * 512], AF.Abs)
                    sc.activation(dsc[:, 512:1024], dsc[:, 0:512], AF.Exp, scale=-1.0)
                    ins = sc.activation(dsc[:, 0:512], dsc[:, 512:1024], AF.Ln, bias=1.0)
                    ins.then_inc(sems["AC"], 1)
                    bump("AC")
            blk.scalar(_)
            ac_gl_base = ac_ktr_done

            def _(v, tag=tag, ac_gl_base=ac_gl_base):
                for odt in range(4):
                    v.wait_ge(sems["AC"], ac_gl_base + odt + 1)
                    v.tensor_scalar_min(dsc[:, 1024:1536], ktrb[:, odt * 512:(odt + 1) * 512], 0.0)
                    ins = v.tensor_sub(glb[:, odt * 512:(odt + 1) * 512], dsc[:, 1024:1536], dsc[:, 0:512])
                    ins.then_inc(sems["VE"], 1)
                    bump("VE")
            blk.vector(_)
            ve_gl_base = cnt["VE"] - 4

            def _(sync, c=c, ve_gl_base=ve_gl_base):
                for odt in range(4):
                    sync.wait_ge(sems["VE"], ve_gl_base + odt + 1)
                    b_ = odt // 2
                    row = b_ * BLK_SHIP + BGL + (odt % 2) * 128
                    ins = sync.dma_start(out=qkv_in[c][row: row + 128, :],
                                         in_=glb[:, odt * 512:(odt + 1) * 512])
                    ins.then_inc(sems["dX"], 16)
                    bump("dX", 16)
            blk.sync(_)
            W[tag + "_shipdone"] = cnt["dX"]

            def _(g, c=c, tag=tag):
                g.wait_ge(sems["dX"], W[tag + "_shipdone"])
                g.wait_ge(sems["CC"], cnt["CC"])
                ins = g.collective_compute("AllGather", ALU.bypass,
                                           replica_groups=[[0, 1], [2, 3], [4, 5], [6, 7]],
                                           ins=[qkv_in[c]], outs=[qkv_out[c]])
                ins.then_inc(sems["CC"], 1)
                bump("CC")
            blk.gpsimd(_)

        snd_pe_end = cnt["PE"]
        snd_ve_end = cnt["VE"]
        snd_ac_end = cnt["AC"]
        snd.close()

        rcv = ExitStack()
        RA = lambda n, s, d: rcv.enter_context(nc.sbuf_tensor(n, s, d))
        stg = RA("stg", [128, 26 * 512], BF16)
        skrT = RA("skrT", [128, KVH * SEQ], BF16)
        vful = RA("vful", [128, KVH * 32 * 128], BF16)
        Acum = RA("Acum", [128, KVH * 512], F32)
        eAb = RA("eAb", [128, KVH * 512], F32)
        eAn = RA("eAn", [128, KVH * 512], F32)
        kt_ = RA("kt_", [128, KVH * 512], BF16)
        ktn = RA("ktn", [128, KVH * 128], BF16)
        zf = RA("zf", [128, 128], BF16)
        Sst = RA("Sst", [128, KVH * 128], F32)
        Ssc = RA("Ssc", [128, KVH * 128], F32)
        Sb = RA("Sb", [128, KVH * 128], BF16)
        pgb = RA("pgb", [128, 128], BF16)
        o05 = RA("o05", [128, H * 512], BF16)
        oTb = RA("oTb", [128, H * 512], BF16)
        Pring = RA("Pring", [128, 2 * 2048], BF16)
        hdall = RA("hdall", [1, H * 512], BF16)
        hdb = RA("hdb", [1, H * 512], BF16)
        yraw = RA("yraw", [128, H * 512], BF16)
        qt_ = yraw
        yscr = RA("yscr", [128, 512], F32)
        ost = RA("ost", [128, 16 * 512], BF16)
        wob = RA("wob", [128, 2 * 2048], BF16)
        x1r = RA("x1r", [128, 2 * 512], F32)
        x2o = RA("x2o", [128, 2 * 512], F32)

        def _(v):
            v.wait_ge(sems["VE"], snd_ve_end)
            v.wait_ge(sems["PE"], snd_pe_end)
            v.wait_ge(sems["AC"], snd_ac_end)
            v.memset(zf[:], 0.0)
            ins = v.memset(Sst[:], 0.0)
            ins.then_inc(sems["VE"], 1)
            W["rcv_init"] = bump("VE")
        blk.vector(_)

        dg0 = cnt["dG"]
        for g_ in range(NG):
            W[f"g{g_}_staged"] = dg0 + (g_ + 1) * 16
        dg_p3 = dg0 + NG * 16
        for c_ in range(NB):
            W[f"wo_c{c_}_ost"] = dg_p3 + (c_ * 17 + 1) * 16
            for dmt_ in range(16):
                W[f"wo_c{c_}_wo{dmt_}"] = dg_p3 + (c_ * 17 + 1 + dmt_ + 1) * 16
        out20 = cnt["OUT"]
        for c_ in range(NB):
            for dmt_ in range(16):
                W[f"wo_c{c_}_st{dmt_}"] = out20 + (c_ * 16 + dmt_ + 1) * 16

        g_pe_end = {-1: snd_pe_end}
        g_ve_end = {-1: W["rcv_init"]}
        g_ac_end = {-1: snd_ac_end}
        g_ship = {-1: 0}

        for g in range(NG):
            slot, step = g // 4, g % 4
            src2d = qkv_out[step]


            def _(v, g=g):
                v.wait_ge(sems["dG"], W[f"g{g}_staged"])
                v.tensor_copy(skrT[:].rearrange("p (kv r) -> p kv r", kv=KVH)[:, :, g * 512:(g + 1) * 512],
                              stg[:, CSKR: CSKR + 1024].rearrange("p (kv r) -> p kv r", kv=KVH))
                for kv in range(KVH):
                    v.tensor_copy(
                        vful[:, kv * 4096 + g * 512: kv * 4096 + (g + 1) * 512].rearrange("p (rs d) -> p rs d", d=128),
                        stg[:, CV:].rearrange("p (rs c) -> p rs c", c=512)[:, :, kv * 128:(kv + 1) * 128])
                for kv in range(KVH):
                    for j in range(4):
                        ins = v.tensor_tensor_scan(Acum[:, kv * 512 + j * 128: kv * 512 + (j + 1) * 128],
                                                   stg[:, CGL + kv * 512 + j * 128: CGL + kv * 512 + (j + 1) * 128],
                                                   zf[:], 0.0, ALU.add, ALU.add)
                ins.then_inc(sems["VE"], 1)
                W[f"g{g}_scan"] = bump("VE")
            blk.vector(_)

            def _(sc, g=g):
                sc.wait_ge(sems["VE"], W[f"g{g}_scan"])
                sc.activation(eAb[:], Acum[:], AF.Exp, scale=1.0 / 16.0)
                ins = sc.activation(eAn[:], Acum[:], AF.Exp, scale=-1.0 / 16.0)
                ins.then_inc(sems["AC"], 1)
                W[f"g{g}_eA"] = bump("AC")
            blk.scalar(_)

            def _(v, g=g):
                v.wait_ge(sems["AC"], W[f"g{g}_eA"])
                for h in range(H):
                    v.tensor_mul(qt_[:, h * 512:(h + 1) * 512], stg[:, CQS + h * 512: CQS + (h + 1) * 512],
                                 eAb[:, (h // 4) * 512:(h // 4 + 1) * 512])
                for kv in range(KVH):
                    ins = v.tensor_mul(kt_[:, kv * 512:(kv + 1) * 512], stg[:, CKS + kv * 512: CKS + (kv + 1) * 512],
                                       eAn[:, kv * 512:(kv + 1) * 512])
                ins.then_inc(sems["VE"], 1)
                W[f"g{g}_qtkt"] = bump("VE")
            blk.vector(_)

            for j in range(4):
                jt = f"g{g}j{j}"
                def _(sc, g=g, j=j, jt=jt):
                    if j > 0:
                        sc.wait_ge(sems["VE"], W[f"g{g}j{j-1}_supd"])
                    elif g > 0:
                        sc.wait_ge(sems["VE"], W[f"g{g-1}j3_supd"])
                    else:
                        sc.wait_ge(sems["VE"], W["rcv_init"])
                    if g == 0 and j == 0:
                        pass
                    sc.wait_ge(sems["VE"], W[f"g{g}_qtkt"])
                    if j == 0 and g > 0:
                        sc.wait_ge(sems["VE"], W[f"g{g-1}_oT"])
                    ins = sc.copy(Sb[:], Sst[:])
                    ins.then_inc(sems["AC"], 1)
                    W[jt + "_sb"] = bump("AC")
                blk.scalar(_)

                ac_ktn0 = cnt["AC"] + 1

                def _(t, g=g, j=j, jt=jt, ac_ktn0=ac_ktn0):
                    t.wait_ge(sems["VE"], W[f"g{g}_qtkt"])
                    if j > 0:
                        t.wait_ge(sems["VE"], W[f"g{g}j{j-1}_supd"])
                    elif g > 0:
                        t.wait_ge(sems["VE"], W[f"g{g-1}j3_supd"])
                    for kv in range(KVH):
                        if kv == 1:
                            t.wait_ge(sems["AC"], ac_ktn0)
                        ins = t.matmul(pb[6][:, 0:128],
                                       kt_[:, kv * 512 + j * 128: kv * 512 + (j + 1) * 128],
                                       IDENT, start=True, stop=True)
                        ins.then_inc(sems["PE"], 1)
                        W[jt + f"_ktr{kv}"] = bump("PE")
                blk.tensor(_)

                def _(sc, jt=jt):
                    for kv in range(KVH):
                        sc.wait_ge(sems["PE"], W[jt + f"_ktr{kv}"])
                        ins = sc.copy(ktn[:, kv * 128:(kv + 1) * 128], pb[6][:, 0:128])
                        ins.then_inc(sems["AC"], 1)
                        W[jt + f"_ktn{kv}"] = bump("AC")
                blk.scalar(_)

                def _(t, g=g, j=j, jt=jt):
                    t.wait_ge(sems["AC"], W[jt + "_ktn1"])
                    for kv in range(KVH):
                        ins = t.matmul(pb[6][:, 128 + kv * 128: 128 + (kv + 1) * 128],
                                       ktn[:, kv * 128:(kv + 1) * 128],
                                       vful[:, kv * 4096 + (g * 4 + j) * 128: kv * 4096 + (g * 4 + j) * 128 + 128],
                                       start=True, stop=True)
                        ins.then_inc(sems["PE"], 1)
                        W[jt + f"_u0{kv}"] = bump("PE")
                blk.tensor(_)

                for h in range(H):
                    kv = h // 4
                    def _(t, g=g, j=j, h=h, kv=kv, jt=jt):
                        if h >= 2:
                            t.wait_ge(sems["VE"], W[jt + f"_pg{h-2}"])
                        ins = t.matmul(pb[4 + h % 2][:, 0:128],
                                       kt_[:, kv * 512 + j * 128: kv * 512 + (j + 1) * 128],
                                       qt_[:, h * 512 + j * 128: h * 512 + (j + 1) * 128],
                                       start=True, stop=True)
                        ins.then_inc(sems["PE"], 1)
                        W[jt + f"_pgm{h}"] = bump("PE")
                    blk.tensor(_)

                    def _(v, h=h, jt=jt):
                        v.wait_ge(sems["PE"], W[jt + f"_pgm{h}"])
                        if h >= 1:
                            v.wait_ge(sems["PE"], W[jt + f"_om{h-1}"])
                        ins = v.tensor_mul(pgb[:], pb[4 + h % 2][:, 0:128], MASKGLA)
                        ins.then_inc(sems["VE"], 1)
                        W[jt + f"_pg{h}"] = bump("VE")
                    blk.vector(_)

                    def _(t, g=g, j=j, h=h, kv=kv, jt=jt):
                        t.wait_ge(sems["VE"], W[jt + f"_pg{h}"])
                        t.wait_ge(sems["AC"], W[jt + "_sb"])
                        if h >= 2:
                            t.wait_ge(sems["AC"], W[jt + f"_oev{h-2}"])
                        t.matmul(pb[2 + h % 2][:, 0:128],
                                 vful[:, kv * 4096 + (g * 4 + j) * 128: kv * 4096 + (g * 4 + j) * 128 + 128],
                                 pgb[:], start=True, stop=False)
                        ins = t.matmul(pb[2 + h % 2][:, 0:128],
                                       Sb[:, kv * 128:(kv + 1) * 128],
                                       qt_[:, h * 512 + j * 128: h * 512 + (j + 1) * 128],
                                       start=False, stop=True)
                        ins.then_inc(sems["PE"], 1)
                        W[jt + f"_om{h}"] = bump("PE")
                    blk.tensor(_)

                    def _(sc, g=g, j=j, h=h, jt=jt):
                        sc.wait_ge(sems["PE"], W[jt + f"_om{h}"])
                        ins = sc.mul(o05[:, h * 512 + j * 128: h * 512 + (j + 1) * 128],
                                     pb[2 + h % 2][:, 0:128], 0.5)
                        ins.then_inc(sems["AC"], 1)
                        W[jt + f"_oev{h}"] = bump("AC")
                    blk.scalar(_)

                def _(v, g=g, j=j, jt=jt):
                    v.wait_ge(sems["PE"], W[jt + "_u01"])
                    v.wait_ge(sems["AC"], W[jt + "_sb"])
                    for kv in range(KVH):
                        eac = eAb[:, kv * 512 + j * 128 + 127: kv * 512 + j * 128 + 128]
                        v.tensor_add(Ssc[:, kv * 128:(kv + 1) * 128], Sst[:, kv * 128:(kv + 1) * 128],
                                     pb[6][:, 128 + kv * 128: 128 + (kv + 1) * 128])
                        ins = v.tensor_scalar(Sst[:, kv * 128:(kv + 1) * 128], Ssc[:, kv * 128:(kv + 1) * 128],
                                              eac, None, ALU.mult)
                        ins.then_inc(sems["VE"], 1)
                        v.wait_ge(sems["VE"], bump("VE"))
                    W[f"g{g}j{j}_supd"] = cnt["VE"]
                blk.vector(_)

            K = 4 * g + 4
            KP = K // 2
            for hp in range(H // 2):
                h0, h1 = 2 * hp, 2 * hp + 1
                pe0 = cnt["PE"]
                ac0 = cnt["AC"]
                ve0 = cnt["VE"]
                order = []
                for kt in range(min(4, K)):
                    order.append(("sc", h0, kt))
                    order.append(("sc", h1, kt))
                for p in range(2, KP):
                    for kt in (2 * p - 4, 2 * p - 3):
                        order.append(("av", h0, kt))
                        order.append(("av", h1, kt))
                    for kt in (2 * p, 2 * p + 1):
                        order.append(("sc", h0, kt))
                        order.append(("sc", h1, kt))
                for kt in (K - 4, K - 3, K - 2, K - 1):
                    order.append(("av", h0, kt))
                    order.append(("av", h1, kt))
                POS = {it[0] + str(it[1]) + "_" + str(it[2]): i + 1 for i, it in enumerate(order)}
                pe_sc_f = lambda h, kt, pe0=pe0, POS=POS: pe0 + POS[f"sc{h}_{kt}"]
                pe_av_f = lambda h, kt, pe0=pe0, POS=POS: pe0 + POS[f"av{h}_{kt}"]
                ac_ex_f = lambda h, p, ac0=ac0: ac0 + 2 * p + 1 + (h % 2)
                ve_mk_f = lambda h, kt, g=g, ve0=ve0: ve0 + 2 * (kt - 4 * g) + 1 + (h % 2)
                PB_Y = {h0: pb[2], h1: pb[6]}
                PB_D = {h0: pb[3], h1: pb[7]}
                PRS = {h0: 0, h1: 2048}

                def _(t, g=g, hp=hp, h0=h0, h1=h1, K=K, KP=KP, order=order):
                    if hp == 0:
                        t.wait_ge(sems["AC"], W[f"g{g}j3_oev{H-1}"])
                        t.wait_ge(sems["VE"], W[f"g{g}j3_pg{H-1}"])
                        t.wait_ge(sems["VE"], W[f"g{g}j3_supd"])
                    else:
                        t.wait_ge(sems["VE"], W[f"g{g}h{h0-2}_dev"])
                        t.wait_ge(sems["AC"], W[f"g{g}h{h0-2}_yev"])
                        t.wait_ge(sems["VE"], W[f"g{g}h{h1-2}_dev"])
                        t.wait_ge(sems["AC"], W[f"g{g}h{h1-2}_yev"])
                    for (kind, h, kt) in order:
                        kv = h // 4
                        p = kt // 2
                        if kind == "sc":
                            if p >= 1:
                                t.wait_ge(sems["AC"], ac_ex_f(h, p - 1))
                            bankpair = pbA if h % 2 == 0 else pbB
                            ins = t.matmul(bankpair[:, (kt % 2) * 512:(kt % 2) * 512 + 512],
                                           skrT[:, kv * SEQ + kt * 128: kv * SEQ + (kt + 1) * 128],
                                           stg[:, CSQR + h * 512: CSQR + (h + 1) * 512],
                                           start=True, stop=True)
                            ins.then_inc(sems["PE"], 1)
                            assert bump("PE") == pe_sc_f(h, kt)
                        else:
                            if kt >= 4 * g:
                                t.wait_ge(sems["VE"], ve_mk_f(h, kt))
                            else:
                                t.wait_ge(sems["AC"], ac_ex_f(h, p))
                            slot = PRS[h] + (p % 2) * 1024 + (kt % 2) * 512
                            t.matmul(PB_Y[h][:],
                                     vful[:, kv * 4096 + kt * 128: kv * 4096 + (kt + 1) * 128],
                                     Pring[:, slot: slot + 512],
                                     start=(kt == 0), stop=(kt == K - 1))
                            ins = t.matmul(PB_D[h][0:1, :], ONES_B,
                                           Pring[:, slot: slot + 512],
                                           start=(kt == 0), stop=(kt == K - 1))
                            ins.then_inc(sems["PE"], 1)
                            assert bump("PE") == pe_av_f(h, kt)
                blk.tensor(_)

                def _(sc, g=g, h0=h0, h1=h1, K=K, KP=KP):
                    for p in range(KP):
                        for h in (h0, h1):
                            sc.wait_ge(sems["PE"], pe_sc_f(h, 2 * p + 1))
                            if p >= 2:
                                sc.wait_ge(sems["PE"], pe_av_f(h, 2 * (p - 2) + 1))
                            bankpair = pbA if h % 2 == 0 else pbB
                            ins = sc.activation(Pring[:, PRS[h] + (p % 2) * 1024: PRS[h] + (p % 2) * 1024 + 1024],
                                                bankpair[:], AF.Exp, scale=float(HD) ** -0.5)
                            ins.then_inc(sems["AC"], 1)
                            assert bump("AC") == ac_ex_f(h, p)
                    for h in (h0, h1):
                        sc.wait_ge(sems["PE"], pe_av_f(h, K - 1))
                        ins = sc.copy(yraw[:, h * 512:(h + 1) * 512], PB_Y[h][:])
                        ins.then_inc(sems["AC"], 1)
                        W[f"g{g}h{h}_yev"] = bump("AC")
                blk.scalar(_)

                def _(v, g=g, h0=h0, h1=h1, K=K):
                    for kt in range(4 * g, K):
                        p = kt // 2
                        for h in (h0, h1):
                            slot = PRS[h] + (p % 2) * 1024 + (kt % 2) * 512
                            v.wait_ge(sems["AC"], ac_ex_f(h, p))
                            ins = v.tensor_mul(Pring[:, slot: slot + 512],
                                               Pring[:, slot: slot + 512],
                                               MASKDIAG(kt - 4 * g))
                            ins.then_inc(sems["VE"], 1)
                            assert bump("VE") == ve_mk_f(h, kt)
                    for h in (h0, h1):
                        v.wait_ge(sems["PE"], pe_av_f(h, K - 1))
                        ins = v.tensor_copy(hdall[0:1, h * 512:(h + 1) * 512], PB_D[h][0:1, :])
                        ins.then_inc(sems["VE"], 1)
                        W[f"g{g}h{h}_dev"] = bump("VE")
                blk.vector(_)

            def _(v, g=g):
                v.wait_ge(sems["AC"], W[f"g{g}h{H-1}_yev"])
                v.wait_ge(sems["VE"], W[f"g{g}h{H-1}_dev"])
                with nc.allow_low_precision(reason="bf16 softmax denominator reciprocal"):
                    ins = v.reciprocal(hdb[0:1, :], hdall[0:1, :])
                ins.then_inc(sems["VE"], 1)
                W[f"g{g}_hdb"] = bump("VE")
            blk.vector(_)

            ve_nm0 = cnt["VE"]

            def _(t, g=g, ve_nm0=ve_nm0):
                t.wait_ge(sems["VE"], W[f"g{g}_hdb"])
                for h in range(H):
                    if h >= 2:
                        t.wait_ge(sems["VE"], ve_nm0 + 2 * (h - 1))
                    ins = t.matmul(pb[6 + h % 2][:], HALFROW_B, hdb[0:1, h * 512:(h + 1) * 512],
                                   start=True, stop=True)
                    ins.then_inc(sems["PE"], 1)
                    W[f"g{g}nb{h}"] = bump("PE")
            blk.tensor(_)

            def _(v, g=g, ve_nm0=ve_nm0):
                for h in range(H):
                    v.wait_ge(sems["PE"], W[f"g{g}nb{h}"])
                    if g > 0 and h == 0:
                        v.wait_ge(sems["dX"], g_ship[g - 1])
                    ins = v.tensor_mul(yscr[:], yraw[:, h * 512:(h + 1) * 512], pb[6 + h % 2][:])
                    ins.then_inc(sems["VE"], 1)
                    bump("VE")
                    ins = v.tensor_add(oTb[:, h * 512:(h + 1) * 512], yscr[:],
                                       o05[:, h * 512:(h + 1) * 512])
                    ins.then_inc(sems["VE"], 1)
                    bump("VE")
            blk.vector(_)
            W[f"g{g}_oT"] = cnt["VE"]

            def _(sync, g=g):
                sync.wait_ge(sems["VE"], W[f"g{g}_oT"])
                ins = sync.dma_start(out=o_in[g].rearrange("(h p) r -> p h r", p=128),
                                     in_=oTb[:].rearrange("p (h r) -> p h r", h=H))
                ins.then_inc(sems["dX"], 16)
                g_ship[g] = bump("dX", 16)
            blk.sync(_)


            g_pe_end[g] = cnt["PE"]
            g_ve_end[g] = cnt["VE"]
            g_ac_end[g] = cnt["AC"]

        o_cc_base = cnt["CC"]

        o_out_flat = o_out.rearrange("g row c -> (g row) c")
        ve_p3 = cnt["VE"]
        pe_p3 = cnt["PE"]
        for c_ in range(NB):
            for dmt_ in range(16):
                W[f"wo_c{c_}_ev{dmt_}"] = ve_p3 + c_ * 16 + dmt_ + 1
                W[f"wo_c{c_}_m{dmt_}"] = pe_p3 + c_ * 16 + dmt_ + 1
            W[f"wo_c{c_}_pe_end"] = pe_p3 + (c_ + 1) * 16
        for c in range(NB):
            wt = f"wo_c{c}"

            def _(sync, c=c, wt=wt):
                for dmt in range(16):
                    if dmt >= 2:
                        sync.wait_ge(sems["VE"], W[wt + f"_ev{dmt-2}"])
                    elif c > 0:
                        sync.wait_ge(sems["VE"], W[f"wo_c{c-1}_ev{dmt+14}"])
                    ins = sync.dma_start(
                        out=x1r[:, (dmt % 2) * 512:(dmt % 2) * 512 + 512].rearrange("p (o r) -> p o r", o=1),
                        in_=x1T.rearrange("(dt p) r -> p dt r", p=128)[:, dmt:dmt + 1, c * 512:(c + 1) * 512])
                    ins.then_inc(sems["dX"], 16)
                    W[wt + f"_x1{dmt}"] = bump("dX", 16)
            blk.sync(_)

            def _(t, c=c, wt=wt):
                t.wait_ge(sems["dG"], W[wt + "_ost"])
                if c == 0:
                    t.wait_ge(sems["AC"], g_ac_end[NG - 1])
                    t.wait_ge(sems["VE"], g_ve_end[NG - 1])
                for dmt in range(16):
                    t.wait_ge(sems["dG"], W[wt + f"_wo{dmt}"])
                    if dmt >= 2:
                        t.wait_ge(sems["VE"], W[wt + f"_ev{dmt-2}"])
                    for ot in range(16):
                        ins = t.matmul(pb[dmt % 2][:],
                                       wob[:, (dmt % 2) * 2048 + ot * 128:(dmt % 2) * 2048 + (ot + 1) * 128],
                                       ost[:, ot * 512:(ot + 1) * 512],
                                       start=(ot == 0), stop=(ot == 15))
                    ins.then_inc(sems["PE"], 1)
                    assert bump("PE") == W[wt + f"_m{dmt}"]
                assert cnt["PE"] == W[wt + "_pe_end"]
            blk.tensor(_)

            def _(v, c=c, wt=wt):
                for dmt in range(16):
                    v.wait_ge(sems["PE"], W[wt + f"_m{dmt}"])
                    v.wait_ge(sems["dX"], W[wt + f"_x1{dmt}"])
                    if dmt >= 2:
                        v.wait_ge(sems["OUT"], W[wt + f"_st{dmt-2}"])
                    elif c > 0:
                        v.wait_ge(sems["OUT"], W[f"wo_c{c-1}_st{dmt+14}"])
                    ins = v.tensor_add(x2o[:, (dmt % 2) * 512:(dmt % 2) * 512 + 512],
                                       pb[dmt % 2][:], x1r[:, (dmt % 2) * 512:(dmt % 2) * 512 + 512])
                    ins.then_inc(sems["VE"], 1)
                    assert bump("VE") == W[wt + f"_ev{dmt}"]
            blk.vector(_)


        qkv_flat = qkv_out.rearrange("s r c -> (s r c)")
        o_flat1 = o_out.rearrange("g row c -> (g row c)")

        def _(gp):
            gp.wait_ge(sems["dX"], W["p_loads"])
            REG = [nc.values_load(offs_s[0:1, i:i + 1], engines=[PoolE],
                                  skip_runtime_bounds_check=True) for i in range(12)]
            for g in range(NG):
                slot, step = g // 4, g % 4
                gp.wait_ge(sems["CC"], cc_qkv_base + step + 1)
                gp.wait_ge(sems["PE"], g_pe_end[g - 1])
                gp.wait_ge(sems["VE"], g_ve_end[g - 1])
                gp.wait_ge(sems["AC"], g_ac_end[g - 1])
                gp.dma_start(
                    out=stg[:].rearrange("p (grp r) -> p grp r", r=512),
                    in_=qkv_flat[ds(REG[g], 26 * 128 * 512)]
                    .rearrange("(grp p r) -> p grp r", p=128, r=512)).then_inc(sems["dG"], 16)
                bump("dG", 16)
                assert cnt["dG"] == W[f"g{g}_staged"]
                if g > 0:
                    gp.wait_ge(sems["dX"], g_ship[g - 1])
                    gp.wait_ge(sems["CC"], cnt["CC"])
                    gp.collective_compute("AllGather", ALU.bypass,
                                          replica_groups=[[0, 1], [2, 3], [4, 5], [6, 7]],
                                          ins=[o_in[g - 1]], outs=[o_out[g - 1]]).then_inc(sems["CC"], 1)
                    assert bump("CC") == o_cc_base + g
            gp.wait_ge(sems["dX"], g_ship[NG - 1])
            gp.wait_ge(sems["CC"], cnt["CC"])
            gp.collective_compute("AllGather", ALU.bypass,
                                  replica_groups=[[0, 1], [2, 3], [4, 5], [6, 7]],
                                  ins=[o_in[NG - 1]], outs=[o_out[NG - 1]]).then_inc(sems["CC"], 1)
            assert bump("CC") == o_cc_base + NG

            for c in range(NB):
                wt = f"wo_c{c}"
                gp.wait_ge(sems["CC"], o_cc_base + min(5 + c, NG))
                if c > 0:
                    gp.wait_ge(sems["PE"], W[f"wo_c{c-1}_pe_end"])
                gp.dma_start(out=ost[:].rearrange("p (ot r) -> p ot r", ot=16),
                             in_=o_flat1[ds(REG[8 + c], 2048 * 512)].rearrange("(ot p r) -> p ot r", p=128, r=512)
                             ).then_inc(sems["dG"], 16)
                assert bump("dG", 16) == W[wt + "_ost"]
                for dmt in range(16):
                    if dmt >= 2:
                        gp.wait_ge(sems["PE"], W[wt + f"_m{dmt-2}"])
                        gp.wait_ge(sems["VE"], W[wt + f"_ev{dmt-2}"])
                        gp.dma_start(
                            out=x2T.rearrange("(dt p) r -> p dt r", p=128)[:, dmt - 2:dmt - 1, c * 512:(c + 1) * 512],
                            in_=x2o[:, (dmt % 2) * 512:(dmt % 2) * 512 + 512].rearrange("p (o r) -> p o r", o=1)
                        ).then_inc(sems["OUT"], 16)
                        assert bump("OUT", 16) == W[wt + f"_st{dmt-2}"]
                    gp.dma_start(out=wob[:, (dmt % 2) * 2048:(dmt % 2) * 2048 + 2048].rearrange("p (ot cc) -> p ot cc", ot=16),
                                 in_=wo_in[dmt].rearrange("ot p cc -> p ot cc")).then_inc(sems["dG"], 16)
                    assert bump("dG", 16) == W[wt + f"_wo{dmt}"]
                for dmt in range(14, 16):
                    gp.wait_ge(sems["VE"], W[wt + f"_ev{dmt}"])
                    gp.dma_start(
                        out=x2T.rearrange("(dt p) r -> p dt r", p=128)[:, dmt:dmt + 1, c * 512:(c + 1) * 512],
                        in_=x2o[:, (dmt % 2) * 512:(dmt % 2) * 512 + 512].rearrange("p (o r) -> p o r", o=1)
                    ).then_inc(sems["OUT"], 16)
                    assert bump("OUT", 16) == W[wt + f"_st{dmt}"]
        blk.gpsimd(_)

        rcv.close()
        return cnt["VE"], max(cnt["dX"], 0), cnt["OUT"]

    with ExitStack() as pes:
        phase_prologue(pes)

    def _(t):
        t.wait_ge(sems["CC"], W["p_wAG0"])
    blk.tensor(_)
    def _(sync):
        sync.wait_ge(sems["CC"], W["p_wAG0"])
    blk.sync(_)

    if N_SUBLAYERS == 1:
        with ExitStack() as fes:
            F = alloc_ffn_bufs(fes, "f1")
            phase_ffn("f1", F, xT_in, out_T, "OUT", free0=0)
        final_out = cnt["OUT"]
    else:
        with ExitStack() as fes:
            F = alloc_ffn_bufs(fes, "f1")
            phase_ffn("f1", F, xT_in, x1T, "dS", free0=0)
        f1_pe_end = cnt["PE"]
        f1_stored = [W[f"f1_b{b}_stored"] for b in range(NB)]
        at_ve_end, at_dx_end, at_out_end = phase_attn(W[f"f1_b{NB-1}_evacall"], f1_pe_end, f1_stored)

        def _(sync, at_out_end=at_out_end):
            sync.wait_ge(sems["OUT"], at_out_end)
        blk.sync(_)

        with ExitStack() as fes:
            F3 = alloc_ffn_bufs(fes, "f3")
            phase_ffn("f3", F3, x2T, out_T, "OUT", free0=at_ve_end, dx_gate=("OUT", at_out_end))
        final_out = cnt["OUT"]

    def _(sync, final_out=final_out):
        sync.wait_ge(sems["OUT"], final_out)
    blk.sync(_)

    blk.__exit__(None, None, None)
    es.close()
    return nc



def _host_prep(inputs):
    x = np.ascontiguousarray(np.asarray(inputs["L0_attn_hidden_states"], dtype=np.float32))
    Wq = np.asarray(inputs["L0_attn_Wq"], dtype=np.float32)
    Wk = np.asarray(inputs["L0_attn_Wk"], dtype=np.float32)
    Wv = np.asarray(inputs["L0_attn_Wv"], dtype=np.float32)
    Wo = np.asarray(inputs["L0_attn_Wo"], dtype=np.float32)
    w1 = np.asarray(inputs["L0_ffn_w1"], dtype=np.float32)
    b1 = np.asarray(inputs["L0_ffn_b1"], dtype=np.float32)
    w2 = np.asarray(inputs["L0_ffn_w2"], dtype=np.float32)
    b2 = np.asarray(inputs["L0_ffn_b2"], dtype=np.float32)
    import ml_dtypes
    bf = ml_dtypes.bfloat16

    w1t = np.ascontiguousarray(w1.reshape(FT, 128, NT, 128).transpose(0, 2, 3, 1))
    w2t = np.ascontiguousarray(w2.reshape(16, 128, FT, 128).transpose(0, 2, 3, 1))
    wqkv_cat = np.concatenate([Wq, Wk, Wv], axis=1)
    wqkvN = np.ascontiguousarray(wqkv_cat.reshape(NT, 128, 6, 512).transpose(2, 0, 1, 3))
    woT = np.ascontiguousarray(Wo.reshape(16, 128, 16, 128).transpose(2, 0, 1, 3))

    constf = np.zeros((128, 96), np.float32)
    constf[:, 0] = 1.0
    constf[:, 1:17] = b2.reshape(16, 128).T
    constf[:, 17:81] = b1.reshape(FT, 128).T
    constf[0, 82] = EPS

    constb = np.zeros((128, 5248), np.float32)
    constb[:, 0] = 1.0
    constb[:, 2:130] = np.eye(128)
    s_idx = np.arange(128)[:, None]
    q_idx512 = np.arange(512)[None, :]
    t_idx = np.arange(128)[None, :]
    constb[:, 130:258] = (t_idx >= s_idx).astype(np.float32)
    for j in range(4):
        constb[:, 258 + j * 512: 258 + (j + 1) * 512] = (q_idx512 >= (s_idx + 128 * j)).astype(np.float32)
    constb[0, 2306:2434] = 1.0
    constb[0, 2434:2946] = 1.0
    constb[0, 2946:4994] = 0.5 * b2
    constb[0, 4994:5122] = 0.5
    constb = constb.astype(bf)

    inv_freq = 1.0 / (10000.0 ** (np.arange(0, HD, 2, dtype=np.float64) / HD))
    pos = np.arange(SEQ, dtype=np.float64)
    fr = pos[:, None] * inv_freq[None, :]
    cosn = np.cos(fr).astype(np.float32).reshape(32, 128, 64).transpose(1, 0, 2).reshape(128, 2048)
    sinn = np.sin(fr).astype(np.float32).reshape(32, 128, 64).transpose(1, 0, 2).reshape(128, 2048)
    trig = np.concatenate([cosn, sinn], axis=1).astype(bf)

    per_core = []
    for c in range(8):
        b_, r = c // 2, c % 2
        xT = np.ascontiguousarray(x[b_, r * ROWS:(r + 1) * ROWS, :].T)
        offs = np.zeros((1, 16), np.int32)
        for g_ in range(8):
            slot_, step_ = g_ // 4, g_ % 4
            offs[0, g_] = (step_ * 13312 + slot_ * 6656 + r * 3328) * 512
        for lc in range(4):
            offs[0, 8 + lc] = ((4 * r + lc) * 2048) * 512
        per_core.append({
            "xT_in": xT,
            "w1s": np.ascontiguousarray(w1t[c * 8:(c + 1) * 8]),
            "w2s": np.ascontiguousarray(w2t[c * 2:(c + 1) * 2]),
            "wqkvN": wqkvN,
            "woT": woT,
            "constf": constf,
            "constb": constb,
            "trigN": trig,
            "offs": offs,
        })
    return per_core


def kernel(**inputs):
    per_core = _host_prep(inputs)
    nc = build_nc()
    res = run_bass_kernel_spmd(nc, per_core, core_ids=list(range(8)))
    out = np.empty((4, SEQ, DM), np.float32)
    for c in range(8):
        b_, r = c // 2, c % 2
        out[b_, r * ROWS:(r + 1) * ROWS, :] = res.results[c]["out_T"].T
    return out
